# version 37
# baseline (speedup 1.0000x reference)
import numpy as np
import ml_dtypes
import contextlib
import concourse.bass as bass
import concourse.mybir as mybir
from concourse.bass import AP
from concourse.bass_utils import run_bass_kernel_spmd

F32 = mybir.dt.float32
FP8 = mybir.dt.float8e4
BF16 = mybir.dt.bfloat16
I32 = mybir.dt.int32
ALU = mybir.AluOpType
AF = mybir.ActivationFunctionType
BF = ml_dtypes.bfloat16

D = 1024
NST = 16
RK = 64
FF = 4096
NCORE = 8
SEQ = 2048
TPC = 2 * SEQ
T = 512
NT = TPC // T
NTC = T // 128
NDC = D // 128
NFB = FF // 128
EPS = 1e-5

ENGS = ("sync", "tensor", "scalar", "vector", "gpsimd")


class Sem:
    __slots__ = ("h", "n", "name")

    def __init__(self, h, name):
        self.h = h
        self.n = 0
        self.name = name


class Planner:

    def __init__(self):
        self.q = {e: [] for e in ENGS}

    def op(self, eng, fn, waits=(), inc=None, amt=1, dma=False):
        ws = []
        for w in waits:
            if w is None:
                continue
            if isinstance(w, list):
                ws.extend(t for t in w if t is not None)
            else:
                ws.append(w)
        tgt = None
        if inc is not None:
            inc.n += amt
            tgt = inc.n
        self.q[eng].append((fn, tuple(ws), inc, tgt, amt, dma))
        return (inc, tgt) if inc is not None else None

    SELF_SER = False

    def emit(self, block):
        for ename in ENGS:
            oplist = self.q[ename]
            deco = getattr(block, ename)

            def section(eng, _ops=oplist):
                marks = {}
                prev_tok = None
                for fn, waits, inc, tgt, amt, dma in _ops:
                    allw = list(waits)
                    if Planner.SELF_SER and not dma and prev_tok is not None:
                        allw.append(prev_tok)
                    for (sem, val) in allw:
                        if marks.get(id(sem), -1) >= val:
                            continue
                        eng.wait_ge(sem.h, val)
                        marks[id(sem)] = val
                    instr = fn(eng)
                    if inc is not None:
                        instr.then_inc(inc.h, amt)
                        if not dma:
                            prev_tok = (inc, tgt)

            deco(section)


def _stat_save_src(sSTAT, c0):
    ap = sSTAT[:]
    return AP(ap.tensor, ap.offset + c0 + 2, [[ap.ap[0][0], 128], [4, 2]])


def _stat_save_dst(sSTAT, tc, par):
    ap = sSTAT[:]
    return AP(ap.tensor, ap.offset + 64 + 8 * par + 2 * tc, [[ap.ap[0][0], 128], [1, 2]])


def _state_pair_src(sH, parH):
    ap = sH[:]
    return AP(ap.tensor, ap.offset + (parH - 1) * T + (T - 1),
              [[ap.ap[0][0], 128], [T, 2]])


def _state_pair_dst(sSTATE, pair):
    ap = sSTATE[:]
    return AP(ap.tensor, ap.offset + (pair - NST), [[ap.ap[0][0], 128], [NST, 2]])


def lane_ap(tile, half, tc):
    ap = tile[:]
    base = (half * 4) * T + tc * 128
    return AP(ap.tensor, ap.offset + base, [[ap.ap[0][0], 128], [T, 4], [1, 128]])


def build_nc(nt):
    nc = bass.Bass("TRN2", target_bir_lowering=False)
    dram = {}

    def din(name, shape, dt):
        dram[name] = nc.dram_tensor(name, shape, dt, kind="ExternalInput").ap()

    din("xT", [TPC, D], F32)
    din("A_c", [128, 128], F32)
    din("xpwT_c", [128, NDC * 96], F32)
    din("dtwT", [RK, D], F32)
    din("dtb_c", [128, NDC], F32)
    din("fc1b_c", [128, NFB], F32)
    din("beta_c", [128, NDC], F32)
    din("rows4", [4, D], F32)
    din("q1blk", [NFB, NDC, 128, 128], I32)
    din("sc1r", [NFB, NDC * 128], F32)
    din("q2blk", [NFB, 128, D], I32)
    din("sc2r", [NFB, D], F32)
    din("idf", [128, 128], F32)
    din("id2b", [128, 128], BF16)
    din("onesf", [1, 128], F32)
    din("cvec", [128, 2], F32)
    din("selb", [32, 32 * 128], FP8)

    outT = nc.dram_tensor("outT", [TPC, D], F32, kind="ExternalOutput").ap()
    w1bf = nc.dram_tensor("w1bf", [NFB, 128, NDC * 128], BF16, kind="Internal").ap()
    w2bf = nc.dram_tensor("w2bf", [NFB, 128, D], BF16, kind="Internal").ap()

    es = contextlib.ExitStack()
    with es:
        def sbuf(name, shape, dt):
            return es.enter_context(nc.sbuf_tensor(name, shape, dt))

        def psum(name, shape, dt):
            return es.enter_context(nc.psum_tensor(name, shape, dt))

        sA = sbuf("sA", [128, 128], F32)
        sXPW = sbuf("sXPW", [128, NDC * 96], BF16)
        sDTW = sbuf("sDTW", [RK, D], BF16)
        sDTB = sbuf("sDTB", [128, NDC], F32)
        sFC1B = sbuf("sFC1B", [128, NFB], F32)
        sGB = sbuf("sGB", [128, D], F32)
        sBETA = sbuf("sBETA", [128, NDC], BF16)
        sDB = sbuf("sDB", [128, D], F32)
        sB2B = sbuf("sB2B", [128, D], F32)
        sIDF = sbuf("sIDF", [128, 128], F32)
        sID2 = sbuf("sID2", [128, 128], BF16)
        sONESF = sbuf("sONESF", [1, 128], F32)
        sCV = sbuf("sCV", [128, 2], F32)
        sSEL = sbuf("sSEL", [32, 32 * 128], FP8)

        pro = contextlib.ExitStack()
        sCLD = pro.enter_context(nc.sbuf_tensor("sCLD", [128, 2048], F32))
        sROWP = pro.enter_context(nc.sbuf_tensor("sROWP", [1, 1024], F32))
        sSCF = pro.enter_context(nc.sbuf_tensor("sSCF", [32, 1024], F32))
        sWO = pro.enter_context(nc.sbuf_tensor("sWO", [128, 4 * 1024], BF16))
        sSC1 = pro.enter_context(nc.sbuf_tensor("sSC1", [NFB, NDC * 128], BF16))
        sSC2 = pro.enter_context(nc.sbuf_tensor("sSC2", [NFB, D], BF16))
        sQ = pro.enter_context(nc.sbuf_tensor("sQ", [128, 3 * 1024], I32))
        pro.close()
        sY = sbuf("sY", [128, NDC * T], F32)
        sT3 = sbuf("sT3", [128, 4 * T], F32)
        sGH = sbuf("sGH", [128, 2 * T], F32)
        sYT = sbuf("sYT", [128, T], F32)
        sCW = sbuf("sCW", [128, T], F32)
        sXC = sbuf("sXC", [128, 2 * D], F32)

        sXT = sbuf("sXT", [128, NTC * D], F32)
        sXNT = sbuf("sXNT", [128, NTC * D], F32)
        sSTAT = sbuf("sSTAT", [128, 80], F32)
        sXNBL = sbuf("sXNBL", [128, NDC * T], BF16)
        sH2T = sbuf("sH2T", [128, D], F32)
        sH2L = sbuf("sH2L", [128, NDC * T], BF16)
        sDRK = sbuf("sDRK", [RK, T], BF16)
        sBCr = sbuf("sBCr", [32, T], BF16)
        sDLT = sbuf("sDLT", [128, NDC * T], F32)
        sUP = sbuf("sUP", [128, NDC * T], BF16)
        sDA = sbuf("sDA", [128, 4 * T], F32)
        sDBU = sbuf("sDBU", [128, 4 * T], BF16)
        sH = sbuf("sH", [128, 2 * T], BF16)
        sTMP = sbuf("sTMP", [128, 4 * T], BF16)
        sBCB = sbuf("sBCB", [128, 2 * T], BF16)
        sBCC = sbuf("sBCC", [128, 2 * T], BF16)
        sSTATE = sbuf("sSTATE", [128, 128], F32)
        sW = sbuf("sW", [128, 4 * 1024], BF16)
        sHID = sbuf("sHID", [128, NFB * T], BF16)
        sOUT = sbuf("sOUT", [128, D], F32)

        P1 = psum("P1", [128, 512], F32)
        P2 = psum("P2", [128, 512], F32)
        PB = psum("PB", [128, 512], F32)
        PC = psum("PC", [128, 512], F32)
        PF = [psum(f"PF{i}", [128, 512], F32) for i in range(4)]

        names = ["C", "SC", "ROW", "Qa", "Qb", "Qc", "WBa", "WBb", "WBc", "WBd",
                 "W1a", "W1b",
                 "W2a", "W2b", "W2c", "W2d", "X", "XCa", "XCb", "OUT", "PE",
                 "ACT", "DVE", "PL"] + [f"YA{d}" for d in range(8)]
        sems = {k: Sem(es.enter_context(nc.semaphore(k)), k) for k in names}
        S_C, S_SC, S_ROW, S_X, S_OUT, S_PE, S_ACT, S_DVE, S_PL = (
            sems[k] for k in ["C", "SC", "ROW", "X", "OUT", "PE", "ACT", "DVE", "PL"])
        S_Q = [sems["Qa"], sems["Qb"], sems["Qc"]]
        S_XC = [sems["XCa"], sems["XCb"]]
        S_YA = [sems[f"YA{d}"] for d in range(8)]
        S_WB = [sems["WBa"], sems["WBb"], sems["WBc"], sems["WBd"]]
        S_W1 = [sems["W1a"], sems["W1b"]]
        S_W2 = [sems["W2a"], sems["W2b"], sems["W2c"], sems["W2d"]]

        p = Planner()

        for nm, dst in [("A_c", sCLD[:, 0:128]), ("xpwT_c", sCLD[:, 128:128 + NDC * 96]),
                        ("dtwT", sCLD[0:RK, 1024:2048]), ("dtb_c", sDTB[:]),
                        ("fc1b_c", sFC1B[:]), ("beta_c", sCLD[0:128, 896:904]),
                        ("idf", sIDF[:]), ("id2b", sID2[:]), ("onesf", sONESF[:]),
                        ("cvec", sCV[:]), ("selb", sSEL[:])]:
            p.op("sync", lambda e, d=dst, n=nm: e.dma_start(out=d, in_=dram[n][:]),
                 inc=S_C, amt=16, dma=True)
        tok_consts = (S_C, S_C.n)
        tk_A = tk_xpw = tk_dtw = tk_dtb = tk_f1b = tk_idf = tk_onef = tk_cv = tk_sel = tok_consts
        tk_s1 = p.op("sync", lambda e: e.dma_start(out=sSCF[:], in_=dram["sc1r"][:]), inc=S_SC, amt=16, dma=True)

        t_ae = p.op("scalar", lambda e: e.activation(sA[:], sCLD[:, 0:128], AF.Exp), waits=[tk_A], inc=S_ACT)
        t_a = p.op("vector", lambda e: e.tensor_scalar(sA[:], sA[:], -1.0, None, ALU.mult), waits=[t_ae], inc=S_DVE)
        t_xpwb = p.op("vector", lambda e: e.tensor_copy(sXPW[:], sCLD[:, 128:128 + NDC * 96]), waits=[tk_xpw], inc=S_DVE)
        t_beta = p.op("vector", lambda e: e.tensor_copy(sBETA[:], sCLD[0:128, 896:904]), waits=[tok_consts], inc=S_DVE)
        t_dtwb = p.op("vector", lambda e: e.tensor_copy(sDTW[:], sCLD[0:RK, 1024:2048]), waits=[tk_dtw], inc=S_DVE)
        t_sc1b = p.op("vector", lambda e: e.tensor_copy(sSC1[:], sSCF[:]), waits=[tk_s1], inc=S_DVE)
        tk_s2 = p.op("sync", lambda e: e.dma_start(out=sSCF[:], in_=dram["sc2r"][:]), waits=[t_sc1b], inc=S_SC, amt=16, dma=True)
        t_sc2b = p.op("vector", lambda e: e.tensor_copy(sSC2[:], sSCF[:]), waits=[tk_s2], inc=S_DVE)

        prev_evs = []
        bc_done_dst = {}
        for idx, (dst, scl) in enumerate([(sGB, 1.0), (sDB, 1.0), (sB2B, 0.5)]):
            rowmap = [0, 2, 3]
            tk_row = p.op("sync",
                          lambda e, i=rowmap[idx]: e.dma_start(out=sROWP[:], in_=dram["rows4"][i:i + 1, :]),
                          waits=prev_evs, inc=S_ROW, amt=16, dma=True)
            evs = []
            for h in range(2):
                PP = P1 if h == 0 else P2
                tm = p.op("tensor",
                          lambda e, hh=h, PPl=PP: e.matmul(
                              PPl[:, 0:512], sONESF[0:1, :],
                              sROWP[0:1, hh * 512:(hh + 1) * 512], start=True, stop=True),
                          waits=[tk_row, tk_onef] + prev_evs, inc=S_PE)
                te = p.op("scalar",
                          lambda e, d=dst, hh=h, s=scl, PPl=PP: e.activation(
                              d[:, hh * 512:(hh + 1) * 512], PPl[:, 0:512], AF.Copy, scale=s),
                          waits=[tm], inc=S_ACT)
                evs.append(te)
            prev_evs = list(evs)
            bc_done_dst[idx] = list(evs)
        bcast_done = evs[-1]
        tk_gb, tk_db, tk_b2 = (bc_done_dst[0], bc_done_dst[1], bc_done_dst[2])

        deq = dict(q_rd=[None] * 3, wb_par=[None] * 4,
                   stt_h=[None, None], pf_rd=[None] * 3, bfold_prev=None,
                   q1tok={}, q2tok={}, stt_fb={},
                   wb2_par=[None] * 4)
        wb1_done = {}
        wb2_done = {}
        bfold_toks = []
        deq_pf_free = [[], [], [], []]

        def emit_deq1_dma(fb):
            par = fb % 3
            qtoks = []
            for d2 in range(NDC):
                tok = p.op("sync",
                           lambda e, f=fb, d=d2, pr=par: e.dma_start(
                               out=sQ[:, pr * 1024 + d * 128: pr * 1024 + (d + 1) * 128],
                               in_=dram["q1blk"][f, d, :, :]),
                           waits=[deq["q_rd"][par]], inc=S_Q[par], amt=16,
                           dma=True)
                qtoks.append(tok)
            deq["q1tok"][fb] = qtoks[-1]

        def emit_deq1(fb):
            par = fb % 3
            s4 = fb % 4
            sts = []
            for h in range(2):
                br = (2 * fb + h) % 3
                PP = PF[[0, 1, 3][br]]
                tm = p.op("tensor",
                          lambda e, f=fb, hh=h, PPl=PP: e.matmul(
                              PPl[:, 0:512], sSEL[:, f * 128:(f + 1) * 128],
                              sSC1[:, hh * 512:(hh + 1) * 512], start=True, stop=True),
                          waits=[t_sc1b, tk_sel, deq["pf_rd"][br]], inc=S_PE)
                ts_ = p.op("vector",
                           lambda e, hh=h, pr=par, w4=s4, PPl=PP: e.scalar_tensor_tensor(
                               sWO[:, w4 * 1024 + hh * 512: w4 * 1024 + (hh + 1) * 512],
                               sQ[:, pr * 1024 + hh * 512: pr * 1024 + (hh + 1) * 512],
                               128.0, PPl[:, 0:512], ALU.subtract, ALU.mult),
                           waits=[deq["q1tok"][fb], tm, deq["wb_par"][s4],
                                  G["wo_rd"][s4]],
                           inc=S_DVE)
                sts.append(ts_)
                deq["stt_h"][h] = ts_
                deq["pf_rd"][br] = ts_
                deq_pf_free[[0, 1, 3][br]] = [ts_]
            deq["q_rd"][par] = sts[-1]
            bmms = []
            for d2 in range(NDC):
                tok = p.op("tensor",
                           lambda e, d=d2, w4=s4: e.matmul(
                               PF[2][0:128, 0:1],
                               sWO[:, w4 * 1024 + d * 128: w4 * 1024 + (d + 1) * 128],
                               sBETA[:, d:d + 1], start=(d == 0), stop=(d == NDC - 1)),
                           waits=sts + [t_beta] +
                                 ([deq["bfold_prev"]] if deq["bfold_prev"] else []),
                           inc=S_PE)
                bmms.append(tok)
            bf = p.op("vector",
                      lambda e, f=fb: e.scalar_tensor_tensor(
                          sFC1B[:, f:f + 1], PF[2][0:128, 0:1], 1.0,
                          sFC1B[:, f:f + 1], ALU.mult, ALU.add),
                      waits=[bmms[-1], tok_consts], inc=S_DVE)
            deq["bfold_prev"] = bf
            bfold_toks.append(bf)
            deq["stt_fb"][fb] = list(sts)
            deq_pf_free[2] = [bf]
            tok = p.op("sync",
                       lambda e, f=fb, w4=s4: e.dma_start(
                           out=w1bf[f, :, :], in_=sWO[:, w4 * 1024:(w4 + 1) * 1024]),
                       waits=sts, inc=S_WB[s4], amt=16, dma=True)
            deq["wb_par"][s4] = tok
            wb1_done[fb] = tok

        def emit_deq2_dma(fb):
            par = fb % 3
            deq["q2tok"][fb] = p.op("sync",
                                    lambda e, f=fb, pr=par: e.dma_start(
                                        out=sQ[:, pr * 1024:(pr + 1) * 1024],
                                        in_=dram["q2blk"][f, :, :]),
                                    waits=[deq["q_rd"][par]],
                                    inc=S_Q[par], amt=16, dma=True)

        def emit_deq2(fb, p12_gate_first):
            par = fb % 3
            s4 = fb % 4
            tq = deq["q2tok"][fb]
            sts = []
            for h in range(2):
                PP = P1 if h == 0 else P2
                tm = p.op("tensor",
                          lambda e, f=fb, hh=h, PPl=PP: e.matmul(
                              PPl[:, 0:512], sSEL[:, f * 128:(f + 1) * 128],
                              sSC2[:, hh * 512:(hh + 1) * 512], start=True, stop=True),
                          waits=[t_sc2b, deq["stt_h"][h]] + p12_gate_first,
                          inc=S_PE)
                ts_ = p.op("vector",
                           lambda e, hh=h, pr=par, w4=s4, PPl=PP: e.scalar_tensor_tensor(
                               sWO[:, w4 * 1024 + hh * 512: w4 * 1024 + (hh + 1) * 512],
                               sQ[:, pr * 1024 + hh * 512: pr * 1024 + (hh + 1) * 512],
                               128.0, PPl[:, 0:512], ALU.subtract, ALU.mult),
                           waits=[tq, tm, deq["wb2_par"][s4], G["wo_rd"][s4]],
                           inc=S_DVE)
                sts.append(ts_)
                deq["stt_h"][h] = ts_
            deq["q_rd"][par] = sts[-1]
            deq["stt_fb"][(1, fb)] = list(sts)
            tok = p.op("sync",
                       lambda e, f=fb, w4=s4: e.dma_start(
                           out=w2bf[f, :, :], in_=sWO[:, w4 * 1024:(w4 + 1) * 1024]),
                       waits=sts, inc=S_WB[s4], amt=16, dma=True)
            deq["wb2_par"][s4] = tok
            wb2_done[fb] = tok

        stage_early = [t_ae, t_xpwb, t_dtwb, t_beta, t_sc2b, bcast_done]

        ROT = 4
        G = {
            "w_free": [None] * 4,
            "pf_free": [[], [], [], []],
            "da_free": [None] * ROT, "dbu_free": [None, None],
            "h_free": [None, None], "tmp_free": [None] * ROT,
            "bcb_rd": [None, None], "bcc_free": [None, None],
            "pb_free": None, "pc_free": None,
            "t3_free": [None] * 4, "gh_free": [None, None],
            "out_gate": None, "xc_free": [None, None], "cw_free": None,
            "tail_gate": None, "p12_mm_gate": None, "wo_rd": [None] * 4,
            "ylast_pe": None,
            "y_final": {dc: list(stage_early) for dc in range(NDC)},
            "hid_pe": None, "h2l_pe": None, "xnbl_pe": None, "up_dve": None,
            "stat_dve": None, "h2t_pe": None,
            "xt_act": None, "xt_dve": None, "xnt_act": None,
            "p12_act": list(bc_done_dst[2]), "p12_pe": None,
            "musave_rd": {0: None, 1: None},
            "st_tok": {}, "syt_last": None,
        }

        def emit_xload(ti):
            for tc in range(NTC):
                p.op("sync",
                     lambda e, t_=tc, r=ti * T: e.dma_start(
                         out=sXT[:, t_ * D:(t_ + 1) * D],
                         in_=dram["xT"][r + t_ * 128: r + (t_ + 1) * 128, :]),
                     waits=[G["xt_act"], G["xt_dve"]], inc=S_X, amt=16, dma=True)
            return (S_X, S_X.n)

        class Prolog:
            pass

        def emit_prolog_piece(PR, ti, k, tok_x, tcs=None):
            if tcs is None:
                tcs = range(NTC)
            if k == 0:
                for tc in tcs:
                    xs = sXT[:, tc * D:(tc + 1) * D]
                    c0 = tc * 8
                    t_sx = p.op("scalar",
                                lambda e, a=xs, c=c0: e.activation(
                                    sOUT[:, 0:D], a, AF.Identity,
                                    accum_out=sSTAT[:, c:c + 1]),
                                waits=[tok_x, G["stat_dve"], G["out_gate"]],
                                inc=S_ACT)
                    t_sq = p.op("scalar",
                                lambda e, a=xs, c=c0: e.activation(
                                    sOUT[:, 0:D], a, AF.Square,
                                    accum_out=sSTAT[:, c + 1:c + 2]),
                                waits=[tok_x], inc=S_ACT)
                    PR.sums1[tc] = (t_sx, t_sq)
            elif k == 1:
                rss = []
                for tc in range(NTC):
                    c0 = tc * 8
                    t_sx, t_sq = PR.sums1[tc]
                    t_mu = p.op("vector",
                                lambda e, c=c0: e.tensor_scalar(
                                    sSTAT[:, c + 2:c + 3], sSTAT[:, c:c + 1], 1.0 / D,
                                    None, ALU.mult),
                                waits=[t_sx], inc=S_DVE)
                    t_m2 = p.op("vector",
                                lambda e, c=c0: e.tensor_tensor(
                                    sSTAT[:, c + 3:c + 4], sSTAT[:, c + 2:c + 3],
                                    sSTAT[:, c + 2:c + 3], ALU.mult),
                                waits=[t_mu], inc=S_DVE)
                    t_var = p.op("vector",
                                 lambda e, c=c0: e.scalar_tensor_tensor(
                                     sSTAT[:, c + 4:c + 5], sSTAT[:, c + 1:c + 2], 1.0 / D,
                                     sSTAT[:, c + 3:c + 4], ALU.mult, ALU.subtract),
                                 waits=[t_sq, t_m2], inc=S_DVE)
                    PR.var1[tc] = t_var
                for tc in range(NTC):
                    c0 = tc * 8
                    t_ln = p.op("scalar",
                                lambda e, c=c0: e.activation(
                                    sSTAT[:, c + 5:c + 6], sSTAT[:, c + 4:c + 5],
                                    AF.Ln, bias=sCV[:, 0:1]),
                                waits=[PR.var1[tc], tk_cv], inc=S_ACT)
                    t_rs = p.op("scalar",
                                lambda e, c=c0: e.activation(
                                    sSTAT[:, c + 6:c + 7], sSTAT[:, c + 5:c + 6],
                                    AF.Exp, scale=-0.5),
                                waits=[t_ln], inc=S_ACT)
                    PR.rs1[tc] = t_rs
            elif k == 2:
                for tc in tcs:
                    c0 = tc * 8
                    c1 = 32 + tc * 8
                    t_xn = p.op("vector",
                                lambda e, t_=tc, c=c0: e.tensor_scalar(
                                    sXNT[:, t_ * D:(t_ + 1) * D], sXT[:, t_ * D:(t_ + 1) * D],
                                    sSTAT[:, c + 2:c + 3], sSTAT[:, c + 6:c + 7],
                                    ALU.subtract, ALU.mult),
                                waits=[PR.rs1[tc], G["xnt_act"]], inc=S_DVE)
                    PR.xn[tc] = t_xn
                    PR.musave[tc] = p.op("vector",
                                         lambda e, t_=tc, c=c0, pp=ti % 2: e.tensor_copy(
                                             _stat_save_dst(sSTAT, t_, pp),
                                             _stat_save_src(sSTAT, c)),
                                         waits=[PR.rs1[tc], G["musave_rd"][ti % 2]],
                                         inc=S_DVE)
                for tc in tcs:
                    c1 = 32 + tc * 8
                    xns = sXNT[:, tc * D:(tc + 1) * D]
                    t_sx2 = p.op("scalar",
                                 lambda e, a=xns, c=c1: e.activation(
                                     sOUT[:, 0:D], a, AF.Identity,
                                     accum_out=sSTAT[:, c:c + 1]),
                                 waits=[PR.xn[tc]], inc=S_ACT)
                    t_sq2 = p.op("scalar",
                                 lambda e, a=xns, c=c1: e.activation(
                                     sOUT[:, 0:D], a, AF.Square,
                                     accum_out=sSTAT[:, c + 1:c + 2]),
                                 waits=[PR.xn[tc]], inc=S_ACT)
                    PR.sums2[tc] = (t_sx2, t_sq2)
                if NTC - 1 not in tcs:
                    return
                PR.xt_act = PR.sums2[NTC - 1][1]
                PR.xt_dve = PR.xn[NTC - 1]
                PR.xnt_act = PR.sums2[NTC - 1][1]
            elif k == 3:
                for tc in range(NTC):
                    c1 = 32 + tc * 8
                    t_sx2, t_sq2 = PR.sums2[tc]
                    t_mu2 = p.op("vector",
                                 lambda e, c=c1: e.tensor_scalar(
                                     sSTAT[:, c + 2:c + 3], sSTAT[:, c:c + 1], 1.0 / D,
                                     None, ALU.mult),
                                 waits=[t_sx2], inc=S_DVE)
                    t_m22 = p.op("vector",
                                 lambda e, c=c1: e.tensor_tensor(
                                     sSTAT[:, c + 3:c + 4], sSTAT[:, c + 2:c + 3],
                                     sSTAT[:, c + 2:c + 3], ALU.mult),
                                 waits=[t_mu2], inc=S_DVE)
                    t_var2 = p.op("vector",
                                  lambda e, c=c1: e.scalar_tensor_tensor(
                                      sSTAT[:, c + 4:c + 5], sSTAT[:, c + 1:c + 2], 1.0 / D,
                                      sSTAT[:, c + 3:c + 4], ALU.mult, ALU.subtract),
                                  waits=[t_sq2, t_m22], inc=S_DVE)
                    PR.var2[tc] = t_var2
                for tc in range(NTC):
                    c1 = 32 + tc * 8
                    t_ln2 = p.op("scalar",
                                 lambda e, c=c1: e.activation(
                                     sSTAT[:, c + 5:c + 6], sSTAT[:, c + 4:c + 5],
                                     AF.Ln, bias=sCV[:, 0:1]),
                                 waits=[PR.var2[tc]], inc=S_ACT)
                    t_rs2 = p.op("scalar",
                                 lambda e, c=c1: e.activation(
                                     sSTAT[:, c + 6:c + 7], sSTAT[:, c + 5:c + 6],
                                     AF.Exp, scale=-0.5),
                                 waits=[t_ln2], inc=S_ACT)
                    PR.rs2[tc] = t_rs2
            elif k == 4 or k == 5:
                for tc in [2 * (k - 4), 2 * (k - 4) + 1]:
                    c1 = 32 + tc * 8
                    xns = sXNT[:, tc * D:(tc + 1) * D]
                    t_t2 = p.op("vector",
                                lambda e, a=xns, c=c1: e.tensor_scalar(
                                    sH2T[:], a, sSTAT[:, c + 2:c + 3],
                                    sSTAT[:, c + 6:c + 7], ALU.subtract, ALU.mult),
                                waits=[PR.rs2[tc], G["h2t_pe"],
                                       PR.h2l_ev[-1] if PR.h2l_ev else None], inc=S_DVE)
                    t_h2 = p.op("vector",
                                lambda e: e.tensor_tensor(sH2T[:], sH2T[:], sGB[:], ALU.mult),
                                waits=[t_t2, tk_gb], inc=S_DVE)
                    if tc == NTC - 1:
                        PR.stat_dve = t_t2
                    for half in range(2):
                        trx, trh = [], []
                        for j in range(4):
                            dc = half * 4 + j
                            tok = p.op("tensor",
                                       lambda e, d=dc, t_=tc, jj=j: e.transpose(
                                           P1[:, jj * 128:(jj + 1) * 128],
                                           sXNT[:, t_ * D + d * 128: t_ * D + (d + 1) * 128],
                                           sIDF[:]),
                                       waits=[PR.xn[tc],
                                              PR.xnl_ev[-1] if PR.xnl_ev else None] +
                                             (G["p12_act"] if not PR.xnl_ev else []),
                                       inc=S_PE)
                            trx.append(tok)
                        te = p.op("scalar",
                                  lambda e, hh=half, t_=tc: e.activation(
                                      lane_ap(sXNBL, hh, t_), P1[:, 0:512], AF.Identity),
                                  waits=trx + [G["xnbl_pe"], G["up_dve"]], inc=S_ACT)
                        PR.xnl_ev.append(te)
                        for j in range(4):
                            dc = half * 4 + j
                            tok = p.op("tensor",
                                       lambda e, d=dc, jj=j: e.transpose(
                                           P2[:, jj * 128:(jj + 1) * 128],
                                           sH2T[:, d * 128:(d + 1) * 128], sIDF[:]),
                                       waits=[t_h2,
                                              PR.h2l_ev[-1] if PR.h2l_ev else None] +
                                             (G["p12_act"] if not PR.h2l_ev else []),
                                       inc=S_PE)
                            trh.append(tok)
                        te2 = p.op("scalar",
                                   lambda e, hh=half, t_=tc: e.activation(
                                       lane_ap(sH2L, hh, t_), P2[:, 0:512], AF.Identity),
                                   waits=trh + [G["h2l_pe"]], inc=S_ACT)
                        PR.h2l_ev.append(te2)
                    if tc == NTC - 1:
                        PR.h2t_pe = trh[-1]
            elif k == 6:
                drk_mm, bcr_mm = [], []
                for dc in range(NDC):
                    tok = p.op("tensor",
                               lambda e, d=dc: e.matmul(
                                   P1[0:64, 0:512], sXPW[:, d * 96: d * 96 + 64],
                                   sXNBL[:, d * T:(d + 1) * T],
                                   start=(d == 0), stop=(d == NDC - 1)),
                               waits=[PR.xnl_ev[-1], t_xpwb], inc=S_PE)
                    drk_mm.append(tok)
                for dc in range(NDC):
                    tok = p.op("tensor",
                               lambda e, d=dc: e.matmul(
                                   P2[0:32, 0:512], sXPW[:, d * 96 + 64: d * 96 + 96],
                                   sXNBL[:, d * T:(d + 1) * T],
                                   start=(d == 0), stop=(d == NDC - 1)),
                               waits=[PR.xnl_ev[-1], PR.h2l_ev[-1], t_xpwb], inc=S_PE)
                    bcr_mm.append(tok)
                PR.xnbl_pe = bcr_mm[-1]
                PR.t_drk = p.op("vector", lambda e: e.tensor_copy(sDRK[:], P1[0:64, 0:512]),
                                waits=[drk_mm[-1]], inc=S_DVE)
                PR.t_bcr = p.op("vector", lambda e: e.tensor_copy(sBCr[:], P2[0:32, 0:512]),
                                waits=[bcr_mm[-1]], inc=S_DVE)
            elif k == 7:
                sp_prev = [None, None]
                for dc in range(NDC):
                    par = dc % 2
                    PP = P1 if par == 0 else P2
                    t_mm = p.op("tensor",
                                lambda e, d=dc, PPl=PP: e.matmul(
                                    PPl[:, 0:512], sDTW[:, d * 128:(d + 1) * 128], sDRK[:],
                                    start=True, stop=True),
                                waits=[PR.t_drk, PR.t_bcr, t_dtwb, sp_prev[par],
                                       G["p12_mm_gate"]],
                                inc=S_PE)
                    t_e = p.op("scalar",
                               lambda e, d=dc, PPl=PP, pr=par: e.activation(
                                   sOUT[:, pr * 512:(pr + 1) * 512], PPl[:, 0:512],
                                   AF.Exp, bias=sDTB[:, d:d + 1]),
                               waits=[t_mm] + ([G["out_gate"]] if dc < 2 else []),
                               inc=S_ACT)
                    t_d = p.op("scalar",
                               lambda e, d=dc, pr=par: e.activation(
                                   sDLT[:, d * T:(d + 1) * T],
                                   sOUT[:, pr * 512:(pr + 1) * 512],
                                   AF.Ln, bias=sCV[:, 1:2]),
                               waits=[t_e, tk_cv, G["up_dve"]], inc=S_ACT)
                    sp_prev[par] = t_e
                    PR.dlt[dc] = t_d
                    PR.up[dc] = p.op("vector",
                                     lambda e, d=dc: e.tensor_tensor(
                                         sUP[:, d * T:(d + 1) * T],
                                         sDLT[:, d * T:(d + 1) * T],
                                         sXNBL[:, d * T:(d + 1) * T], ALU.mult),
                                     waits=[t_d, PR.xnl_ev[-1]], inc=S_DVE)
                PR.up_dve = PR.up[NDC - 1]
                PR.delta_mm_last = t_mm

        def new_prolog():
            PR = Prolog()
            PR.xn = {}
            PR.musave = {}
            PR.sums1 = {}
            PR.var1 = {}
            PR.rs1 = {}
            PR.sums2 = {}
            PR.var2 = {}
            PR.rs2 = {}
            PR.xnl_ev = []
            PR.h2l_ev = []
            PR.dlt = {}
            PR.up = {}
            return PR

        def adopt(PR):
            for k in ["xt_act", "xt_dve", "xnt_act", "stat_dve", "h2t_pe",
                      "xnbl_pe", "up_dve"]:
                G[k] = getattr(PR, k)

        tok_x0 = emit_xload(0)
        PRO = new_prolog()
        for k in range(8):
            emit_prolog_piece(PRO, 0, k, tok_x0)
        adopt(PRO)
        emit_deq1_dma(0)
        emit_deq1_dma(1)
        emit_deq1_dma(2)

        for ti in range(nt):
            r0 = ti * T
            init_zero = (r0 % SEQ) == 0
            dlt_toks = PRO.dlt
            up_toks = PRO.up
            h2l_last = PRO.h2l_ev[-1]
            t_bcr = PRO.t_bcr

            tok_xn_next = emit_xload(ti + 1) if ti + 1 < nt else None

            y_ms = {}
            for dc in range(NDC):
                y_ms[dc] = p.op("gpsimd",
                                lambda e, d=dc: e.memset(sY[:, d * T:(d + 1) * T], 0.0),
                                waits=[G["ylast_pe"], G["y_final"][dc]], inc=S_PL)

            y_last = dict(y_ms)
            scan_idx = 0
            hid_toks = {}
            ev_par = [None, None]
            fc1mm_par = [None, None]
            state_rd = None
            PRN = new_prolog() if ti + 1 < nt else None

            def emit_fc1(fb):
                par = fb % 2
                PP = P1 if par == 0 else P2
                if ti == 0:
                    wsrc = sWO
                    wbase = (fb % 4) * 1024
                    wgate = [deq["stt_fb"][fb]]
                else:
                    wsrc = sW
                    wbase = par * 1024
                    wgate = [p.op("sync",
                                  lambda e, f=fb, pr=par: e.dma_start(
                                      out=sW[:, pr * 1024:(pr + 1) * 1024],
                                      in_=w1bf[f, :, :]),
                                  waits=[fc1mm_par[par] if fc1mm_par[par]
                                         else G["w_free"][par], wb1_done[fb]],
                                  inc=S_W1[par], amt=16, dma=True)]
                mms = []
                for dc in range(NDC):
                    tok = p.op("tensor",
                               lambda e, d=dc, wb=wbase, W=wsrc, PPl=PP: e.matmul(
                                   PPl[:, 0:512],
                                   W[:, wb + d * 128: wb + (d + 1) * 128],
                                   sH2L[:, d * T:(d + 1) * T],
                                   start=(d == 0), stop=(d == NDC - 1)),
                               waits=wgate + [h2l_last, ev_par[par],
                                              dlt_toks[NDC - 1]], inc=S_PE)
                    mms.append(tok)
                fc1mm_par[par] = mms[-1]
                if ti == 0:
                    G["wo_rd"][fb % 4] = mms[-1]
                t_si = p.op("scalar",
                            lambda e, f=fb, PPl=PP: e.activation(
                                sHID[:, f * T:(f + 1) * T], PPl[:, 0:512],
                                AF.Identity, bias=sFC1B[:, f:f + 1]),
                            waits=[mms[-1], tk_f1b, G["hid_pe"], bfold_toks[fb]],
                            inc=S_ACT)
                ev_par[par] = t_si
                hid_toks[fb] = t_si

            bc_ev = {}
            pre_da = {}

            def emit_bc_pe(n):
                t_bcB = p.op("tensor",
                             lambda e, nn=n: e.matmul(
                                 PB[:, 0:512], sSEL[:, nn * 128:(nn + 1) * 128],
                                 sBCr[:], start=True, stop=True),
                             waits=[t_bcr, G["pb_free"]], inc=S_PE)
                t_bcC = p.op("tensor",
                             lambda e, nn=n: e.matmul(
                                 PC[:, 0:512], sSEL[:, (nn + 16) * 128:(nn + 17) * 128],
                                 sBCr[:], start=True, stop=True),
                             waits=[t_bcr, G["pc_free"]], inc=S_PE)
                return (t_bcB, t_bcC)

            def emit_bc_ev(n, pe_toks):
                par = n % 2
                t_evB = p.op("scalar",
                             lambda e, pr=par: e.activation(
                                 sBCB[:, pr * T:(pr + 1) * T], PB[:, 0:512], AF.Identity),
                             waits=[pe_toks[0], G["bcb_rd"][par]], inc=S_ACT)
                G["pb_free"] = t_evB
                t_evC = p.op("scalar",
                             lambda e, pr=par: e.activation(
                                 sBCC[:, pr * T:(pr + 1) * T], PC[:, 0:512], AF.Identity),
                             waits=[pe_toks[1], G["bcc_free"][par]], inc=S_ACT)
                G["pc_free"] = t_evC
                bc_ev[n] = (t_evB, t_evC)

            def scan_group(n):
                nonlocal scan_idx, state_rd
                parB = n % 2
                parC = n % 2
                t_evB, t_evC = bc_ev[n]
                pe_next = emit_bc_pe(n + 1) if n + 1 < NST else None
                last_db = None

                def rep2(tile, par):
                    ap = tile[:]
                    return AP(ap.tensor, ap.offset + par * T,
                              [[ap.ap[0][0], 128], [0, 2], [1, T]])

                for j in range(NDC // 2):
                    dc0, dc1 = 2 * j, 2 * j + 1
                    pair0 = dc0 * NST + n
                    pair1 = dc1 * NST + n
                    parA0 = scan_idx % ROT
                    parA1 = (scan_idx + 1) % ROT
                    pj = scan_idx // 2
                    parU = pj % 2
                    parM = pj % 2
                    if n == 8 and dc0 in pre_da:
                        t_da0, ap0, slot0 = pre_da[dc0]
                    else:
                        ap0 = sDA[:, parA0 * T:(parA0 + 1) * T]
                        slot0 = parA0
                        t_da0 = p.op("scalar",
                                     lambda e, d=dc0, nn=n, a=ap0: e.activation(
                                         a, sDLT[:, d * T:(d + 1) * T],
                                         AF.Exp, scale=sA[:, d * 16 + nn: d * 16 + nn + 1]),
                                     waits=[dlt_toks[dc0], t_a, G["da_free"][parA0]],
                                     inc=S_ACT)
                    if n == 8 and dc1 in pre_da:
                        t_da1, ap1, slot1 = pre_da[dc1]
                    else:
                        ap1 = sDA[:, parA1 * T:(parA1 + 1) * T]
                        slot1 = parA1
                        t_da1 = p.op("scalar",
                                     lambda e, d=dc1, nn=n, a=ap1: e.activation(
                                         a, sDLT[:, d * T:(d + 1) * T],
                                         AF.Exp, scale=sA[:, d * 16 + nn: d * 16 + nn + 1]),
                                     waits=[dlt_toks[dc1], t_a, G["da_free"][parA1]],
                                     inc=S_ACT)
                    t_db = p.op("vector",
                                lambda e, d=dc0, pu=parU, prB=parB: e.tensor_tensor(
                                    sDBU[:, pu * 2 * T:(pu + 1) * 2 * T],
                                    sUP[:, d * T:(d + 2) * T],
                                    rep2(sBCB, prB), ALU.mult),
                                waits=[up_toks[dc0], up_toks[dc1], t_evB,
                                       G["dbu_free"][parU]], inc=S_DVE)
                    last_db = t_db
                    init0 = 0.0 if init_zero else sSTATE[:, pair0:pair0 + 1]
                    init1 = 0.0 if init_zero else sSTATE[:, pair1:pair1 + 1]
                    t_sc0 = p.op("vector",
                                 lambda e, a=ap0, pu=parU, ii=init0: e.tensor_tensor_scan(
                                     sH[:, 0:T], a,
                                     sDBU[:, pu * 2 * T:pu * 2 * T + T],
                                     ii, ALU.mult, ALU.add),
                                 waits=[t_da0, t_db, G["h_free"][0]] +
                                       ([] if init_zero else [G["st_tok"].get(pair0)]),
                                 inc=S_DVE)
                    t_sc1 = p.op("vector",
                                 lambda e, a=ap1, pu=parU, ii=init1: e.tensor_tensor_scan(
                                     sH[:, T:2 * T], a,
                                     sDBU[:, pu * 2 * T + T:(pu + 1) * 2 * T],
                                     ii, ALU.mult, ALU.add),
                                 waits=[t_da1, G["h_free"][1]] +
                                       ([] if init_zero else [G["st_tok"].get(pair1)]),
                                 inc=S_DVE)
                    if slot0 is not None:
                        G["da_free"][slot0] = t_sc0
                    else:
                        G["p12_act"] = G["p12_act"] + [t_sc0]
                    if slot1 is not None:
                        G["da_free"][slot1] = t_sc1
                    else:
                        G["p12_act"] = G["p12_act"] + [t_sc1]
                    G["dbu_free"][parU] = t_sc1
                    t_st = p.op("vector",
                                lambda e, pp=pair1: e.tensor_copy(
                                    _state_pair_dst(sSTATE, pp),
                                    _state_pair_src(sH, 1)),
                                waits=[t_sc1], inc=S_DVE)
                    state_rd = t_st
                    G["st_tok"][pair0] = t_st
                    G["st_tok"][pair1] = t_st
                    t_tm = p.op("vector",
                                lambda e, pm=parM, prC=parC: e.tensor_tensor(
                                    sTMP[:, pm * 2 * T:(pm + 1) * 2 * T],
                                    sH[:, 0:2 * T],
                                    rep2(sBCC, prC), ALU.mult),
                                waits=[t_sc1, t_evC, G["tmp_free"][parM]],
                                inc=S_DVE)
                    G["h_free"][0] = t_tm
                    G["h_free"][1] = t_tm
                    t_y = p.op("gpsimd",
                               lambda e, d=dc0, pm=parM: e.dma_start(
                                   out=sY[:, d * T:(d + 2) * T],
                                   in_=sTMP[:, pm * 2 * T:(pm + 1) * 2 * T],
                                   accum_op=ALU.add),
                               waits=[t_tm, y_last[dc0], y_last[dc1]],
                               inc=S_YA[dc0], amt=16, dma=True)
                    G["tmp_free"][parM] = t_y
                    y_last[dc0] = t_y
                    y_last[dc1] = t_y
                    scan_idx += 2
                G["bcb_rd"][parB] = last_db
                G["bcc_free"][parC] = y_last[NDC - 1]
                if pe_next is not None:
                    emit_bc_ev(n + 1, pe_next)

            emit_bc_ev(0, emit_bc_pe(0))
            for n in range(8):
                scan_group(n)
                for f in range(4 * n, 4 * n + 4):
                    if ti == 0:
                        emit_deq1(f)
                    emit_fc1(f)
                    if ti == 0 and f + 3 < NFB:
                        emit_deq1_dma(f + 3)
            if ti == 0:
                G["pf_free"] = [list(t) for t in deq_pf_free]
                deq["wb2_par"] = list(deq["wb_par"])
                emit_deq2_dma(0)
                emit_deq2_dma(1)
                emit_deq2_dma(2)

            for dc in range(6 if ti != 0 else 4):
                if ti != 0 and dc < 2:
                    dap = (P1 if dc == 0 else P2)[:, 0:512]
                    w = [dlt_toks[dc], t_a, ev_par[dc]]
                    sl = None
                else:
                    sl = dc % 4
                    dap = sDA[:, sl * T:(sl + 1) * T]
                    w = [dlt_toks[dc], t_a, G["da_free"][sl]]
                tok = p.op("scalar",
                           lambda e, d=dc, a=dap: e.activation(
                               a, sDLT[:, d * T:(d + 1) * T],
                               AF.Exp, scale=sA[:, d * 16 + 8: d * 16 + 9]),
                           waits=w, inc=S_ACT)
                pre_da[dc] = (tok, dap, sl)

            def emit_silus(j):
                for fb in range(8 * j, 8 * j + 8):
                    hid_toks[fb] = p.op("scalar",
                                        lambda e, f=fb: e.activation(
                                            sHID[:, f * T:(f + 1) * T],
                                            sHID[:, f * T:(f + 1) * T], AF.Silu),
                                        waits=[hid_toks[fb]], inc=S_ACT)

            G["p12_act"] = [ev_par[0], ev_par[1]]

            f2mm_rot = [None] * 4
            f2mm = {0: {}, 1: {}}
            f2_last_mm = None
            t3a_early = {}

            def emit_fc2_group(tcp, fb):
                nonlocal f2_last_mm
                rot = fb % 4
                if tcp == 0 and ti == 0:
                    wsrc = sWO
                    wbase = (fb % 4) * 1024
                    wgate = [deq["stt_fb"][(1, fb)]]
                else:
                    if f2mm_rot[rot] is not None:
                        gate = f2mm_rot[rot]
                    elif rot < 2:
                        gate = fc1mm_par[rot]
                    else:
                        gate = G["w_free"][rot]
                    wsrc = sW
                    wbase = rot * 1024
                    wgate = [p.op("sync",
                                  lambda e, f=fb, pr=rot: e.dma_start(
                                      out=sW[:, pr * 1024:(pr + 1) * 1024],
                                      in_=w2bf[f, :, :]),
                                  waits=[gate, wb2_done[fb]], inc=S_W2[rot],
                                  amt=16, dma=True)]
                for i2 in range(2):
                    tc = tcp * 2 + i2
                    for db in range(2):
                        pi = i2 * 2 + db
                        if fb == 0:
                            pfg = G["pf_free"][pi] if tcp == 0 else [t3a_early[pi]]
                        else:
                            pfg = []
                        tok = p.op("tensor",
                                   lambda e, f=fb, t_=tc, d2=db, p_=pi, W=wsrc,
                                   wb=wbase: e.matmul(
                                       PF[p_][:, 0:512],
                                       sHID[:, f * T + t_ * 128: f * T + (t_ + 1) * 128],
                                       W[:, wb + d2 * 512: wb + (d2 + 1) * 512],
                                       start=(f == 0), stop=(f == NFB - 1)),
                                   waits=wgate + [hid_toks[fb]] + pfg, inc=S_PE)
                        f2mm[tcp][pi] = tok
                        f2_last_mm = tok
                if tcp == 0 and ti == 0:
                    G["wo_rd"][fb % 4] = tok
                else:
                    f2mm_rot[rot] = tok

            for j in range(4):
                emit_silus(j)
            for n in range(8, 12):
                scan_group(n)
                for fb in range(8 * (n - 8), 8 * (n - 8) + 8):
                    if ti == 0:
                        emit_deq2(fb, G["p12_act"] if fb < 2 else [])
                    emit_fc2_group(0, fb)
                    if ti == 0 and fb + 3 < NFB:
                        emit_deq2_dma(fb + 3)
                if PRN is not None:
                    j = n - 8
                    if j == 0:
                        emit_prolog_piece(PRN, ti + 1, 0, tok_xn_next, [0, 1])
                    elif j == 1:
                        emit_prolog_piece(PRN, ti + 1, 0, tok_xn_next, [2, 3])
                        emit_prolog_piece(PRN, ti + 1, 1, tok_xn_next)
                    elif j == 2:
                        emit_prolog_piece(PRN, ti + 1, 2, tok_xn_next, [0, 1])
                    else:
                        emit_prolog_piece(PRN, ti + 1, 2, tok_xn_next, [2, 3])
                        emit_prolog_piece(PRN, ti + 1, 3, tok_xn_next)
            if ti == 0:
                late_free = [deq["stt_h"][0], deq["stt_h"][1],
                             wb2_done[NFB - 4], wb2_done[NFB - 3],
                             wb2_done[NFB - 2], wb2_done[NFB - 1]] + \
                    list(G["wo_rd"])
                G["t3_free"] = [list(late_free) for _ in range(4)]
                G["gh_free"] = [list(late_free), list(late_free)]
                G["cw_free"] = list(late_free)
                G["tail_gate"] = list(late_free)
                G["p12_mm_gate"] = list(late_free)
                G["p12_act"] = G["p12_act"] + [deq["stt_h"][0], deq["stt_h"][1]]

            for pi in range(4):
                db = pi % 2
                t3a_early[pi] = p.op("vector",
                                     lambda e, p_=pi, d2=db: e.scalar_tensor_tensor(
                                         sT3[:, p_ * T:(p_ + 1) * T], PF[p_][:, 0:512], 0.5,
                                         sB2B[:, d2 * 512:(d2 + 1) * 512],
                                         ALU.mult, ALU.add),
                                     waits=[f2mm[0][pi], G["t3_free"][pi], tk_b2],
                                     inc=S_DVE)

            for n in range(12, NST):
                scan_group(n)
                for fb in range(8 * (n - 12), 8 * (n - 12) + 8):
                    emit_fc2_group(1, fb)
                if PRN is not None and n - 8 < 6:
                    emit_prolog_piece(PRN, ti + 1, n - 8, tok_xn_next)
            if PRN is not None:
                emit_prolog_piece(PRN, ti + 1, 6, tok_xn_next)
                emit_prolog_piece(PRN, ti + 1, 7, tok_xn_next)

            comb_prev = None
            trt_last = None
            out_prev = G["out_gate"]
            syt_free = G["syt_last"]
            yt_free = [None, None]
            t3slot_g = [None] * 4
            if PRN is not None:
                p12_gate = [PRN.dlt[NDC - 2], PRN.dlt[NDC - 1]]
                out_wr_gate = [PRN.dlt[NDC - 2], PRN.dlt[NDC - 1]]
            else:
                p12_gate = [ev_par[0], ev_par[1]]
                out_wr_gate = [None, None]

            def emit_xc(tc, gate):
                return p.op("sync",
                            lambda e, t_=tc, r=r0: e.dma_start(
                                out=sXC[:, (t_ % 2) * D:(t_ % 2 + 1) * D],
                                in_=dram["xT"][r + t_ * 128: r + (t_ + 1) * 128, :]),
                            waits=[gate, G["tail_gate"]], inc=S_XC[tc % 2],
                            amt=16, dma=True)

            xc_toks = {0: emit_xc(0, G["xc_free"][0]),
                       1: emit_xc(1, G["xc_free"][1])}
            for tcp in range(2):
                if tcp == 1:
                    for pi in range(4):
                        db = pi % 2
                        t3a_early[pi] = p.op("vector",
                                             lambda e, p_=pi, d2=db: e.scalar_tensor_tensor(
                                                 sT3[:, p_ * T:(p_ + 1) * T],
                                                 PF[p_][:, 0:512], 0.5,
                                                 sB2B[:, d2 * 512:(d2 + 1) * 512],
                                                 ALU.mult, ALU.add),
                                             waits=[f2mm[1][pi], t3slot_g[pi], tk_b2],
                                             inc=S_DVE)
                        G["pf_free"][pi] = [t3a_early[pi]]
                for i2 in range(2):
                    tc = tcp * 2 + i2
                    xb = (tc % 2) * D
                    tok_xc = xc_toks[tc]
                    this_out_gate = out_prev
                    halves = []
                    for db in range(2):
                        pi = i2 * 2 + db
                        PY = P1 if db == 0 else P2
                        t_xn2 = p.op("vector",
                                     lambda e, t_=tc, d2=db, x0=xb, pp=ti % 2: e.tensor_scalar(
                                         sCW[:], sXC[:, x0 + d2 * 512:x0 + (d2 + 1) * 512],
                                         sSTAT[:, 64 + 8 * pp + 2 * t_: 65 + 8 * pp + 2 * t_],
                                         sSTAT[:, 65 + 8 * pp + 2 * t_: 66 + 8 * pp + 2 * t_],
                                         ALU.subtract, ALU.mult),
                                     waits=[tok_xc, PRO.musave[tc], G["cw_free"]], inc=S_DVE)
                        t_t3 = p.op("vector",
                                    lambda e, p_=pi: e.tensor_tensor(
                                        sT3[:, p_ * T:(p_ + 1) * T],
                                        sT3[:, p_ * T:(p_ + 1) * T], sCW[:], ALU.add),
                                    waits=[t3a_early[pi], t_xn2], inc=S_DVE)
                        t_g = p.op("scalar",
                                   lambda e, pr=db, p_=pi: e.activation(
                                       sGH[:, pr * T:(pr + 1) * T],
                                       sT3[:, p_ * T:(p_ + 1) * T], AF.Silu),
                                   waits=[t_t3, G["gh_free"][db]], inc=S_ACT)
                        t3slot_g[pi] = t_g
                        G["t3_free"][pi] = t_g
                        trt = []
                        for j in range(4):
                            dc = db * 4 + j
                            tok = p.op("tensor",
                                       lambda e, d=dc, t_=tc, jj=j, PYl=PY: e.transpose(
                                           PYl[:, jj * 128:(jj + 1) * 128],
                                           sY[:, d * T + t_ * 128: d * T + (t_ + 1) * 128],
                                           sIDF[:]),
                                       waits=[y_last[dc], yt_free[db],
                                              p12_gate[db]], inc=S_PE)
                            trt.append(tok)
                        trt_last = trt[-1]
                        t_yt = p.op("vector",
                                    lambda e, PYl=PY: e.tensor_copy(sYT[:], PYl[:, 0:512]),
                                    waits=trt + [G["tail_gate"], syt_free],
                                    inc=S_DVE)
                        yt_free[db] = t_yt
                        t_cw = p.op("vector",
                                    lambda e, d2=db: e.tensor_tensor(
                                        sCW[:], sCW[:],
                                        sDB[:, d2 * 512:(d2 + 1) * 512], ALU.mult),
                                    waits=[t_t3, t_yt, tk_db], inc=S_DVE)
                        t_s = p.op("vector",
                                   lambda e: e.tensor_tensor(sYT[:], sYT[:], sCW[:], ALU.add),
                                   waits=[t_cw], inc=S_DVE)
                        G["cw_free"] = t_s
                        t_m = p.op("vector",
                                   lambda e, pr=db: e.tensor_tensor(
                                       sYT[:], sYT[:], sGH[:, pr * T:(pr + 1) * T], ALU.mult),
                                   waits=[t_s, t_g], inc=S_DVE)
                        G["gh_free"][db] = t_m
                        t_o = p.op("vector",
                                   lambda e, d2=db, x0=xb: e.tensor_tensor(
                                       sOUT[:, d2 * 512:(d2 + 1) * 512], sYT[:],
                                       sXC[:, x0 + d2 * 512:x0 + (d2 + 1) * 512], ALU.add),
                                   waits=[t_m, tok_xc, this_out_gate,
                                          out_wr_gate[db]], inc=S_DVE)
                        comb_prev = t_o
                        syt_free = t_o
                        halves.append(t_o)
                    if tc + 2 < NTC:
                        xc_toks[tc + 2] = emit_xc(tc + 2, halves[-1])
                    G["xc_free"][tc % 2] = halves[-1]
                    t_store = p.op("sync",
                                   lambda e, t_=tc, r=r0: e.dma_start(
                                       out=outT[r + t_ * 128: r + (t_ + 1) * 128, :],
                                       in_=sOUT[:]),
                                   waits=halves, inc=S_OUT, amt=16, dma=True)
                    out_prev = t_store
            G["out_gate"] = out_prev
            G["syt_last"] = syt_free
            G["musave_rd"][ti % 2] = comb_prev
            G["ylast_pe"] = trt_last
            G["y_final"] = dict(y_last)
            G["hid_pe"] = f2_last_mm
            G["h2l_pe"] = fc1mm_par[1]
            G["w_free"] = list(f2mm_rot)
            if PRN is not None:
                adopt(PRN)
                PRO = PRN

        p.op("sync", lambda e: e.nop(), waits=[(S_OUT, S_OUT.n)])

        with nc.Block() as block:
            p.emit(block)

    return nc


def _host_stage(L0_attn_x, L0_attn_A_log, L0_attn_D, L0_attn_x_proj_w,
                L0_attn_dt_proj_w, L0_attn_dt_proj_b, L0_ffn_ln_gamma,
                L0_ffn_ln_beta, L0_ffn_fc1_q, L0_ffn_fc1_scales, L0_ffn_fc1_bias,
                L0_ffn_fc2_q, L0_ffn_fc2_scales, L0_ffn_fc2_bias):
    x = np.ascontiguousarray(np.asarray(L0_attn_x, dtype=np.float32))
    assert x.shape == (16, SEQ, D)

    A_c = np.ascontiguousarray(
        np.asarray(L0_attn_A_log, np.float32).reshape(NDC, 128, NST)
        .transpose(1, 0, 2).reshape(128, 128))
    xpwT = np.asarray(L0_attn_x_proj_w, np.float32).T
    xpwT_c = np.ascontiguousarray(
        xpwT.reshape(NDC, 128, 96).transpose(1, 0, 2).reshape(128, NDC * 96))
    dtwT = np.ascontiguousarray(np.asarray(L0_attn_dt_proj_w, np.float32).T)
    dtb_c = np.ascontiguousarray(np.asarray(L0_attn_dt_proj_b, np.float32).reshape(NDC, 128).T)
    fc1b_c = np.ascontiguousarray(np.asarray(L0_ffn_fc1_bias, np.float32).reshape(NFB, 128).T)
    beta_c = np.ascontiguousarray(np.asarray(L0_ffn_ln_beta, np.float32).reshape(NDC, 128).T)
    rows4 = np.ascontiguousarray(np.stack([
        np.asarray(L0_ffn_ln_gamma, np.float32),
        np.asarray(L0_ffn_ln_beta, np.float32),
        np.asarray(L0_attn_D, np.float32),
        np.asarray(L0_ffn_fc2_bias, np.float32)], axis=0))
    q1blk = np.ascontiguousarray(
        np.asarray(L0_ffn_fc1_q, np.int32).reshape(NDC, 128, NFB, 128).transpose(2, 0, 1, 3))
    sc1r = np.ascontiguousarray(
        np.asarray(L0_ffn_fc1_scales, np.float32).reshape(NDC, NFB, 128)
        .transpose(1, 0, 2).reshape(NFB, NDC * 128))
    q2blk = np.ascontiguousarray(np.asarray(L0_ffn_fc2_q, np.int32).reshape(NFB, 128, D))
    sc2r = np.ascontiguousarray(np.asarray(L0_ffn_fc2_scales, np.float32))
    idf = np.eye(128, dtype=np.float32)
    id2b = (2.0 * np.eye(128)).astype(BF)
    onesf = np.ones((1, 128), dtype=np.float32)
    cvec = np.ascontiguousarray(np.stack([np.full(128, EPS, np.float32), np.ones(128, np.float32)], axis=1))
    selb = np.zeros((32, 32 * 128), dtype=ml_dtypes.float8_e4m3)
    for k in range(32):
        selb[k, k * 128:(k + 1) * 128] = ml_dtypes.float8_e4m3(1.0)

    shared = dict(A_c=A_c, xpwT_c=xpwT_c, dtwT=dtwT, dtb_c=dtb_c, fc1b_c=fc1b_c,
                  beta_c=beta_c,
                  rows4=rows4, q1blk=q1blk, sc1r=sc1r, q2blk=q2blk, sc2r=sc2r,
                  idf=idf, id2b=id2b, onesf=onesf, selb=selb, cvec=cvec)
    return shared, x


def kernel(**inputs):
    shared, x = _host_stage(**inputs)
    nc = build_nc(NT)
    in_maps = []
    for c in range(NCORE):
        m = dict(shared)
        m["xT"] = np.ascontiguousarray(x[2 * c:2 * c + 2].reshape(TPC, D))
        in_maps.append(m)

    res = run_bass_kernel_spmd(nc, in_maps, core_ids=list(range(NCORE)))
    out = np.empty((16, SEQ, D), dtype=np.float32)
    for c in range(NCORE):
        out[2 * c:2 * c + 2] = np.asarray(res.results[c]["outT"]).reshape(2, SEQ, D)
    return out


# version 38
# speedup vs baseline: 1.1746x; 1.1746x over previous
import numpy as np
import ml_dtypes
import contextlib
import concourse.bass as bass
import concourse.mybir as mybir
from concourse.bass import AP
from concourse.bass_utils import run_bass_kernel_spmd

F32 = mybir.dt.float32
FP8 = mybir.dt.float8e4
BF16 = mybir.dt.bfloat16
I32 = mybir.dt.int32
ALU = mybir.AluOpType
AF = mybir.ActivationFunctionType
BF = ml_dtypes.bfloat16

D = 1024
NST = 16
RK = 64
FF = 4096
NCORE = 8
SEQ = 2048
TPC = 2 * SEQ
T = 512
NT = TPC // T
NTC = T // 128
NDC = D // 128
NFB = FF // 128
EPS = 1e-5

ENGS = ("sync", "tensor", "scalar", "vector", "gpsimd")


class Sem:
    __slots__ = ("h", "n", "name")

    def __init__(self, h, name):
        self.h = h
        self.n = 0
        self.name = name


class Planner:

    def __init__(self):
        self.q = {e: [] for e in ENGS}

    def op(self, eng, fn, waits=(), inc=None, amt=1, dma=False):
        ws = []
        for w in waits:
            if w is None:
                continue
            if isinstance(w, list):
                ws.extend(t for t in w if t is not None)
            else:
                ws.append(w)
        tgt = None
        if inc is not None:
            inc.n += amt
            tgt = inc.n
        self.q[eng].append((fn, tuple(ws), inc, tgt, amt, dma))
        return (inc, tgt) if inc is not None else None

    SELF_SER = False

    def emit(self, block):
        for ename in ENGS:
            oplist = self.q[ename]
            deco = getattr(block, ename)

            def section(eng, _ops=oplist):
                marks = {}
                prev_tok = None
                for fn, waits, inc, tgt, amt, dma in _ops:
                    allw = list(waits)
                    if Planner.SELF_SER and not dma and prev_tok is not None:
                        allw.append(prev_tok)
                    for (sem, val) in allw:
                        if marks.get(id(sem), -1) >= val:
                            continue
                        eng.wait_ge(sem.h, val)
                        marks[id(sem)] = val
                    instr = fn(eng)
                    if inc is not None:
                        instr.then_inc(inc.h, amt)
                        if not dma:
                            prev_tok = (inc, tgt)

            deco(section)


def _stat_save_src(sSTAT, c0):
    ap = sSTAT[:]
    return AP(ap.tensor, ap.offset + c0 + 2, [[ap.ap[0][0], 128], [4, 2]])


def _stat_save_dst(sSTAT, tc, par):
    ap = sSTAT[:]
    return AP(ap.tensor, ap.offset + 64 + 8 * par + 2 * tc, [[ap.ap[0][0], 128], [1, 2]])


def _state_pair_src(sH, parH):
    ap = sH[:]
    return AP(ap.tensor, ap.offset + (parH - 1) * T + (T - 1),
              [[ap.ap[0][0], 128], [T, 2]])


def _state_pair_dst(sSTATE, pair):
    ap = sSTATE[:]
    return AP(ap.tensor, ap.offset + (pair - NST), [[ap.ap[0][0], 128], [NST, 2]])


def lane_ap(tile, half, tc):
    ap = tile[:]
    base = (half * 4) * T + tc * 128
    return AP(ap.tensor, ap.offset + base, [[ap.ap[0][0], 128], [T, 4], [1, 128]])


def build_nc(nt):
    nc = bass.Bass("TRN2", target_bir_lowering=False)
    dram = {}

    def din(name, shape, dt):
        dram[name] = nc.dram_tensor(name, shape, dt, kind="ExternalInput").ap()

    din("xT", [TPC, D], F32)
    din("A_c", [128, 128], F32)
    din("xpwT_c", [128, NDC * 96], F32)
    din("dtwT", [RK, D], F32)
    din("dtb_c", [128, NDC], F32)
    din("fc1b_c", [128, NFB], F32)
    din("beta_c", [128, NDC], F32)
    din("rows4", [4, D], F32)
    din("q1blk", [NFB, NDC, 128, 128], I32)
    din("sc1r", [NFB, NDC * 128], F32)
    din("q2blk", [NFB, 128, D], I32)
    din("sc2r", [NFB, D], F32)
    din("idf", [128, 128], F32)
    din("id2b", [128, 128], BF16)
    din("onesf", [1, 128], F32)
    din("cvec", [128, 2], F32)
    din("selb", [32, 32 * 128], FP8)

    outT = nc.dram_tensor("outT", [TPC, D], F32, kind="ExternalOutput").ap()
    w1bf = nc.dram_tensor("w1bf", [NFB, 128, NDC * 128], BF16, kind="Internal").ap()
    w2bf = nc.dram_tensor("w2bf", [NFB, 128, D], BF16, kind="Internal").ap()

    es = contextlib.ExitStack()
    with es:
        def sbuf(name, shape, dt):
            return es.enter_context(nc.sbuf_tensor(name, shape, dt))

        def psum(name, shape, dt):
            return es.enter_context(nc.psum_tensor(name, shape, dt))

        sA = sbuf("sA", [128, 128], F32)
        sXPW = sbuf("sXPW", [128, NDC * 96], BF16)
        sDTW = sbuf("sDTW", [RK, D], BF16)
        sDTB = sbuf("sDTB", [128, NDC], F32)
        sFC1B = sbuf("sFC1B", [128, NFB], F32)
        sGB = sbuf("sGB", [128, D], F32)
        sBETA = sbuf("sBETA", [128, NDC], BF16)
        sDB = sbuf("sDB", [128, D], F32)
        sB2B = sbuf("sB2B", [128, D], F32)
        sIDF = sbuf("sIDF", [128, 128], F32)
        sID2 = sbuf("sID2", [128, 128], BF16)
        sONESF = sbuf("sONESF", [1, 128], F32)
        sCV = sbuf("sCV", [128, 2], F32)
        sSEL = sbuf("sSEL", [32, 32 * 128], FP8)

        pro = contextlib.ExitStack()
        sCLD = pro.enter_context(nc.sbuf_tensor("sCLD", [128, 2048], F32))
        sROWP = pro.enter_context(nc.sbuf_tensor("sROWP", [1, 1024], F32))
        sSCF = pro.enter_context(nc.sbuf_tensor("sSCF", [32, 1024], F32))
        sWO = pro.enter_context(nc.sbuf_tensor("sWO", [128, 4 * 1024], BF16))
        sSC1 = pro.enter_context(nc.sbuf_tensor("sSC1", [NFB, NDC * 128], BF16))
        sSC2 = pro.enter_context(nc.sbuf_tensor("sSC2", [NFB, D], BF16))
        sQ = pro.enter_context(nc.sbuf_tensor("sQ", [128, 3 * 1024], I32))
        pro.close()
        sY = sbuf("sY", [128, NDC * T], F32)
        sT3 = sbuf("sT3", [128, 4 * T], F32)
        sGH = sbuf("sGH", [128, 2 * T], F32)
        sYT = sbuf("sYT", [128, T], F32)
        sCW = sbuf("sCW", [128, T], F32)
        sXC = sbuf("sXC", [128, 2 * D], F32)

        sXT = sbuf("sXT", [128, NTC * D], F32)
        sXNT = sbuf("sXNT", [128, NTC * D], F32)
        sSTAT = sbuf("sSTAT", [128, 80], F32)
        sXNBL = sbuf("sXNBL", [128, NDC * T], BF16)
        sH2T = sbuf("sH2T", [128, D], F32)
        sH2L = sbuf("sH2L", [128, NDC * T], BF16)
        sDRK = sbuf("sDRK", [RK, T], BF16)
        sBCr = sbuf("sBCr", [32, T], BF16)
        sDLT = sbuf("sDLT", [128, NDC * T], F32)
        sUP = sbuf("sUP", [128, NDC * T], BF16)
        sDA = sbuf("sDA", [128, 4 * T], F32)
        sDBU = sbuf("sDBU", [128, 4 * T], BF16)
        sH = sbuf("sH", [128, 2 * T], BF16)
        sTMP = sbuf("sTMP", [128, 4 * T], BF16)
        sBCB = sbuf("sBCB", [128, 2 * T], BF16)
        sBCC = sbuf("sBCC", [128, 2 * T], BF16)
        sSTATE = sbuf("sSTATE", [128, 128], F32)
        sW = sbuf("sW", [128, 4 * 1024], BF16)
        sHID = sbuf("sHID", [128, NFB * T], BF16)
        sOUT = sbuf("sOUT", [128, D], F32)

        P1 = psum("P1", [128, 512], F32)
        P2 = psum("P2", [128, 512], F32)
        PB = psum("PB", [128, 512], F32)
        PC = psum("PC", [128, 512], F32)
        PF = [psum(f"PF{i}", [128, 512], F32) for i in range(4)]

        names = ["C", "SC", "ROW", "Qa", "Qb", "Qc", "WBa", "WBb", "WBc", "WBd",
                 "W1a", "W1b",
                 "W2a", "W2b", "W2c", "W2d", "X", "XCa", "XCb", "OUT", "PE",
                 "ACT", "DVE", "PL"] + [f"YA{d}" for d in range(8)]
        sems = {k: Sem(es.enter_context(nc.semaphore(k)), k) for k in names}
        S_C, S_SC, S_ROW, S_X, S_OUT, S_PE, S_ACT, S_DVE, S_PL = (
            sems[k] for k in ["C", "SC", "ROW", "X", "OUT", "PE", "ACT", "DVE", "PL"])
        S_Q = [sems["Qa"], sems["Qb"], sems["Qc"]]
        S_XC = [sems["XCa"], sems["XCb"]]
        S_YA = [sems[f"YA{d}"] for d in range(8)]
        S_WB = [sems["WBa"], sems["WBb"], sems["WBc"], sems["WBd"]]
        S_W1 = [sems["W1a"], sems["W1b"]]
        S_W2 = [sems["W2a"], sems["W2b"], sems["W2c"], sems["W2d"]]

        p = Planner()

        for nm, dst in [("A_c", sCLD[:, 0:128]), ("xpwT_c", sCLD[:, 128:128 + NDC * 96]),
                        ("dtwT", sCLD[0:RK, 1024:2048]), ("dtb_c", sDTB[:]),
                        ("fc1b_c", sFC1B[:]), ("beta_c", sCLD[0:128, 896:904]),
                        ("idf", sIDF[:]), ("id2b", sID2[:]), ("onesf", sONESF[:]),
                        ("cvec", sCV[:]), ("selb", sSEL[:])]:
            p.op("sync", lambda e, d=dst, n=nm: e.dma_start(out=d, in_=dram[n][:]),
                 inc=S_C, amt=16, dma=True)
        tok_consts = (S_C, S_C.n)
        tk_A = tk_xpw = tk_dtw = tk_dtb = tk_f1b = tk_idf = tk_onef = tk_cv = tk_sel = tok_consts
        tk_s1 = p.op("sync", lambda e: e.dma_start(out=sSCF[:], in_=dram["sc1r"][:]), inc=S_SC, amt=16, dma=True)

        t_ae = p.op("scalar", lambda e: e.activation(sA[:], sCLD[:, 0:128], AF.Exp), waits=[tk_A], inc=S_ACT)
        t_a = p.op("vector", lambda e: e.tensor_scalar(sA[:], sA[:], -1.0, None, ALU.mult), waits=[t_ae], inc=S_DVE)
        t_xpwb = p.op("vector", lambda e: e.tensor_copy(sXPW[:], sCLD[:, 128:128 + NDC * 96]), waits=[tk_xpw], inc=S_DVE)
        t_beta = p.op("vector", lambda e: e.tensor_copy(sBETA[:], sCLD[0:128, 896:904]), waits=[tok_consts], inc=S_DVE)
        t_dtwb = p.op("vector", lambda e: e.tensor_copy(sDTW[:], sCLD[0:RK, 1024:2048]), waits=[tk_dtw], inc=S_DVE)
        t_sc1b = p.op("vector", lambda e: e.tensor_copy(sSC1[:], sSCF[:]), waits=[tk_s1], inc=S_DVE)
        tk_s2 = p.op("sync", lambda e: e.dma_start(out=sSCF[:], in_=dram["sc2r"][:]), waits=[t_sc1b], inc=S_SC, amt=16, dma=True)
        t_sc2b = p.op("vector", lambda e: e.tensor_copy(sSC2[:], sSCF[:]), waits=[tk_s2], inc=S_DVE)

        prev_evs = []
        bc_done_dst = {}
        for idx, (dst, scl) in enumerate([(sGB, 1.0), (sDB, 1.0), (sB2B, 0.5)]):
            rowmap = [0, 2, 3]
            tk_row = p.op("sync",
                          lambda e, i=rowmap[idx]: e.dma_start(out=sROWP[:], in_=dram["rows4"][i:i + 1, :]),
                          waits=prev_evs, inc=S_ROW, amt=16, dma=True)
            evs = []
            for h in range(2):
                PP = P1 if h == 0 else P2
                tm = p.op("tensor",
                          lambda e, hh=h, PPl=PP: e.matmul(
                              PPl[:, 0:512], sONESF[0:1, :],
                              sROWP[0:1, hh * 512:(hh + 1) * 512], start=True, stop=True),
                          waits=[tk_row, tk_onef] + prev_evs, inc=S_PE)
                te = p.op("scalar",
                          lambda e, d=dst, hh=h, s=scl, PPl=PP: e.activation(
                              d[:, hh * 512:(hh + 1) * 512], PPl[:, 0:512], AF.Copy, scale=s),
                          waits=[tm], inc=S_ACT)
                evs.append(te)
            prev_evs = list(evs)
            bc_done_dst[idx] = list(evs)
        bcast_done = evs[-1]
        tk_gb, tk_db, tk_b2 = (bc_done_dst[0], bc_done_dst[1], bc_done_dst[2])

        deq = dict(q_rd=[None] * 3, wb_par=[None] * 4,
                   stt_h=[None, None], pf_rd=[None] * 3, bfold_prev=None,
                   q1tok={}, q2tok={}, stt_fb={},
                   wb2_par=[None] * 4)
        wb1_done = {}
        wb2_done = {}
        bfold_toks = []
        deq_pf_free = [[], [], [], []]

        def emit_deq1_dma(fb):
            par = fb % 3
            qtoks = []
            for d2 in range(NDC):
                tok = p.op("sync",
                           lambda e, f=fb, d=d2, pr=par: e.dma_start(
                               out=sQ[:, pr * 1024 + d * 128: pr * 1024 + (d + 1) * 128],
                               in_=dram["q1blk"][f, d, :, :]),
                           waits=[deq["q_rd"][par]], inc=S_Q[par], amt=16,
                           dma=True)
                qtoks.append(tok)
            deq["q1tok"][fb] = qtoks[-1]

        def emit_deq1(fb):
            par = fb % 3
            s4 = fb % 4
            sts = []
            for h in range(2):
                PP = PF[h]
                tm = p.op("tensor",
                          lambda e, f=fb, hh=h, PPl=PP: e.matmul(
                              PPl[:, 0:512], sSEL[:, f * 128:(f + 1) * 128],
                              sSC1[:, hh * 512:(hh + 1) * 512], start=True, stop=True),
                          waits=[t_sc1b, tk_sel, deq["stt_h"][h]], inc=S_PE)
                ts_ = p.op("vector",
                           lambda e, hh=h, pr=par, w4=s4, PPl=PP: e.scalar_tensor_tensor(
                               sWO[:, w4 * 1024 + hh * 512: w4 * 1024 + (hh + 1) * 512],
                               sQ[:, pr * 1024 + hh * 512: pr * 1024 + (hh + 1) * 512],
                               128.0, PPl[:, 0:512], ALU.subtract, ALU.mult),
                           waits=[deq["q1tok"][fb], tm, deq["wb_par"][s4],
                                  G["wo_rd"][s4]],
                           inc=S_DVE)
                sts.append(ts_)
                deq["stt_h"][h] = ts_
                deq_pf_free[h] = [ts_]
            deq["q_rd"][par] = sts[-1]
            bmms = []
            for d2 in range(NDC):
                tok = p.op("tensor",
                           lambda e, d=d2, w4=s4: e.matmul(
                               PF[2][0:128, 0:1],
                               sWO[:, w4 * 1024 + d * 128: w4 * 1024 + (d + 1) * 128],
                               sBETA[:, d:d + 1], start=(d == 0), stop=(d == NDC - 1)),
                           waits=sts + [t_beta] +
                                 ([deq["bfold_prev"]] if deq["bfold_prev"] else []),
                           inc=S_PE)
                bmms.append(tok)
            bf = p.op("vector",
                      lambda e, f=fb: e.scalar_tensor_tensor(
                          sFC1B[:, f:f + 1], PF[2][0:128, 0:1], 1.0,
                          sFC1B[:, f:f + 1], ALU.mult, ALU.add),
                      waits=[bmms[-1], tok_consts], inc=S_DVE)
            deq["bfold_prev"] = bf
            bfold_toks.append(bf)
            deq["stt_fb"][fb] = list(sts)
            deq_pf_free[2] = [bf]
            tok = p.op("sync",
                       lambda e, f=fb, w4=s4: e.dma_start(
                           out=w1bf[f, :, :], in_=sWO[:, w4 * 1024:(w4 + 1) * 1024]),
                       waits=sts, inc=S_WB[s4], amt=16, dma=True)
            deq["wb_par"][s4] = tok
            wb1_done[fb] = tok

        def emit_deq2_dma(fb):
            par = fb % 3
            deq["q2tok"][fb] = p.op("sync",
                                    lambda e, f=fb, pr=par: e.dma_start(
                                        out=sQ[:, pr * 1024:(pr + 1) * 1024],
                                        in_=dram["q2blk"][f, :, :]),
                                    waits=[deq["q_rd"][par]],
                                    inc=S_Q[par], amt=16, dma=True)

        def emit_deq2(fb, p12_gate_first):
            par = fb % 3
            s4 = fb % 4
            tq = deq["q2tok"][fb]
            sts = []
            for h in range(2):
                PP = P1 if h == 0 else P2
                tm = p.op("tensor",
                          lambda e, f=fb, hh=h, PPl=PP: e.matmul(
                              PPl[:, 0:512], sSEL[:, f * 128:(f + 1) * 128],
                              sSC2[:, hh * 512:(hh + 1) * 512], start=True, stop=True),
                          waits=[t_sc2b, deq["stt_h"][h]] + p12_gate_first,
                          inc=S_PE)
                ts_ = p.op("vector",
                           lambda e, hh=h, pr=par, w4=s4, PPl=PP: e.scalar_tensor_tensor(
                               sWO[:, w4 * 1024 + hh * 512: w4 * 1024 + (hh + 1) * 512],
                               sQ[:, pr * 1024 + hh * 512: pr * 1024 + (hh + 1) * 512],
                               128.0, PPl[:, 0:512], ALU.subtract, ALU.mult),
                           waits=[tq, tm, deq["wb2_par"][s4], G["wo_rd"][s4]],
                           inc=S_DVE)
                sts.append(ts_)
                deq["stt_h"][h] = ts_
            deq["q_rd"][par] = sts[-1]
            deq["stt_fb"][(1, fb)] = list(sts)
            tok = p.op("sync",
                       lambda e, f=fb, w4=s4: e.dma_start(
                           out=w2bf[f, :, :], in_=sWO[:, w4 * 1024:(w4 + 1) * 1024]),
                       waits=sts, inc=S_WB[s4], amt=16, dma=True)
            deq["wb2_par"][s4] = tok
            wb2_done[fb] = tok

        stage_early = [t_ae, t_xpwb, t_dtwb, t_beta, t_sc2b, bcast_done]

        ROT = 4
        G = {
            "w_free": [None] * 4,
            "pf_free": [[], [], [], []],
            "da_free": [None] * ROT, "dbu_free": [None, None],
            "h_free": [None, None], "tmp_free": [None] * ROT,
            "bcb_rd": [None, None], "bcc_free": [None, None],
            "pb_free": None, "pc_free": None,
            "t3_free": [None] * 4, "gh_free": [None, None],
            "out_gate": None, "xc_free": [None, None], "cw_free": None,
            "tail_gate": None, "p12_mm_gate": None, "wo_rd": [None] * 4,
            "ylast_pe": None,
            "y_final": {dc: list(stage_early) for dc in range(NDC)},
            "hid_pe": None, "h2l_pe": None, "xnbl_pe": None, "up_dve": None,
            "stat_dve": None, "h2t_pe": None,
            "xt_act": None, "xt_dve": None, "xnt_act": None,
            "p12_act": list(bc_done_dst[2]), "p12_pe": None,
            "musave_rd": {0: None, 1: None},
            "st_tok": {}, "syt_last": None,
        }

        def emit_xload(ti):
            for tc in range(NTC):
                p.op("sync",
                     lambda e, t_=tc, r=ti * T: e.dma_start(
                         out=sXT[:, t_ * D:(t_ + 1) * D],
                         in_=dram["xT"][r + t_ * 128: r + (t_ + 1) * 128, :]),
                     waits=[G["xt_act"], G["xt_dve"]], inc=S_X, amt=16, dma=True)
            return (S_X, S_X.n)

        class Prolog:
            pass

        def emit_prolog_piece(PR, ti, k, tok_x, tcs=None):
            if tcs is None:
                tcs = range(NTC)
            if k == 0:
                for tc in tcs:
                    xs = sXT[:, tc * D:(tc + 1) * D]
                    c0 = tc * 8
                    t_sx = p.op("scalar",
                                lambda e, a=xs, c=c0: e.activation(
                                    sOUT[:, 0:D], a, AF.Identity,
                                    accum_out=sSTAT[:, c:c + 1]),
                                waits=[tok_x, G["stat_dve"], G["out_gate"]],
                                inc=S_ACT)
                    t_sq = p.op("scalar",
                                lambda e, a=xs, c=c0: e.activation(
                                    sOUT[:, 0:D], a, AF.Square,
                                    accum_out=sSTAT[:, c + 1:c + 2]),
                                waits=[tok_x], inc=S_ACT)
                    PR.sums1[tc] = (t_sx, t_sq)
            elif k == 1:
                rss = []
                for tc in range(NTC):
                    c0 = tc * 8
                    t_sx, t_sq = PR.sums1[tc]
                    t_mu = p.op("vector",
                                lambda e, c=c0: e.tensor_scalar(
                                    sSTAT[:, c + 2:c + 3], sSTAT[:, c:c + 1], 1.0 / D,
                                    None, ALU.mult),
                                waits=[t_sx], inc=S_DVE)
                    t_m2 = p.op("vector",
                                lambda e, c=c0: e.tensor_tensor(
                                    sSTAT[:, c + 3:c + 4], sSTAT[:, c + 2:c + 3],
                                    sSTAT[:, c + 2:c + 3], ALU.mult),
                                waits=[t_mu], inc=S_DVE)
                    t_var = p.op("vector",
                                 lambda e, c=c0: e.scalar_tensor_tensor(
                                     sSTAT[:, c + 4:c + 5], sSTAT[:, c + 1:c + 2], 1.0 / D,
                                     sSTAT[:, c + 3:c + 4], ALU.mult, ALU.subtract),
                                 waits=[t_sq, t_m2], inc=S_DVE)
                    PR.var1[tc] = t_var
                for tc in range(NTC):
                    c0 = tc * 8
                    t_ln = p.op("scalar",
                                lambda e, c=c0: e.activation(
                                    sSTAT[:, c + 5:c + 6], sSTAT[:, c + 4:c + 5],
                                    AF.Ln, bias=sCV[:, 0:1]),
                                waits=[PR.var1[tc], tk_cv], inc=S_ACT)
                    t_rs = p.op("scalar",
                                lambda e, c=c0: e.activation(
                                    sSTAT[:, c + 6:c + 7], sSTAT[:, c + 5:c + 6],
                                    AF.Exp, scale=-0.5),
                                waits=[t_ln], inc=S_ACT)
                    PR.rs1[tc] = t_rs
            elif k == 2:
                for tc in tcs:
                    c0 = tc * 8
                    c1 = 32 + tc * 8
                    t_xn = p.op("vector",
                                lambda e, t_=tc, c=c0: e.tensor_scalar(
                                    sXNT[:, t_ * D:(t_ + 1) * D], sXT[:, t_ * D:(t_ + 1) * D],
                                    sSTAT[:, c + 2:c + 3], sSTAT[:, c + 6:c + 7],
                                    ALU.subtract, ALU.mult),
                                waits=[PR.rs1[tc], G["xnt_act"]], inc=S_DVE)
                    PR.xn[tc] = t_xn
                    PR.musave[tc] = p.op("vector",
                                         lambda e, t_=tc, c=c0, pp=ti % 2: e.tensor_copy(
                                             _stat_save_dst(sSTAT, t_, pp),
                                             _stat_save_src(sSTAT, c)),
                                         waits=[PR.rs1[tc], G["musave_rd"][ti % 2]],
                                         inc=S_DVE)
                for tc in tcs:
                    c1 = 32 + tc * 8
                    xns = sXNT[:, tc * D:(tc + 1) * D]
                    t_sx2 = p.op("scalar",
                                 lambda e, a=xns, c=c1: e.activation(
                                     sOUT[:, 0:D], a, AF.Identity,
                                     accum_out=sSTAT[:, c:c + 1]),
                                 waits=[PR.xn[tc]], inc=S_ACT)
                    t_sq2 = p.op("scalar",
                                 lambda e, a=xns, c=c1: e.activation(
                                     sOUT[:, 0:D], a, AF.Square,
                                     accum_out=sSTAT[:, c + 1:c + 2]),
                                 waits=[PR.xn[tc]], inc=S_ACT)
                    PR.sums2[tc] = (t_sx2, t_sq2)
                if NTC - 1 not in tcs:
                    return
                PR.xt_act = PR.sums2[NTC - 1][1]
                PR.xt_dve = PR.xn[NTC - 1]
                PR.xnt_act = PR.sums2[NTC - 1][1]
            elif k == 3:
                for tc in range(NTC):
                    c1 = 32 + tc * 8
                    t_sx2, t_sq2 = PR.sums2[tc]
                    t_mu2 = p.op("vector",
                                 lambda e, c=c1: e.tensor_scalar(
                                     sSTAT[:, c + 2:c + 3], sSTAT[:, c:c + 1], 1.0 / D,
                                     None, ALU.mult),
                                 waits=[t_sx2], inc=S_DVE)
                    t_m22 = p.op("vector",
                                 lambda e, c=c1: e.tensor_tensor(
                                     sSTAT[:, c + 3:c + 4], sSTAT[:, c + 2:c + 3],
                                     sSTAT[:, c + 2:c + 3], ALU.mult),
                                 waits=[t_mu2], inc=S_DVE)
                    t_var2 = p.op("vector",
                                  lambda e, c=c1: e.scalar_tensor_tensor(
                                      sSTAT[:, c + 4:c + 5], sSTAT[:, c + 1:c + 2], 1.0 / D,
                                      sSTAT[:, c + 3:c + 4], ALU.mult, ALU.subtract),
                                  waits=[t_sq2, t_m22], inc=S_DVE)
                    PR.var2[tc] = t_var2
                for tc in range(NTC):
                    c1 = 32 + tc * 8
                    t_ln2 = p.op("scalar",
                                 lambda e, c=c1: e.activation(
                                     sSTAT[:, c + 5:c + 6], sSTAT[:, c + 4:c + 5],
                                     AF.Ln, bias=sCV[:, 0:1]),
                                 waits=[PR.var2[tc]], inc=S_ACT)
                    t_rs2 = p.op("scalar",
                                 lambda e, c=c1: e.activation(
                                     sSTAT[:, c + 6:c + 7], sSTAT[:, c + 5:c + 6],
                                     AF.Exp, scale=-0.5),
                                 waits=[t_ln2], inc=S_ACT)
                    PR.rs2[tc] = t_rs2
            elif k == 4 or k == 5:
                for tc in [2 * (k - 4), 2 * (k - 4) + 1]:
                    c1 = 32 + tc * 8
                    xns = sXNT[:, tc * D:(tc + 1) * D]
                    t_t2 = p.op("vector",
                                lambda e, a=xns, c=c1: e.tensor_scalar(
                                    sH2T[:], a, sSTAT[:, c + 2:c + 3],
                                    sSTAT[:, c + 6:c + 7], ALU.subtract, ALU.mult),
                                waits=[PR.rs2[tc], G["h2t_pe"],
                                       PR.h2l_ev[-1] if PR.h2l_ev else None], inc=S_DVE)
                    t_h2 = p.op("vector",
                                lambda e: e.tensor_tensor(sH2T[:], sH2T[:], sGB[:], ALU.mult),
                                waits=[t_t2, tk_gb], inc=S_DVE)
                    if tc == NTC - 1:
                        PR.stat_dve = t_t2
                    for half in range(2):
                        trx, trh = [], []
                        for j in range(4):
                            dc = half * 4 + j
                            tok = p.op("tensor",
                                       lambda e, d=dc, t_=tc, jj=j: e.transpose(
                                           P1[:, jj * 128:(jj + 1) * 128],
                                           sXNT[:, t_ * D + d * 128: t_ * D + (d + 1) * 128],
                                           sIDF[:]),
                                       waits=[PR.xn[tc],
                                              PR.xnl_ev[-1] if PR.xnl_ev else None] +
                                             (G["p12_act"] if not PR.xnl_ev else []),
                                       inc=S_PE)
                            trx.append(tok)
                        te = p.op("scalar",
                                  lambda e, hh=half, t_=tc: e.activation(
                                      lane_ap(sXNBL, hh, t_), P1[:, 0:512], AF.Identity),
                                  waits=trx + [G["xnbl_pe"], G["up_dve"]], inc=S_ACT)
                        PR.xnl_ev.append(te)
                        for j in range(4):
                            dc = half * 4 + j
                            tok = p.op("tensor",
                                       lambda e, d=dc, jj=j: e.transpose(
                                           P2[:, jj * 128:(jj + 1) * 128],
                                           sH2T[:, d * 128:(d + 1) * 128], sIDF[:]),
                                       waits=[t_h2,
                                              PR.h2l_ev[-1] if PR.h2l_ev else None] +
                                             (G["p12_act"] if not PR.h2l_ev else []),
                                       inc=S_PE)
                            trh.append(tok)
                        te2 = p.op("scalar",
                                   lambda e, hh=half, t_=tc: e.activation(
                                       lane_ap(sH2L, hh, t_), P2[:, 0:512], AF.Identity),
                                   waits=trh + [G["h2l_pe"]], inc=S_ACT)
                        PR.h2l_ev.append(te2)
                    if tc == NTC - 1:
                        PR.h2t_pe = trh[-1]
            elif k == 6:
                drk_mm, bcr_mm = [], []
                for dc in range(NDC):
                    tok = p.op("tensor",
                               lambda e, d=dc: e.matmul(
                                   P1[0:64, 0:512], sXPW[:, d * 96: d * 96 + 64],
                                   sXNBL[:, d * T:(d + 1) * T],
                                   start=(d == 0), stop=(d == NDC - 1)),
                               waits=[PR.xnl_ev[-1], t_xpwb], inc=S_PE)
                    drk_mm.append(tok)
                for dc in range(NDC):
                    tok = p.op("tensor",
                               lambda e, d=dc: e.matmul(
                                   P2[0:32, 0:512], sXPW[:, d * 96 + 64: d * 96 + 96],
                                   sXNBL[:, d * T:(d + 1) * T],
                                   start=(d == 0), stop=(d == NDC - 1)),
                               waits=[PR.xnl_ev[-1], PR.h2l_ev[-1], t_xpwb], inc=S_PE)
                    bcr_mm.append(tok)
                PR.xnbl_pe = bcr_mm[-1]
                PR.t_drk = p.op("vector", lambda e: e.tensor_copy(sDRK[:], P1[0:64, 0:512]),
                                waits=[drk_mm[-1]], inc=S_DVE)
                PR.t_bcr = p.op("vector", lambda e: e.tensor_copy(sBCr[:], P2[0:32, 0:512]),
                                waits=[bcr_mm[-1]], inc=S_DVE)
            elif k == 7:
                sp_prev = [None, None]
                for dc in range(NDC):
                    par = dc % 2
                    PP = P1 if par == 0 else P2
                    t_mm = p.op("tensor",
                                lambda e, d=dc, PPl=PP: e.matmul(
                                    PPl[:, 0:512], sDTW[:, d * 128:(d + 1) * 128], sDRK[:],
                                    start=True, stop=True),
                                waits=[PR.t_drk, PR.t_bcr, t_dtwb, sp_prev[par],
                                       G["p12_mm_gate"]],
                                inc=S_PE)
                    t_e = p.op("scalar",
                               lambda e, d=dc, PPl=PP, pr=par: e.activation(
                                   sOUT[:, pr * 512:(pr + 1) * 512], PPl[:, 0:512],
                                   AF.Exp, bias=sDTB[:, d:d + 1]),
                               waits=[t_mm] + ([G["out_gate"]] if dc < 2 else []),
                               inc=S_ACT)
                    t_d = p.op("scalar",
                               lambda e, d=dc, pr=par: e.activation(
                                   sDLT[:, d * T:(d + 1) * T],
                                   sOUT[:, pr * 512:(pr + 1) * 512],
                                   AF.Ln, bias=sCV[:, 1:2]),
                               waits=[t_e, tk_cv, G["up_dve"]], inc=S_ACT)
                    sp_prev[par] = t_e
                    PR.dlt[dc] = t_d
                    PR.up[dc] = p.op("vector",
                                     lambda e, d=dc: e.tensor_tensor(
                                         sUP[:, d * T:(d + 1) * T],
                                         sDLT[:, d * T:(d + 1) * T],
                                         sXNBL[:, d * T:(d + 1) * T], ALU.mult),
                                     waits=[t_d, PR.xnl_ev[-1]], inc=S_DVE)
                PR.up_dve = PR.up[NDC - 1]
                PR.delta_mm_last = t_mm

        def new_prolog():
            PR = Prolog()
            PR.xn = {}
            PR.musave = {}
            PR.sums1 = {}
            PR.var1 = {}
            PR.rs1 = {}
            PR.sums2 = {}
            PR.var2 = {}
            PR.rs2 = {}
            PR.xnl_ev = []
            PR.h2l_ev = []
            PR.dlt = {}
            PR.up = {}
            return PR

        def adopt(PR):
            for k in ["xt_act", "xt_dve", "xnt_act", "stat_dve", "h2t_pe",
                      "xnbl_pe", "up_dve"]:
                G[k] = getattr(PR, k)

        tok_x0 = emit_xload(0)
        PRO = new_prolog()
        for k in range(8):
            emit_prolog_piece(PRO, 0, k, tok_x0)
        adopt(PRO)
        emit_deq1_dma(0)
        emit_deq1_dma(1)
        emit_deq1_dma(2)

        for ti in range(nt):
            r0 = ti * T
            init_zero = (r0 % SEQ) == 0
            dlt_toks = PRO.dlt
            up_toks = PRO.up
            h2l_last = PRO.h2l_ev[-1]
            t_bcr = PRO.t_bcr

            tok_xn_next = emit_xload(ti + 1) if ti + 1 < nt else None

            y_ms = {}
            for dc in range(NDC):
                y_ms[dc] = p.op("gpsimd",
                                lambda e, d=dc: e.memset(sY[:, d * T:(d + 1) * T], 0.0),
                                waits=[G["ylast_pe"], G["y_final"][dc]], inc=S_PL)

            y_last = dict(y_ms)
            scan_idx = 0
            hid_toks = {}
            ev_par = [None, None]
            fc1mm_par = [None, None]
            state_rd = None
            PRN = new_prolog() if ti + 1 < nt else None

            def emit_fc1(fb):
                par = fb % 2
                PP = P1 if par == 0 else P2
                if ti == 0:
                    wsrc = sWO
                    wbase = (fb % 4) * 1024
                    wgate = [deq["stt_fb"][fb]]
                else:
                    wsrc = sW
                    wbase = par * 1024
                    wgate = [p.op("sync",
                                  lambda e, f=fb, pr=par: e.dma_start(
                                      out=sW[:, pr * 1024:(pr + 1) * 1024],
                                      in_=w1bf[f, :, :]),
                                  waits=[fc1mm_par[par] if fc1mm_par[par]
                                         else G["w_free"][par], wb1_done[fb]],
                                  inc=S_W1[par], amt=16, dma=True)]
                mms = []
                for dc in range(NDC):
                    tok = p.op("tensor",
                               lambda e, d=dc, wb=wbase, W=wsrc, PPl=PP: e.matmul(
                                   PPl[:, 0:512],
                                   W[:, wb + d * 128: wb + (d + 1) * 128],
                                   sH2L[:, d * T:(d + 1) * T],
                                   start=(d == 0), stop=(d == NDC - 1)),
                               waits=wgate + [h2l_last, ev_par[par],
                                              dlt_toks[NDC - 1]], inc=S_PE)
                    mms.append(tok)
                fc1mm_par[par] = mms[-1]
                if ti == 0:
                    G["wo_rd"][fb % 4] = mms[-1]
                t_si = p.op("scalar",
                            lambda e, f=fb, PPl=PP: e.activation(
                                sHID[:, f * T:(f + 1) * T], PPl[:, 0:512],
                                AF.Identity, bias=sFC1B[:, f:f + 1]),
                            waits=[mms[-1], tk_f1b, G["hid_pe"], bfold_toks[fb]],
                            inc=S_ACT)
                ev_par[par] = t_si
                hid_toks[fb] = t_si

            bc_ev = {}
            pre_da = {}

            def emit_bc_pe(n):
                t_bcB = p.op("tensor",
                             lambda e, nn=n: e.matmul(
                                 PB[:, 0:512], sSEL[:, nn * 128:(nn + 1) * 128],
                                 sBCr[:], start=True, stop=True),
                             waits=[t_bcr, G["pb_free"]], inc=S_PE)
                t_bcC = p.op("tensor",
                             lambda e, nn=n: e.matmul(
                                 PC[:, 0:512], sSEL[:, (nn + 16) * 128:(nn + 17) * 128],
                                 sBCr[:], start=True, stop=True),
                             waits=[t_bcr, G["pc_free"]], inc=S_PE)
                return (t_bcB, t_bcC)

            def emit_bc_ev(n, pe_toks):
                par = n % 2
                t_evB = p.op("scalar",
                             lambda e, pr=par: e.activation(
                                 sBCB[:, pr * T:(pr + 1) * T], PB[:, 0:512], AF.Identity),
                             waits=[pe_toks[0], G["bcb_rd"][par]], inc=S_ACT)
                G["pb_free"] = t_evB
                t_evC = p.op("scalar",
                             lambda e, pr=par: e.activation(
                                 sBCC[:, pr * T:(pr + 1) * T], PC[:, 0:512], AF.Identity),
                             waits=[pe_toks[1], G["bcc_free"][par]], inc=S_ACT)
                G["pc_free"] = t_evC
                bc_ev[n] = (t_evB, t_evC)

            def scan_group(n):
                nonlocal scan_idx, state_rd
                parB = n % 2
                parC = n % 2
                t_evB, t_evC = bc_ev[n]
                pe_next = emit_bc_pe(n + 1) if n + 1 < NST else None
                last_db = None

                def rep2(tile, par):
                    ap = tile[:]
                    return AP(ap.tensor, ap.offset + par * T,
                              [[ap.ap[0][0], 128], [0, 2], [1, T]])

                for j in range(NDC // 2):
                    dc0, dc1 = 2 * j, 2 * j + 1
                    pair0 = dc0 * NST + n
                    pair1 = dc1 * NST + n
                    parA0 = scan_idx % ROT
                    parA1 = (scan_idx + 1) % ROT
                    pj = scan_idx // 2
                    parU = pj % 2
                    parM = pj % 2
                    if n == 8 and dc0 in pre_da:
                        t_da0, ap0, slot0 = pre_da[dc0]
                    else:
                        ap0 = sDA[:, parA0 * T:(parA0 + 1) * T]
                        slot0 = parA0
                        t_da0 = p.op("scalar",
                                     lambda e, d=dc0, nn=n, a=ap0: e.activation(
                                         a, sDLT[:, d * T:(d + 1) * T],
                                         AF.Exp, scale=sA[:, d * 16 + nn: d * 16 + nn + 1]),
                                     waits=[dlt_toks[dc0], t_a, G["da_free"][parA0]],
                                     inc=S_ACT)
                    if n == 8 and dc1 in pre_da:
                        t_da1, ap1, slot1 = pre_da[dc1]
                    else:
                        ap1 = sDA[:, parA1 * T:(parA1 + 1) * T]
                        slot1 = parA1
                        t_da1 = p.op("scalar",
                                     lambda e, d=dc1, nn=n, a=ap1: e.activation(
                                         a, sDLT[:, d * T:(d + 1) * T],
                                         AF.Exp, scale=sA[:, d * 16 + nn: d * 16 + nn + 1]),
                                     waits=[dlt_toks[dc1], t_a, G["da_free"][parA1]],
                                     inc=S_ACT)
                    t_db = p.op("vector",
                                lambda e, d=dc0, pu=parU, prB=parB: e.tensor_tensor(
                                    sDBU[:, pu * 2 * T:(pu + 1) * 2 * T],
                                    sUP[:, d * T:(d + 2) * T],
                                    rep2(sBCB, prB), ALU.mult),
                                waits=[up_toks[dc0], up_toks[dc1], t_evB,
                                       G["dbu_free"][parU]], inc=S_DVE)
                    last_db = t_db
                    init0 = 0.0 if init_zero else sSTATE[:, pair0:pair0 + 1]
                    init1 = 0.0 if init_zero else sSTATE[:, pair1:pair1 + 1]
                    t_sc0 = p.op("vector",
                                 lambda e, a=ap0, pu=parU, ii=init0: e.tensor_tensor_scan(
                                     sH[:, 0:T], a,
                                     sDBU[:, pu * 2 * T:pu * 2 * T + T],
                                     ii, ALU.mult, ALU.add),
                                 waits=[t_da0, t_db, G["h_free"][0]] +
                                       ([] if init_zero else [G["st_tok"].get(pair0)]),
                                 inc=S_DVE)
                    t_sc1 = p.op("vector",
                                 lambda e, a=ap1, pu=parU, ii=init1: e.tensor_tensor_scan(
                                     sH[:, T:2 * T], a,
                                     sDBU[:, pu * 2 * T + T:(pu + 1) * 2 * T],
                                     ii, ALU.mult, ALU.add),
                                 waits=[t_da1, G["h_free"][1]] +
                                       ([] if init_zero else [G["st_tok"].get(pair1)]),
                                 inc=S_DVE)
                    if slot0 is not None:
                        G["da_free"][slot0] = t_sc0
                    else:
                        G["p12_act"] = G["p12_act"] + [t_sc0]
                    if slot1 is not None:
                        G["da_free"][slot1] = t_sc1
                    else:
                        G["p12_act"] = G["p12_act"] + [t_sc1]
                    G["dbu_free"][parU] = t_sc1
                    t_st = p.op("vector",
                                lambda e, pp=pair1: e.tensor_copy(
                                    _state_pair_dst(sSTATE, pp),
                                    _state_pair_src(sH, 1)),
                                waits=[t_sc1], inc=S_DVE)
                    state_rd = t_st
                    G["st_tok"][pair0] = t_st
                    G["st_tok"][pair1] = t_st
                    t_tm = p.op("vector",
                                lambda e, pm=parM, prC=parC: e.tensor_tensor(
                                    sTMP[:, pm * 2 * T:(pm + 1) * 2 * T],
                                    sH[:, 0:2 * T],
                                    rep2(sBCC, prC), ALU.mult),
                                waits=[t_sc1, t_evC, G["tmp_free"][parM]],
                                inc=S_DVE)
                    G["h_free"][0] = t_tm
                    G["h_free"][1] = t_tm
                    t_y = p.op("gpsimd",
                               lambda e, d=dc0, pm=parM: e.dma_start(
                                   out=sY[:, d * T:(d + 2) * T],
                                   in_=sTMP[:, pm * 2 * T:(pm + 1) * 2 * T],
                                   accum_op=ALU.add),
                               waits=[t_tm, y_last[dc0], y_last[dc1]],
                               inc=S_YA[dc0], amt=16, dma=True)
                    G["tmp_free"][parM] = t_y
                    y_last[dc0] = t_y
                    y_last[dc1] = t_y
                    scan_idx += 2
                G["bcb_rd"][parB] = last_db
                G["bcc_free"][parC] = y_last[NDC - 1]
                if pe_next is not None:
                    emit_bc_ev(n + 1, pe_next)

            emit_bc_ev(0, emit_bc_pe(0))
            for n in range(8):
                scan_group(n)
                for f in range(4 * n, 4 * n + 4):
                    if ti == 0:
                        emit_deq1(f)
                    emit_fc1(f)
                    if ti == 0 and f + 3 < NFB:
                        emit_deq1_dma(f + 3)
            if ti == 0:
                G["pf_free"] = [list(t) for t in deq_pf_free]
                deq["wb2_par"] = list(deq["wb_par"])
                emit_deq2_dma(0)
                emit_deq2_dma(1)
                emit_deq2_dma(2)

            for dc in range(6 if ti != 0 else 4):
                if ti != 0 and dc < 2:
                    dap = (P1 if dc == 0 else P2)[:, 0:512]
                    w = [dlt_toks[dc], t_a, ev_par[dc]]
                    sl = None
                else:
                    sl = dc % 4
                    dap = sDA[:, sl * T:(sl + 1) * T]
                    w = [dlt_toks[dc], t_a, G["da_free"][sl]]
                tok = p.op("scalar",
                           lambda e, d=dc, a=dap: e.activation(
                               a, sDLT[:, d * T:(d + 1) * T],
                               AF.Exp, scale=sA[:, d * 16 + 8: d * 16 + 9]),
                           waits=w, inc=S_ACT)
                pre_da[dc] = (tok, dap, sl)

            def emit_silus(j):
                for fb in range(8 * j, 8 * j + 8):
                    hid_toks[fb] = p.op("scalar",
                                        lambda e, f=fb: e.activation(
                                            sHID[:, f * T:(f + 1) * T],
                                            sHID[:, f * T:(f + 1) * T], AF.Silu),
                                        waits=[hid_toks[fb]], inc=S_ACT)

            G["p12_act"] = [ev_par[0], ev_par[1]]

            f2mm_rot = [None] * 4
            f2mm = {0: {}, 1: {}}
            f2_last_mm = None
            t3a_early = {}

            def emit_fc2_group(tcp, fb):
                nonlocal f2_last_mm
                rot = fb % 4
                if tcp == 0 and ti == 0:
                    wsrc = sWO
                    wbase = (fb % 4) * 1024
                    wgate = [deq["stt_fb"][(1, fb)]]
                else:
                    if f2mm_rot[rot] is not None:
                        gate = f2mm_rot[rot]
                    elif rot < 2:
                        gate = fc1mm_par[rot]
                    else:
                        gate = G["w_free"][rot]
                    wsrc = sW
                    wbase = rot * 1024
                    wgate = [p.op("sync",
                                  lambda e, f=fb, pr=rot: e.dma_start(
                                      out=sW[:, pr * 1024:(pr + 1) * 1024],
                                      in_=w2bf[f, :, :]),
                                  waits=[gate, wb2_done[fb]], inc=S_W2[rot],
                                  amt=16, dma=True)]
                for i2 in range(2):
                    tc = tcp * 2 + i2
                    for db in range(2):
                        pi = i2 * 2 + db
                        if fb == 0:
                            pfg = G["pf_free"][pi] if tcp == 0 else [t3a_early[pi]]
                        else:
                            pfg = []
                        tok = p.op("tensor",
                                   lambda e, f=fb, t_=tc, d2=db, p_=pi, W=wsrc,
                                   wb=wbase: e.matmul(
                                       PF[p_][:, 0:512],
                                       sHID[:, f * T + t_ * 128: f * T + (t_ + 1) * 128],
                                       W[:, wb + d2 * 512: wb + (d2 + 1) * 512],
                                       start=(f == 0), stop=(f == NFB - 1)),
                                   waits=wgate + [hid_toks[fb]] + pfg, inc=S_PE)
                        f2mm[tcp][pi] = tok
                        f2_last_mm = tok
                if tcp == 0 and ti == 0:
                    G["wo_rd"][fb % 4] = tok
                else:
                    f2mm_rot[rot] = tok

            for j in range(4):
                emit_silus(j)
            for n in range(8, 12):
                scan_group(n)
                for fb in range(8 * (n - 8), 8 * (n - 8) + 8):
                    if ti == 0:
                        emit_deq2(fb, G["p12_act"] if fb < 2 else [])
                    emit_fc2_group(0, fb)
                    if ti == 0 and fb + 3 < NFB:
                        emit_deq2_dma(fb + 3)
                if PRN is not None:
                    j = n - 8
                    if j == 0:
                        emit_prolog_piece(PRN, ti + 1, 0, tok_xn_next, [0, 1])
                    elif j == 1:
                        emit_prolog_piece(PRN, ti + 1, 0, tok_xn_next, [2, 3])
                        emit_prolog_piece(PRN, ti + 1, 1, tok_xn_next)
                    elif j == 2:
                        emit_prolog_piece(PRN, ti + 1, 2, tok_xn_next, [0, 1])
                    else:
                        emit_prolog_piece(PRN, ti + 1, 2, tok_xn_next, [2, 3])
                        emit_prolog_piece(PRN, ti + 1, 3, tok_xn_next)
            if ti == 0:
                late_free = [deq["stt_h"][0], deq["stt_h"][1],
                             wb2_done[NFB - 4], wb2_done[NFB - 3],
                             wb2_done[NFB - 2], wb2_done[NFB - 1]] + \
                    list(G["wo_rd"])
                G["t3_free"] = [list(late_free) for _ in range(4)]
                G["gh_free"] = [list(late_free), list(late_free)]
                G["cw_free"] = list(late_free)
                G["tail_gate"] = list(late_free)
                G["p12_mm_gate"] = list(late_free)
                G["p12_act"] = G["p12_act"] + [deq["stt_h"][0], deq["stt_h"][1]]

            for pi in range(4):
                db = pi % 2
                t3a_early[pi] = p.op("vector",
                                     lambda e, p_=pi, d2=db: e.scalar_tensor_tensor(
                                         sT3[:, p_ * T:(p_ + 1) * T], PF[p_][:, 0:512], 0.5,
                                         sB2B[:, d2 * 512:(d2 + 1) * 512],
                                         ALU.mult, ALU.add),
                                     waits=[f2mm[0][pi], G["t3_free"][pi], tk_b2],
                                     inc=S_DVE)

            for n in range(12, NST):
                scan_group(n)
                for fb in range(8 * (n - 12), 8 * (n - 12) + 8):
                    emit_fc2_group(1, fb)
                if PRN is not None and n - 8 < 6:
                    emit_prolog_piece(PRN, ti + 1, n - 8, tok_xn_next)
            if PRN is not None:
                emit_prolog_piece(PRN, ti + 1, 6, tok_xn_next)
                emit_prolog_piece(PRN, ti + 1, 7, tok_xn_next)

            comb_prev = None
            trt_last = None
            out_prev = G["out_gate"]
            syt_free = G["syt_last"]
            yt_free = [None, None]
            t3slot_g = [None] * 4
            if PRN is not None:
                p12_gate = [PRN.dlt[NDC - 2], PRN.dlt[NDC - 1]]
                out_wr_gate = [PRN.dlt[NDC - 2], PRN.dlt[NDC - 1]]
            else:
                p12_gate = [ev_par[0], ev_par[1]]
                out_wr_gate = [None, None]

            def emit_xc(tc, gate):
                return p.op("sync",
                            lambda e, t_=tc, r=r0: e.dma_start(
                                out=sXC[:, (t_ % 2) * D:(t_ % 2 + 1) * D],
                                in_=dram["xT"][r + t_ * 128: r + (t_ + 1) * 128, :]),
                            waits=[gate, G["tail_gate"]], inc=S_XC[tc % 2],
                            amt=16, dma=True)

            xc_toks = {0: emit_xc(0, G["xc_free"][0]),
                       1: emit_xc(1, G["xc_free"][1])}
            for tcp in range(2):
                if tcp == 1:
                    for pi in range(4):
                        db = pi % 2
                        t3a_early[pi] = p.op("vector",
                                             lambda e, p_=pi, d2=db: e.scalar_tensor_tensor(
                                                 sT3[:, p_ * T:(p_ + 1) * T],
                                                 PF[p_][:, 0:512], 0.5,
                                                 sB2B[:, d2 * 512:(d2 + 1) * 512],
                                                 ALU.mult, ALU.add),
                                             waits=[f2mm[1][pi], t3slot_g[pi], tk_b2],
                                             inc=S_DVE)
                        G["pf_free"][pi] = [t3a_early[pi]]
                for i2 in range(2):
                    tc = tcp * 2 + i2
                    xb = (tc % 2) * D
                    tok_xc = xc_toks[tc]
                    this_out_gate = out_prev
                    halves = []
                    for db in range(2):
                        pi = i2 * 2 + db
                        PY = P1 if db == 0 else P2
                        t_xn2 = p.op("vector",
                                     lambda e, t_=tc, d2=db, x0=xb, pp=ti % 2: e.tensor_scalar(
                                         sCW[:], sXC[:, x0 + d2 * 512:x0 + (d2 + 1) * 512],
                                         sSTAT[:, 64 + 8 * pp + 2 * t_: 65 + 8 * pp + 2 * t_],
                                         sSTAT[:, 65 + 8 * pp + 2 * t_: 66 + 8 * pp + 2 * t_],
                                         ALU.subtract, ALU.mult),
                                     waits=[tok_xc, PRO.musave[tc], G["cw_free"]], inc=S_DVE)
                        t_t3 = p.op("vector",
                                    lambda e, p_=pi: e.tensor_tensor(
                                        sT3[:, p_ * T:(p_ + 1) * T],
                                        sT3[:, p_ * T:(p_ + 1) * T], sCW[:], ALU.add),
                                    waits=[t3a_early[pi], t_xn2], inc=S_DVE)
                        t_g = p.op("scalar",
                                   lambda e, pr=db, p_=pi: e.activation(
                                       sGH[:, pr * T:(pr + 1) * T],
                                       sT3[:, p_ * T:(p_ + 1) * T], AF.Silu),
                                   waits=[t_t3, G["gh_free"][db]], inc=S_ACT)
                        t3slot_g[pi] = t_g
                        G["t3_free"][pi] = t_g
                        trt = []
                        for j in range(4):
                            dc = db * 4 + j
                            tok = p.op("tensor",
                                       lambda e, d=dc, t_=tc, jj=j, PYl=PY: e.transpose(
                                           PYl[:, jj * 128:(jj + 1) * 128],
                                           sY[:, d * T + t_ * 128: d * T + (t_ + 1) * 128],
                                           sIDF[:]),
                                       waits=[y_last[dc], yt_free[db],
                                              p12_gate[db]], inc=S_PE)
                            trt.append(tok)
                        trt_last = trt[-1]
                        t_yt = p.op("vector",
                                    lambda e, PYl=PY: e.tensor_copy(sYT[:], PYl[:, 0:512]),
                                    waits=trt + [G["tail_gate"], syt_free],
                                    inc=S_DVE)
                        yt_free[db] = t_yt
                        t_cw = p.op("vector",
                                    lambda e, d2=db: e.tensor_tensor(
                                        sCW[:], sCW[:],
                                        sDB[:, d2 * 512:(d2 + 1) * 512], ALU.mult),
                                    waits=[t_t3, t_yt, tk_db], inc=S_DVE)
                        t_s = p.op("vector",
                                   lambda e: e.tensor_tensor(sYT[:], sYT[:], sCW[:], ALU.add),
                                   waits=[t_cw], inc=S_DVE)
                        G["cw_free"] = t_s
                        t_m = p.op("vector",
                                   lambda e, pr=db: e.tensor_tensor(
                                       sYT[:], sYT[:], sGH[:, pr * T:(pr + 1) * T], ALU.mult),
                                   waits=[t_s, t_g], inc=S_DVE)
                        G["gh_free"][db] = t_m
                        t_o = p.op("vector",
                                   lambda e, d2=db, x0=xb: e.tensor_tensor(
                                       sOUT[:, d2 * 512:(d2 + 1) * 512], sYT[:],
                                       sXC[:, x0 + d2 * 512:x0 + (d2 + 1) * 512], ALU.add),
                                   waits=[t_m, tok_xc, this_out_gate,
                                          out_wr_gate[db]], inc=S_DVE)
                        comb_prev = t_o
                        syt_free = t_o
                        halves.append(t_o)
                    if tc + 2 < NTC:
                        xc_toks[tc + 2] = emit_xc(tc + 2, halves[-1])
                    G["xc_free"][tc % 2] = halves[-1]
                    t_store = p.op("sync",
                                   lambda e, t_=tc, r=r0: e.dma_start(
                                       out=outT[r + t_ * 128: r + (t_ + 1) * 128, :],
                                       in_=sOUT[:]),
                                   waits=halves, inc=S_OUT, amt=16, dma=True)
                    out_prev = t_store
            G["out_gate"] = out_prev
            G["syt_last"] = syt_free
            G["musave_rd"][ti % 2] = comb_prev
            G["ylast_pe"] = trt_last
            G["y_final"] = dict(y_last)
            G["hid_pe"] = f2_last_mm
            G["h2l_pe"] = fc1mm_par[1]
            G["w_free"] = list(f2mm_rot)
            if PRN is not None:
                adopt(PRN)
                PRO = PRN

        p.op("sync", lambda e: e.nop(), waits=[(S_OUT, S_OUT.n)])

        with nc.Block() as block:
            p.emit(block)

    return nc


def _host_stage(L0_attn_x, L0_attn_A_log, L0_attn_D, L0_attn_x_proj_w,
                L0_attn_dt_proj_w, L0_attn_dt_proj_b, L0_ffn_ln_gamma,
                L0_ffn_ln_beta, L0_ffn_fc1_q, L0_ffn_fc1_scales, L0_ffn_fc1_bias,
                L0_ffn_fc2_q, L0_ffn_fc2_scales, L0_ffn_fc2_bias):
    x = np.ascontiguousarray(np.asarray(L0_attn_x, dtype=np.float32))
    assert x.shape == (16, SEQ, D)

    A_c = np.ascontiguousarray(
        np.asarray(L0_attn_A_log, np.float32).reshape(NDC, 128, NST)
        .transpose(1, 0, 2).reshape(128, 128))
    xpwT = np.asarray(L0_attn_x_proj_w, np.float32).T
    xpwT_c = np.ascontiguousarray(
        xpwT.reshape(NDC, 128, 96).transpose(1, 0, 2).reshape(128, NDC * 96))
    dtwT = np.ascontiguousarray(np.asarray(L0_attn_dt_proj_w, np.float32).T)
    dtb_c = np.ascontiguousarray(np.asarray(L0_attn_dt_proj_b, np.float32).reshape(NDC, 128).T)
    fc1b_c = np.ascontiguousarray(np.asarray(L0_ffn_fc1_bias, np.float32).reshape(NFB, 128).T)
    beta_c = np.ascontiguousarray(np.asarray(L0_ffn_ln_beta, np.float32).reshape(NDC, 128).T)
    rows4 = np.ascontiguousarray(np.stack([
        np.asarray(L0_ffn_ln_gamma, np.float32),
        np.asarray(L0_ffn_ln_beta, np.float32),
        np.asarray(L0_attn_D, np.float32),
        np.asarray(L0_ffn_fc2_bias, np.float32)], axis=0))
    q1blk = np.ascontiguousarray(
        np.asarray(L0_ffn_fc1_q, np.int32).reshape(NDC, 128, NFB, 128).transpose(2, 0, 1, 3))
    sc1r = np.ascontiguousarray(
        np.asarray(L0_ffn_fc1_scales, np.float32).reshape(NDC, NFB, 128)
        .transpose(1, 0, 2).reshape(NFB, NDC * 128))
    q2blk = np.ascontiguousarray(np.asarray(L0_ffn_fc2_q, np.int32).reshape(NFB, 128, D))
    sc2r = np.ascontiguousarray(np.asarray(L0_ffn_fc2_scales, np.float32))
    idf = np.eye(128, dtype=np.float32)
    id2b = (2.0 * np.eye(128)).astype(BF)
    onesf = np.ones((1, 128), dtype=np.float32)
    cvec = np.ascontiguousarray(np.stack([np.full(128, EPS, np.float32), np.ones(128, np.float32)], axis=1))
    selb = np.zeros((32, 32 * 128), dtype=ml_dtypes.float8_e4m3)
    for k in range(32):
        selb[k, k * 128:(k + 1) * 128] = ml_dtypes.float8_e4m3(1.0)

    shared = dict(A_c=A_c, xpwT_c=xpwT_c, dtwT=dtwT, dtb_c=dtb_c, fc1b_c=fc1b_c,
                  beta_c=beta_c,
                  rows4=rows4, q1blk=q1blk, sc1r=sc1r, q2blk=q2blk, sc2r=sc2r,
                  idf=idf, id2b=id2b, onesf=onesf, selb=selb, cvec=cvec)
    return shared, x


def kernel(**inputs):
    shared, x = _host_stage(**inputs)
    nc = build_nc(NT)
    in_maps = []
    for c in range(NCORE):
        m = dict(shared)
        m["xT"] = np.ascontiguousarray(x[2 * c:2 * c + 2].reshape(TPC, D))
        in_maps.append(m)

    res = run_bass_kernel_spmd(nc, in_maps, core_ids=list(range(NCORE)))
    out = np.empty((16, SEQ, D), dtype=np.float32)
    for c in range(NCORE):
        out[2 * c:2 * c + 2] = np.asarray(res.results[c]["outT"]).reshape(2, SEQ, D)
    return out


# version 39
# speedup vs baseline: 1.1973x; 1.0193x over previous
import numpy as np
import ml_dtypes
import contextlib
import concourse.bass as bass
import concourse.mybir as mybir
from concourse.bass import AP
from concourse.bass_utils import run_bass_kernel_spmd

F32 = mybir.dt.float32
FP8 = mybir.dt.float8e4
BF16 = mybir.dt.bfloat16
I32 = mybir.dt.int32
ALU = mybir.AluOpType
AF = mybir.ActivationFunctionType
BF = ml_dtypes.bfloat16

D = 1024
NST = 16
RK = 64
FF = 4096
NCORE = 8
SEQ = 2048
TPC = 2 * SEQ
T = 512
NT = TPC // T
NTC = T // 128
NDC = D // 128
NFB = FF // 128
EPS = 1e-5

ENGS = ("sync", "tensor", "scalar", "vector", "gpsimd")


class Sem:
    __slots__ = ("h", "n", "name")

    def __init__(self, h, name):
        self.h = h
        self.n = 0
        self.name = name


class Planner:

    def __init__(self):
        self.q = {e: [] for e in ENGS}

    def op(self, eng, fn, waits=(), inc=None, amt=1, dma=False):
        ws = []
        for w in waits:
            if w is None:
                continue
            if isinstance(w, list):
                ws.extend(t for t in w if t is not None)
            else:
                ws.append(w)
        tgt = None
        if inc is not None:
            inc.n += amt
            tgt = inc.n
        self.q[eng].append((fn, tuple(ws), inc, tgt, amt, dma))
        return (inc, tgt) if inc is not None else None

    SELF_SER = False

    def emit(self, block):
        for ename in ENGS:
            oplist = self.q[ename]
            deco = getattr(block, ename)

            def section(eng, _ops=oplist):
                marks = {}
                prev_tok = None
                for fn, waits, inc, tgt, amt, dma in _ops:
                    allw = list(waits)
                    if Planner.SELF_SER and not dma and prev_tok is not None:
                        allw.append(prev_tok)
                    for (sem, val) in allw:
                        if marks.get(id(sem), -1) >= val:
                            continue
                        eng.wait_ge(sem.h, val)
                        marks[id(sem)] = val
                    instr = fn(eng)
                    if inc is not None:
                        instr.then_inc(inc.h, amt)
                        if not dma:
                            prev_tok = (inc, tgt)

            deco(section)


def _stat_save_src(sSTAT, c0):
    ap = sSTAT[:]
    return AP(ap.tensor, ap.offset + c0 + 2, [[ap.ap[0][0], 128], [4, 2]])


def _stat_save_dst(sSTAT, tc, par):
    ap = sSTAT[:]
    return AP(ap.tensor, ap.offset + 64 + 8 * par + 2 * tc, [[ap.ap[0][0], 128], [1, 2]])


def _state_pair_src(sH, parH):
    ap = sH[:]
    return AP(ap.tensor, ap.offset + (parH - 1) * T + (T - 1),
              [[ap.ap[0][0], 128], [T, 2]])


def _state_pair_dst(sSTATE, pair):
    ap = sSTATE[:]
    return AP(ap.tensor, ap.offset + (pair - NST), [[ap.ap[0][0], 128], [NST, 2]])


def lane_ap(tile, half, tc):
    ap = tile[:]
    base = (half * 4) * T + tc * 128
    return AP(ap.tensor, ap.offset + base, [[ap.ap[0][0], 128], [T, 4], [1, 128]])


def build_nc(nt):
    nc = bass.Bass("TRN2", target_bir_lowering=False)
    dram = {}

    def din(name, shape, dt):
        dram[name] = nc.dram_tensor(name, shape, dt, kind="ExternalInput").ap()

    din("xT", [TPC, D], F32)
    din("A_c", [128, 128], F32)
    din("xpwT_c", [128, NDC * 96], F32)
    din("dtwT", [RK, D], F32)
    din("dtb_c", [128, NDC], F32)
    din("fc1b_c", [128, NFB], F32)
    din("beta_c", [128, NDC], F32)
    din("rows4", [4, D], F32)
    din("q1blk", [NFB, NDC, 128, 128], I32)
    din("sc1r", [NFB, NDC * 128], F32)
    din("q2blk", [NFB, 128, D], I32)
    din("sc2r", [NFB, D], F32)
    din("idf", [128, 128], F32)
    din("id2b", [128, 128], BF16)
    din("glane", [128, NDC], F32)
    din("onesf", [1, 128], F32)
    din("cvec", [128, 2], F32)
    din("selb", [32, 32 * 128], FP8)

    outT = nc.dram_tensor("outT", [TPC, D], F32, kind="ExternalOutput").ap()
    w1bf = nc.dram_tensor("w1bf", [NFB, 128, NDC * 128], BF16, kind="Internal").ap()
    w2bf = nc.dram_tensor("w2bf", [NFB, 128, D], BF16, kind="Internal").ap()

    es = contextlib.ExitStack()
    with es:
        def sbuf(name, shape, dt):
            return es.enter_context(nc.sbuf_tensor(name, shape, dt))

        def psum(name, shape, dt):
            return es.enter_context(nc.psum_tensor(name, shape, dt))

        sA = sbuf("sA", [128, 128], F32)
        sXPW = sbuf("sXPW", [128, NDC * 96], BF16)
        sDTW = sbuf("sDTW", [RK, D], BF16)
        sDTB = sbuf("sDTB", [128, NDC], F32)
        sFC1B = sbuf("sFC1B", [128, NFB], F32)
        sGB = sbuf("sGB", [128, D], BF16)
        sBETA = sbuf("sBETA", [128, NDC], BF16)
        sDB = sbuf("sDB", [128, D], F32)
        sB2B = sbuf("sB2B", [128, D], F32)
        sIDF = sbuf("sIDF", [128, 128], F32)
        sID2 = sbuf("sID2", [128, 128], BF16)
        sONESF = sbuf("sONESF", [1, 128], F32)
        sCV = sbuf("sCV", [128, 2], F32)
        sGL = sbuf("sGL", [128, NDC], F32)
        sSEL = sbuf("sSEL", [32, 32 * 128], FP8)

        pro = contextlib.ExitStack()
        sCLD = pro.enter_context(nc.sbuf_tensor("sCLD", [128, 2048], F32))
        sROWP = pro.enter_context(nc.sbuf_tensor("sROWP", [1, 1024], F32))
        sSCF = pro.enter_context(nc.sbuf_tensor("sSCF", [32, 1024], F32))
        sWO = pro.enter_context(nc.sbuf_tensor("sWO", [128, 4 * 1024], BF16))
        sSC1 = pro.enter_context(nc.sbuf_tensor("sSC1", [NFB, NDC * 128], BF16))
        sSC2 = pro.enter_context(nc.sbuf_tensor("sSC2", [NFB, D], BF16))
        sQ = pro.enter_context(nc.sbuf_tensor("sQ", [128, 3 * 1024], I32))
        pro.close()
        sY = sbuf("sY", [128, NDC * T], F32)
        sT3 = sbuf("sT3", [128, 4 * T], F32)
        sGH = sbuf("sGH", [128, 2 * T], F32)
        sYT = sbuf("sYT", [128, T], F32)
        sCW = sbuf("sCW", [128, T], F32)
        sXC = sbuf("sXC", [128, 2 * D], F32)

        sXT = sbuf("sXT", [128, NTC * D], F32)
        sXNT = sbuf("sXNT", [128, NTC * D], F32)
        sSTAT = sbuf("sSTAT", [128, 80], F32)
        sXNBL = sbuf("sXNBL", [128, NDC * T], BF16)
        sH2T = sbuf("sH2T", [128, 2 * D], BF16)
        sH2L = sbuf("sH2L", [128, NDC * T], BF16)
        sDRK = sbuf("sDRK", [RK, T], BF16)
        sBCr = sbuf("sBCr", [32, T], BF16)
        sDLT = sbuf("sDLT", [128, NDC * T], F32)
        sUP = sbuf("sUP", [128, NDC * T], BF16)
        sDA = sbuf("sDA", [128, 4 * T], F32)
        sDBU = sbuf("sDBU", [128, 4 * T], BF16)
        sH = sbuf("sH", [128, 2 * T], BF16)
        sTMP = sbuf("sTMP", [128, 4 * T], BF16)
        sBCB = sbuf("sBCB", [128, 2 * T], BF16)
        sBCC = sbuf("sBCC", [128, 2 * T], BF16)
        sSTATE = sbuf("sSTATE", [128, 128], F32)
        sW = sbuf("sW", [128, 4 * 1024], BF16)
        sHID = sbuf("sHID", [128, NFB * T], BF16)
        sOUT = sbuf("sOUT", [128, D], F32)

        P1 = psum("P1", [128, 512], F32)
        P2 = psum("P2", [128, 512], F32)
        PB = psum("PB", [128, 512], F32)
        PC = psum("PC", [128, 512], F32)
        PF = [psum(f"PF{i}", [128, 512], F32) for i in range(4)]

        names = ["C", "SC", "ROW", "Qa", "Qb", "Qc", "WBa", "WBb", "WBc", "WBd",
                 "W1a", "W1b",
                 "W2a", "W2b", "W2c", "W2d", "X", "XCa", "XCb", "OUT", "PE",
                 "ACT", "DVE", "PL"] + [f"YA{d}" for d in range(8)]
        sems = {k: Sem(es.enter_context(nc.semaphore(k)), k) for k in names}
        S_C, S_SC, S_ROW, S_X, S_OUT, S_PE, S_ACT, S_DVE, S_PL = (
            sems[k] for k in ["C", "SC", "ROW", "X", "OUT", "PE", "ACT", "DVE", "PL"])
        S_Q = [sems["Qa"], sems["Qb"], sems["Qc"]]
        S_XC = [sems["XCa"], sems["XCb"]]
        S_YA = [sems[f"YA{d}"] for d in range(8)]
        S_WB = [sems["WBa"], sems["WBb"], sems["WBc"], sems["WBd"]]
        S_W1 = [sems["W1a"], sems["W1b"]]
        S_W2 = [sems["W2a"], sems["W2b"], sems["W2c"], sems["W2d"]]

        p = Planner()

        for nm, dst in [("A_c", sCLD[:, 0:128]), ("xpwT_c", sCLD[:, 128:128 + NDC * 96]),
                        ("dtwT", sCLD[0:RK, 1024:2048]), ("dtb_c", sDTB[:]),
                        ("fc1b_c", sFC1B[:]), ("beta_c", sCLD[0:128, 896:904]),
                        ("idf", sIDF[:]), ("id2b", sID2[:]), ("onesf", sONESF[:]),
                        ("cvec", sCV[:]), ("glane", sGL[:]), ("selb", sSEL[:])]:
            p.op("sync", lambda e, d=dst, n=nm: e.dma_start(out=d, in_=dram[n][:]),
                 inc=S_C, amt=16, dma=True)
        tok_consts = (S_C, S_C.n)
        tk_A = tk_xpw = tk_dtw = tk_dtb = tk_f1b = tk_idf = tk_onef = tk_cv = tk_sel = tok_consts
        tk_s1 = p.op("sync", lambda e: e.dma_start(out=sSCF[:], in_=dram["sc1r"][:]), inc=S_SC, amt=16, dma=True)

        t_ae = p.op("scalar", lambda e: e.activation(sA[:], sCLD[:, 0:128], AF.Exp), waits=[tk_A], inc=S_ACT)
        t_a = p.op("vector", lambda e: e.tensor_scalar(sA[:], sA[:], -1.0, None, ALU.mult), waits=[t_ae], inc=S_DVE)
        t_xpwb = p.op("vector", lambda e: e.tensor_copy(sXPW[:], sCLD[:, 128:128 + NDC * 96]), waits=[tk_xpw], inc=S_DVE)
        t_beta = p.op("vector", lambda e: e.tensor_copy(sBETA[:], sCLD[0:128, 896:904]), waits=[tok_consts], inc=S_DVE)
        t_dtwb = p.op("vector", lambda e: e.tensor_copy(sDTW[:], sCLD[0:RK, 1024:2048]), waits=[tk_dtw], inc=S_DVE)
        t_sc1b = p.op("vector", lambda e: e.tensor_copy(sSC1[:], sSCF[:]), waits=[tk_s1], inc=S_DVE)
        tk_s2 = p.op("sync", lambda e: e.dma_start(out=sSCF[:], in_=dram["sc2r"][:]), waits=[t_sc1b], inc=S_SC, amt=16, dma=True)
        t_sc2b = p.op("vector", lambda e: e.tensor_copy(sSC2[:], sSCF[:]), waits=[tk_s2], inc=S_DVE)

        prev_evs = []
        bc_done_dst = {}
        for idx, (dst, scl) in enumerate([(sGB, 1.0), (sDB, 1.0), (sB2B, 0.5)]):
            rowmap = [0, 2, 3]
            tk_row = p.op("sync",
                          lambda e, i=rowmap[idx]: e.dma_start(out=sROWP[:], in_=dram["rows4"][i:i + 1, :]),
                          waits=prev_evs, inc=S_ROW, amt=16, dma=True)
            evs = []
            for h in range(2):
                PP = P1 if h == 0 else P2
                tm = p.op("tensor",
                          lambda e, hh=h, PPl=PP: e.matmul(
                              PPl[:, 0:512], sONESF[0:1, :],
                              sROWP[0:1, hh * 512:(hh + 1) * 512], start=True, stop=True),
                          waits=[tk_row, tk_onef] + prev_evs, inc=S_PE)
                te = p.op("scalar",
                          lambda e, d=dst, hh=h, s=scl, PPl=PP: e.activation(
                              d[:, hh * 512:(hh + 1) * 512], PPl[:, 0:512], AF.Copy, scale=s),
                          waits=[tm], inc=S_ACT)
                evs.append(te)
            prev_evs = list(evs)
            bc_done_dst[idx] = list(evs)
        bcast_done = evs[-1]
        tk_gb, tk_db, tk_b2 = (bc_done_dst[0], bc_done_dst[1], bc_done_dst[2])

        deq = dict(q_rd=[None] * 3, wb_par=[None] * 4,
                   stt_h=[None, None], pf_rd=[None] * 3, bfold_prev=None,
                   q1tok={}, q2tok={}, stt_fb={},
                   wb2_par=[None] * 4)
        wb1_done = {}
        wb2_done = {}
        bfold_toks = []
        deq_pf_free = [[], [], [], []]

        def emit_deq1_dma(fb):
            par = fb % 3
            qtoks = []
            for d2 in range(NDC):
                tok = p.op("sync",
                           lambda e, f=fb, d=d2, pr=par: e.dma_start(
                               out=sQ[:, pr * 1024 + d * 128: pr * 1024 + (d + 1) * 128],
                               in_=dram["q1blk"][f, d, :, :]),
                           waits=[deq["q_rd"][par]], inc=S_Q[par], amt=16,
                           dma=True)
                qtoks.append(tok)
            deq["q1tok"][fb] = qtoks[-1]

        def emit_deq1(fb):
            par = fb % 3
            s4 = fb % 4
            sts = []
            for h in range(2):
                PP = PF[h]
                tm = p.op("tensor",
                          lambda e, f=fb, hh=h, PPl=PP: e.matmul(
                              PPl[:, 0:512], sSEL[:, f * 128:(f + 1) * 128],
                              sSC1[:, hh * 512:(hh + 1) * 512], start=True, stop=True),
                          waits=[t_sc1b, tk_sel, deq["stt_h"][h]], inc=S_PE)
                ts_ = p.op("vector",
                           lambda e, hh=h, pr=par, w4=s4, PPl=PP: e.scalar_tensor_tensor(
                               sWO[:, w4 * 1024 + hh * 512: w4 * 1024 + (hh + 1) * 512],
                               sQ[:, pr * 1024 + hh * 512: pr * 1024 + (hh + 1) * 512],
                               128.0, PPl[:, 0:512], ALU.subtract, ALU.mult),
                           waits=[deq["q1tok"][fb], tm, deq["wb_par"][s4],
                                  G["wo_rd"][s4]],
                           inc=S_DVE)
                sts.append(ts_)
                deq["stt_h"][h] = ts_
                deq_pf_free[h] = [ts_]
            deq["q_rd"][par] = sts[-1]
            bmms = []
            for d2 in range(NDC):
                tok = p.op("tensor",
                           lambda e, d=d2, w4=s4: e.matmul(
                               PF[2][0:128, 0:1],
                               sWO[:, w4 * 1024 + d * 128: w4 * 1024 + (d + 1) * 128],
                               sBETA[:, d:d + 1], start=(d == 0), stop=(d == NDC - 1)),
                           waits=sts + [t_beta] +
                                 ([deq["bfold_prev"]] if deq["bfold_prev"] else []),
                           inc=S_PE)
                bmms.append(tok)
            bf = p.op("vector",
                      lambda e, f=fb: e.scalar_tensor_tensor(
                          sFC1B[:, f:f + 1], PF[2][0:128, 0:1], 1.0,
                          sFC1B[:, f:f + 1], ALU.mult, ALU.add),
                      waits=[bmms[-1], tok_consts], inc=S_DVE)
            deq["bfold_prev"] = bf
            bfold_toks.append(bf)
            deq["stt_fb"][fb] = list(sts)
            deq_pf_free[2] = [bf]
            tok = p.op("sync",
                       lambda e, f=fb, w4=s4: e.dma_start(
                           out=w1bf[f, :, :], in_=sWO[:, w4 * 1024:(w4 + 1) * 1024]),
                       waits=sts, inc=S_WB[s4], amt=16, dma=True)
            deq["wb_par"][s4] = tok
            wb1_done[fb] = tok

        def emit_deq2_dma(fb):
            par = fb % 3
            deq["q2tok"][fb] = p.op("sync",
                                    lambda e, f=fb, pr=par: e.dma_start(
                                        out=sQ[:, pr * 1024:(pr + 1) * 1024],
                                        in_=dram["q2blk"][f, :, :]),
                                    waits=[deq["q_rd"][par]],
                                    inc=S_Q[par], amt=16, dma=True)

        def emit_deq2(fb, p12_gate_first):
            par = fb % 3
            s4 = fb % 4
            tq = deq["q2tok"][fb]
            sts = []
            for h in range(2):
                PP = P1 if h == 0 else P2
                tm = p.op("tensor",
                          lambda e, f=fb, hh=h, PPl=PP: e.matmul(
                              PPl[:, 0:512], sSEL[:, f * 128:(f + 1) * 128],
                              sSC2[:, hh * 512:(hh + 1) * 512], start=True, stop=True),
                          waits=[t_sc2b, deq["stt_h"][h]] + p12_gate_first,
                          inc=S_PE)
                ts_ = p.op("vector",
                           lambda e, hh=h, pr=par, w4=s4, PPl=PP: e.scalar_tensor_tensor(
                               sWO[:, w4 * 1024 + hh * 512: w4 * 1024 + (hh + 1) * 512],
                               sQ[:, pr * 1024 + hh * 512: pr * 1024 + (hh + 1) * 512],
                               128.0, PPl[:, 0:512], ALU.subtract, ALU.mult),
                           waits=[tq, tm, deq["wb2_par"][s4], G["wo_rd"][s4]],
                           inc=S_DVE)
                sts.append(ts_)
                deq["stt_h"][h] = ts_
            deq["q_rd"][par] = sts[-1]
            deq["stt_fb"][(1, fb)] = list(sts)
            tok = p.op("sync",
                       lambda e, f=fb, w4=s4: e.dma_start(
                           out=w2bf[f, :, :], in_=sWO[:, w4 * 1024:(w4 + 1) * 1024]),
                       waits=sts, inc=S_WB[s4], amt=16, dma=True)
            deq["wb2_par"][s4] = tok
            wb2_done[fb] = tok

        stage_early = [t_ae, t_xpwb, t_dtwb, t_beta, t_sc2b, bcast_done]

        ROT = 4
        G = {
            "w_free": [None] * 4,
            "pf_free": [[], [], [], []],
            "da_free": [None] * ROT, "dbu_free": [None, None],
            "h_free": [None, None], "tmp_free": [None] * ROT,
            "bcb_rd": [None, None], "bcc_free": [None, None],
            "pb_free": None, "pc_free": None,
            "t3_free": [None] * 4, "gh_free": [None, None],
            "out_gate": None, "xc_free": [None, None], "cw_free": None,
            "tail_gate": None, "p12_mm_gate": None, "wo_rd": [None] * 4,
            "ylast_pe": None,
            "y_final": {dc: list(stage_early) for dc in range(NDC)},
            "hid_pe": None, "h2l_pe": None, "xnbl_pe": None, "up_dve": None,
            "stat_dve": None, "h2t_pe": [None, None],
            "xt_act": None, "xt_dve": None, "xnt_act": None,
            "p12_act": list(bc_done_dst[2]), "p12_pe": None,
            "musave_rd": {0: None, 1: None},
            "st_tok": {}, "syt_last": None,
        }

        def emit_xload(ti):
            for tc in range(NTC):
                p.op("sync",
                     lambda e, t_=tc, r=ti * T: e.dma_start(
                         out=sXT[:, t_ * D:(t_ + 1) * D],
                         in_=dram["xT"][r + t_ * 128: r + (t_ + 1) * 128, :]),
                     waits=[G["xt_act"], G["xt_dve"]], inc=S_X, amt=16, dma=True)
            return (S_X, S_X.n)

        class Prolog:
            pass

        def emit_prolog_piece(PR, ti, k, tok_x, tcs=None):
            if tcs is None:
                tcs = range(NTC)
            if k == 0:
                for tc in tcs:
                    xs = sXT[:, tc * D:(tc + 1) * D]
                    c0 = tc * 8
                    t_sx = p.op("scalar",
                                lambda e, a=xs, c=c0: e.activation(
                                    sOUT[:, 0:D], a, AF.Identity,
                                    accum_out=sSTAT[:, c:c + 1]),
                                waits=[tok_x, G["stat_dve"], G["out_gate"]],
                                inc=S_ACT)
                    t_sq = p.op("scalar",
                                lambda e, a=xs, c=c0: e.activation(
                                    sOUT[:, 0:D], a, AF.Square,
                                    accum_out=sSTAT[:, c + 1:c + 2]),
                                waits=[tok_x], inc=S_ACT)
                    PR.sums1[tc] = (t_sx, t_sq)
            elif k == 1:
                rss = []
                for tc in range(NTC):
                    c0 = tc * 8
                    t_sx, t_sq = PR.sums1[tc]
                    t_mu = p.op("vector",
                                lambda e, c=c0: e.tensor_scalar(
                                    sSTAT[:, c + 2:c + 3], sSTAT[:, c:c + 1], 1.0 / D,
                                    None, ALU.mult),
                                waits=[t_sx], inc=S_DVE)
                    t_m2 = p.op("vector",
                                lambda e, c=c0: e.tensor_tensor(
                                    sSTAT[:, c + 3:c + 4], sSTAT[:, c + 2:c + 3],
                                    sSTAT[:, c + 2:c + 3], ALU.mult),
                                waits=[t_mu], inc=S_DVE)
                    t_var = p.op("vector",
                                 lambda e, c=c0: e.scalar_tensor_tensor(
                                     sSTAT[:, c + 4:c + 5], sSTAT[:, c + 1:c + 2], 1.0 / D,
                                     sSTAT[:, c + 3:c + 4], ALU.mult, ALU.subtract),
                                 waits=[t_sq, t_m2], inc=S_DVE)
                    PR.var1[tc] = t_var
                for tc in range(NTC):
                    c0 = tc * 8
                    t_ln = p.op("scalar",
                                lambda e, c=c0: e.activation(
                                    sSTAT[:, c + 5:c + 6], sSTAT[:, c + 4:c + 5],
                                    AF.Ln, bias=sCV[:, 0:1]),
                                waits=[PR.var1[tc], tk_cv], inc=S_ACT)
                    t_rs = p.op("scalar",
                                lambda e, c=c0: e.activation(
                                    sSTAT[:, c + 6:c + 7], sSTAT[:, c + 5:c + 6],
                                    AF.Exp, scale=-0.5),
                                waits=[t_ln], inc=S_ACT)
                    PR.rs1[tc] = t_rs
            elif k == 2:
                for tc in tcs:
                    c0 = tc * 8
                    c1 = 32 + tc * 8
                    t_xn = p.op("vector",
                                lambda e, t_=tc, c=c0: e.tensor_scalar(
                                    sXNT[:, t_ * D:(t_ + 1) * D], sXT[:, t_ * D:(t_ + 1) * D],
                                    sSTAT[:, c + 2:c + 3], sSTAT[:, c + 6:c + 7],
                                    ALU.subtract, ALU.mult),
                                waits=[PR.rs1[tc], G["xnt_act"]], inc=S_DVE)
                    PR.xn[tc] = t_xn
                    PR.musave[tc] = p.op("vector",
                                         lambda e, t_=tc, c=c0, pp=ti % 2: e.tensor_copy(
                                             _stat_save_dst(sSTAT, t_, pp),
                                             _stat_save_src(sSTAT, c)),
                                         waits=[PR.rs1[tc], G["musave_rd"][ti % 2]],
                                         inc=S_DVE)
                for tc in tcs:
                    c1 = 32 + tc * 8
                    xns = sXNT[:, tc * D:(tc + 1) * D]
                    t_sx2 = p.op("scalar",
                                 lambda e, a=xns, c=c1: e.activation(
                                     sOUT[:, 0:D], a, AF.Identity,
                                     accum_out=sSTAT[:, c:c + 1]),
                                 waits=[PR.xn[tc]], inc=S_ACT)
                    t_sq2 = p.op("scalar",
                                 lambda e, a=xns, c=c1: e.activation(
                                     sOUT[:, 0:D], a, AF.Square,
                                     accum_out=sSTAT[:, c + 1:c + 2]),
                                 waits=[PR.xn[tc]], inc=S_ACT)
                    PR.sums2[tc] = (t_sx2, t_sq2)
                if NTC - 1 not in tcs:
                    return
                PR.xt_act = PR.sums2[NTC - 1][1]
                PR.xt_dve = PR.xn[NTC - 1]
                PR.xnt_act = PR.sums2[NTC - 1][1]
            elif k == 3:
                for tc in range(NTC):
                    c1 = 32 + tc * 8
                    t_sx2, t_sq2 = PR.sums2[tc]
                    t_mu2 = p.op("vector",
                                 lambda e, c=c1: e.tensor_scalar(
                                     sSTAT[:, c + 2:c + 3], sSTAT[:, c:c + 1], 1.0 / D,
                                     None, ALU.mult),
                                 waits=[t_sx2], inc=S_DVE)
                    t_m22 = p.op("vector",
                                 lambda e, c=c1: e.tensor_tensor(
                                     sSTAT[:, c + 3:c + 4], sSTAT[:, c + 2:c + 3],
                                     sSTAT[:, c + 2:c + 3], ALU.mult),
                                 waits=[t_mu2], inc=S_DVE)
                    t_var2 = p.op("vector",
                                  lambda e, c=c1: e.scalar_tensor_tensor(
                                      sSTAT[:, c + 4:c + 5], sSTAT[:, c + 1:c + 2], 1.0 / D,
                                      sSTAT[:, c + 3:c + 4], ALU.mult, ALU.subtract),
                                  waits=[t_sq2, t_m22], inc=S_DVE)
                    PR.var2[tc] = t_var2
                for tc in range(NTC):
                    c1 = 32 + tc * 8
                    t_ln2 = p.op("scalar",
                                 lambda e, c=c1: e.activation(
                                     sSTAT[:, c + 5:c + 6], sSTAT[:, c + 4:c + 5],
                                     AF.Ln, bias=sCV[:, 0:1]),
                                 waits=[PR.var2[tc]], inc=S_ACT)
                    t_rs2 = p.op("scalar",
                                 lambda e, c=c1: e.activation(
                                     sSTAT[:, c + 6:c + 7], sSTAT[:, c + 5:c + 6],
                                     AF.Exp, scale=-0.5),
                                 waits=[t_ln2], inc=S_ACT)
                    PR.rs2[tc] = t_rs2
            elif k == 4 or k == 5:
                for tc in [2 * (k - 4), 2 * (k - 4) + 1]:
                    c1 = 32 + tc * 8
                    h2s = (tc % 2) * D
                    xns = sXNT[:, tc * D:(tc + 1) * D]
                    t_t2 = p.op("vector",
                                lambda e, a=xns, c=c1, s=h2s: e.tensor_scalar(
                                    sH2T[:, s:s + D], a, sSTAT[:, c + 2:c + 3],
                                    sSTAT[:, c + 6:c + 7], ALU.subtract, ALU.mult),
                                waits=[PR.rs2[tc], G["h2t_pe"][tc % 2],
                                       PR.h2l_ev[-1] if PR.h2l_ev else None], inc=S_DVE)
                    t_h2 = t_t2
                    if tc == NTC - 1:
                        PR.stat_dve = t_t2
                    for half in range(2):
                        trx, trh = [], []
                        for j in range(4):
                            dc = half * 4 + j
                            tok = p.op("tensor",
                                       lambda e, d=dc, t_=tc, jj=j: e.transpose(
                                           P1[:, jj * 128:(jj + 1) * 128],
                                           sXNT[:, t_ * D + d * 128: t_ * D + (d + 1) * 128],
                                           sIDF[:]),
                                       waits=[PR.xn[tc],
                                              PR.xnl_ev[-1] if PR.xnl_ev else None] +
                                             (G["p12_act"] if not PR.xnl_ev else []),
                                       inc=S_PE)
                            trx.append(tok)
                        te = p.op("scalar",
                                  lambda e, hh=half, t_=tc: e.activation(
                                      lane_ap(sXNBL, hh, t_), P1[:, 0:512], AF.Identity),
                                  waits=trx + [G["xnbl_pe"], G["up_dve"]], inc=S_ACT)
                        PR.xnl_ev.append(te)
                        for j in range(4):
                            dc = half * 4 + j
                            tok = p.op("tensor",
                                       lambda e, d=dc, jj=j, s=h2s: e.matmul(
                                           P2[:, jj * 128:(jj + 1) * 128],
                                           sH2T[:, s + d * 128: s + (d + 1) * 128],
                                           sID2[:], start=True, stop=True),
                                       waits=[t_h2,
                                              PR.h2l_ev[-1] if PR.h2l_ev else None] +
                                             (G["p12_act"] if not PR.h2l_ev else []),
                                       inc=S_PE)
                            trh.append(tok)
                        for j in range(4):
                            dc = half * 4 + j
                            te2 = p.op("scalar",
                                       lambda e, d=dc, jj=j, t_=tc: e.activation(
                                           sH2L[:, d * T + t_ * 128: d * T + (t_ + 1) * 128],
                                           P2[:, jj * 128:(jj + 1) * 128],
                                           AF.Identity, scale=sGL[:, d:d + 1]),
                                       waits=[trh[j], tok_consts, G["h2l_pe"]],
                                       inc=S_ACT)
                        PR.h2l_ev.append(te2)
                    G["h2t_pe"][tc % 2] = trh[-1]
            elif k == 6:
                drk_mm, bcr_mm = [], []
                for dc in range(NDC):
                    tok = p.op("tensor",
                               lambda e, d=dc: e.matmul(
                                   P1[0:64, 0:512], sXPW[:, d * 96: d * 96 + 64],
                                   sXNBL[:, d * T:(d + 1) * T],
                                   start=(d == 0), stop=(d == NDC - 1)),
                               waits=[PR.xnl_ev[-1], t_xpwb], inc=S_PE)
                    drk_mm.append(tok)
                for dc in range(NDC):
                    tok = p.op("tensor",
                               lambda e, d=dc: e.matmul(
                                   P2[0:32, 0:512], sXPW[:, d * 96 + 64: d * 96 + 96],
                                   sXNBL[:, d * T:(d + 1) * T],
                                   start=(d == 0), stop=(d == NDC - 1)),
                               waits=[PR.xnl_ev[-1], PR.h2l_ev[-1], t_xpwb], inc=S_PE)
                    bcr_mm.append(tok)
                PR.xnbl_pe = bcr_mm[-1]
                PR.t_drk = p.op("vector", lambda e: e.tensor_copy(sDRK[:], P1[0:64, 0:512]),
                                waits=[drk_mm[-1]], inc=S_DVE)
                PR.t_bcr = p.op("vector", lambda e: e.tensor_copy(sBCr[:], P2[0:32, 0:512]),
                                waits=[bcr_mm[-1]], inc=S_DVE)
            elif k == 7:
                sp_prev = [None, None]
                for dc in range(NDC):
                    par = dc % 2
                    PP = P1 if par == 0 else P2
                    t_mm = p.op("tensor",
                                lambda e, d=dc, PPl=PP: e.matmul(
                                    PPl[:, 0:512], sDTW[:, d * 128:(d + 1) * 128], sDRK[:],
                                    start=True, stop=True),
                                waits=[PR.t_drk, PR.t_bcr, t_dtwb, sp_prev[par],
                                       G["p12_mm_gate"]],
                                inc=S_PE)
                    t_e = p.op("scalar",
                               lambda e, d=dc, PPl=PP, pr=par: e.activation(
                                   sOUT[:, pr * 512:(pr + 1) * 512], PPl[:, 0:512],
                                   AF.Exp, bias=sDTB[:, d:d + 1]),
                               waits=[t_mm] + ([G["out_gate"]] if dc < 2 else []),
                               inc=S_ACT)
                    t_d = p.op("scalar",
                               lambda e, d=dc, pr=par: e.activation(
                                   sDLT[:, d * T:(d + 1) * T],
                                   sOUT[:, pr * 512:(pr + 1) * 512],
                                   AF.Ln, bias=sCV[:, 1:2]),
                               waits=[t_e, tk_cv, G["up_dve"]], inc=S_ACT)
                    sp_prev[par] = t_e
                    PR.dlt[dc] = t_d
                    PR.up[dc] = p.op("vector",
                                     lambda e, d=dc: e.tensor_tensor(
                                         sUP[:, d * T:(d + 1) * T],
                                         sDLT[:, d * T:(d + 1) * T],
                                         sXNBL[:, d * T:(d + 1) * T], ALU.mult),
                                     waits=[t_d, PR.xnl_ev[-1]], inc=S_DVE)
                PR.up_dve = PR.up[NDC - 1]
                PR.delta_mm_last = t_mm

        def new_prolog():
            PR = Prolog()
            PR.xn = {}
            PR.musave = {}
            PR.sums1 = {}
            PR.var1 = {}
            PR.rs1 = {}
            PR.sums2 = {}
            PR.var2 = {}
            PR.rs2 = {}
            PR.xnl_ev = []
            PR.h2l_ev = []
            PR.dlt = {}
            PR.up = {}
            return PR

        def adopt(PR):
            for k in ["xt_act", "xt_dve", "xnt_act", "stat_dve",
                      "xnbl_pe", "up_dve"]:
                G[k] = getattr(PR, k)

        tok_x0 = emit_xload(0)
        PRO = new_prolog()
        for k in range(8):
            emit_prolog_piece(PRO, 0, k, tok_x0)
        adopt(PRO)
        emit_deq1_dma(0)
        emit_deq1_dma(1)
        emit_deq1_dma(2)

        for ti in range(nt):
            r0 = ti * T
            init_zero = (r0 % SEQ) == 0
            dlt_toks = PRO.dlt
            up_toks = PRO.up
            h2l_last = PRO.h2l_ev[-1]
            t_bcr = PRO.t_bcr

            tok_xn_next = emit_xload(ti + 1) if ti + 1 < nt else None

            y_ms = {}
            for dc in range(NDC):
                y_ms[dc] = p.op("gpsimd",
                                lambda e, d=dc: e.memset(sY[:, d * T:(d + 1) * T], 0.0),
                                waits=[G["ylast_pe"], G["y_final"][dc]], inc=S_PL)

            y_last = dict(y_ms)
            scan_idx = 0
            hid_toks = {}
            ev_par = [None, None]
            fc1mm_par = [None, None]
            state_rd = None
            PRN = new_prolog() if ti + 1 < nt else None

            def emit_fc1(fb):
                par = fb % 2
                PP = P1 if par == 0 else P2
                if ti == 0:
                    wsrc = sWO
                    wbase = (fb % 4) * 1024
                    wgate = [deq["stt_fb"][fb]]
                else:
                    wsrc = sW
                    wbase = par * 1024
                    wgate = [p.op("sync",
                                  lambda e, f=fb, pr=par: e.dma_start(
                                      out=sW[:, pr * 1024:(pr + 1) * 1024],
                                      in_=w1bf[f, :, :]),
                                  waits=[fc1mm_par[par] if fc1mm_par[par]
                                         else G["w_free"][par], wb1_done[fb]],
                                  inc=S_W1[par], amt=16, dma=True)]
                mms = []
                for dc in range(NDC):
                    tok = p.op("tensor",
                               lambda e, d=dc, wb=wbase, W=wsrc, PPl=PP: e.matmul(
                                   PPl[:, 0:512],
                                   W[:, wb + d * 128: wb + (d + 1) * 128],
                                   sH2L[:, d * T:(d + 1) * T],
                                   start=(d == 0), stop=(d == NDC - 1)),
                               waits=wgate + [h2l_last, ev_par[par],
                                              dlt_toks[NDC - 1]], inc=S_PE)
                    mms.append(tok)
                fc1mm_par[par] = mms[-1]
                if ti == 0:
                    G["wo_rd"][fb % 4] = mms[-1]
                t_si = p.op("scalar",
                            lambda e, f=fb, PPl=PP: e.activation(
                                sHID[:, f * T:(f + 1) * T], PPl[:, 0:512],
                                AF.Identity, bias=sFC1B[:, f:f + 1]),
                            waits=[mms[-1], tk_f1b, G["hid_pe"], bfold_toks[fb]],
                            inc=S_ACT)
                ev_par[par] = t_si
                hid_toks[fb] = t_si

            bc_ev = {}
            pre_da = {}

            def emit_bc_pe(n):
                t_bcB = p.op("tensor",
                             lambda e, nn=n: e.matmul(
                                 PB[:, 0:512], sSEL[:, nn * 128:(nn + 1) * 128],
                                 sBCr[:], start=True, stop=True),
                             waits=[t_bcr, G["pb_free"]], inc=S_PE)
                t_bcC = p.op("tensor",
                             lambda e, nn=n: e.matmul(
                                 PC[:, 0:512], sSEL[:, (nn + 16) * 128:(nn + 17) * 128],
                                 sBCr[:], start=True, stop=True),
                             waits=[t_bcr, G["pc_free"]], inc=S_PE)
                return (t_bcB, t_bcC)

            def emit_bc_ev(n, pe_toks):
                par = n % 2
                t_evB = p.op("scalar",
                             lambda e, pr=par: e.activation(
                                 sBCB[:, pr * T:(pr + 1) * T], PB[:, 0:512], AF.Identity),
                             waits=[pe_toks[0], G["bcb_rd"][par]], inc=S_ACT)
                G["pb_free"] = t_evB
                t_evC = p.op("scalar",
                             lambda e, pr=par: e.activation(
                                 sBCC[:, pr * T:(pr + 1) * T], PC[:, 0:512], AF.Identity),
                             waits=[pe_toks[1], G["bcc_free"][par]], inc=S_ACT)
                G["pc_free"] = t_evC
                bc_ev[n] = (t_evB, t_evC)

            def scan_group(n):
                nonlocal scan_idx, state_rd
                parB = n % 2
                parC = n % 2
                t_evB, t_evC = bc_ev[n]
                pe_next = emit_bc_pe(n + 1) if n + 1 < NST else None
                last_db = None

                def rep2(tile, par):
                    ap = tile[:]
                    return AP(ap.tensor, ap.offset + par * T,
                              [[ap.ap[0][0], 128], [0, 2], [1, T]])

                for j in range(NDC // 2):
                    dc0, dc1 = 2 * j, 2 * j + 1
                    pair0 = dc0 * NST + n
                    pair1 = dc1 * NST + n
                    parA0 = scan_idx % ROT
                    parA1 = (scan_idx + 1) % ROT
                    pj = scan_idx // 2
                    parU = pj % 2
                    parM = pj % 2
                    if n == 8 and dc0 in pre_da:
                        t_da0, ap0, slot0 = pre_da[dc0]
                    else:
                        ap0 = sDA[:, parA0 * T:(parA0 + 1) * T]
                        slot0 = parA0
                        t_da0 = p.op("scalar",
                                     lambda e, d=dc0, nn=n, a=ap0: e.activation(
                                         a, sDLT[:, d * T:(d + 1) * T],
                                         AF.Exp, scale=sA[:, d * 16 + nn: d * 16 + nn + 1]),
                                     waits=[dlt_toks[dc0], t_a, G["da_free"][parA0]],
                                     inc=S_ACT)
                    if n == 8 and dc1 in pre_da:
                        t_da1, ap1, slot1 = pre_da[dc1]
                    else:
                        ap1 = sDA[:, parA1 * T:(parA1 + 1) * T]
                        slot1 = parA1
                        t_da1 = p.op("scalar",
                                     lambda e, d=dc1, nn=n, a=ap1: e.activation(
                                         a, sDLT[:, d * T:(d + 1) * T],
                                         AF.Exp, scale=sA[:, d * 16 + nn: d * 16 + nn + 1]),
                                     waits=[dlt_toks[dc1], t_a, G["da_free"][parA1]],
                                     inc=S_ACT)
                    t_db = p.op("vector",
                                lambda e, d=dc0, pu=parU, prB=parB: e.tensor_tensor(
                                    sDBU[:, pu * 2 * T:(pu + 1) * 2 * T],
                                    sUP[:, d * T:(d + 2) * T],
                                    rep2(sBCB, prB), ALU.mult),
                                waits=[up_toks[dc0], up_toks[dc1], t_evB,
                                       G["dbu_free"][parU]], inc=S_DVE)
                    last_db = t_db
                    init0 = 0.0 if init_zero else sSTATE[:, pair0:pair0 + 1]
                    init1 = 0.0 if init_zero else sSTATE[:, pair1:pair1 + 1]
                    t_sc0 = p.op("vector",
                                 lambda e, a=ap0, pu=parU, ii=init0: e.tensor_tensor_scan(
                                     sH[:, 0:T], a,
                                     sDBU[:, pu * 2 * T:pu * 2 * T + T],
                                     ii, ALU.mult, ALU.add),
                                 waits=[t_da0, t_db, G["h_free"][0]] +
                                       ([] if init_zero else [G["st_tok"].get(pair0)]),
                                 inc=S_DVE)
                    t_sc1 = p.op("vector",
                                 lambda e, a=ap1, pu=parU, ii=init1: e.tensor_tensor_scan(
                                     sH[:, T:2 * T], a,
                                     sDBU[:, pu * 2 * T + T:(pu + 1) * 2 * T],
                                     ii, ALU.mult, ALU.add),
                                 waits=[t_da1, G["h_free"][1]] +
                                       ([] if init_zero else [G["st_tok"].get(pair1)]),
                                 inc=S_DVE)
                    if slot0 is not None:
                        G["da_free"][slot0] = t_sc0
                    else:
                        G["p12_act"] = G["p12_act"] + [t_sc0]
                    if slot1 is not None:
                        G["da_free"][slot1] = t_sc1
                    else:
                        G["p12_act"] = G["p12_act"] + [t_sc1]
                    G["dbu_free"][parU] = t_sc1
                    t_st = p.op("vector",
                                lambda e, pp=pair1: e.tensor_copy(
                                    _state_pair_dst(sSTATE, pp),
                                    _state_pair_src(sH, 1)),
                                waits=[t_sc1], inc=S_DVE)
                    state_rd = t_st
                    G["st_tok"][pair0] = t_st
                    G["st_tok"][pair1] = t_st
                    t_tm = p.op("vector",
                                lambda e, pm=parM, prC=parC: e.tensor_tensor(
                                    sTMP[:, pm * 2 * T:(pm + 1) * 2 * T],
                                    sH[:, 0:2 * T],
                                    rep2(sBCC, prC), ALU.mult),
                                waits=[t_sc1, t_evC, G["tmp_free"][parM]],
                                inc=S_DVE)
                    G["h_free"][0] = t_tm
                    G["h_free"][1] = t_tm
                    t_y = p.op("gpsimd",
                               lambda e, d=dc0, pm=parM: e.dma_start(
                                   out=sY[:, d * T:(d + 2) * T],
                                   in_=sTMP[:, pm * 2 * T:(pm + 1) * 2 * T],
                                   accum_op=ALU.add),
                               waits=[t_tm, y_last[dc0], y_last[dc1]],
                               inc=S_YA[dc0], amt=16, dma=True)
                    G["tmp_free"][parM] = t_y
                    y_last[dc0] = t_y
                    y_last[dc1] = t_y
                    scan_idx += 2
                G["bcb_rd"][parB] = last_db
                G["bcc_free"][parC] = y_last[NDC - 1]
                if pe_next is not None:
                    emit_bc_ev(n + 1, pe_next)

            emit_bc_ev(0, emit_bc_pe(0))
            for n in range(8):
                scan_group(n)
                for f in range(4 * n, 4 * n + 4):
                    if ti == 0:
                        emit_deq1(f)
                    emit_fc1(f)
                    if ti == 0 and f + 3 < NFB:
                        emit_deq1_dma(f + 3)
            if ti == 0:
                G["pf_free"] = [list(t) for t in deq_pf_free]
                deq["wb2_par"] = list(deq["wb_par"])
                emit_deq2_dma(0)
                emit_deq2_dma(1)
                emit_deq2_dma(2)

            for dc in range(6 if ti != 0 else 4):
                if ti != 0 and dc < 2:
                    dap = (P1 if dc == 0 else P2)[:, 0:512]
                    w = [dlt_toks[dc], t_a, ev_par[dc]]
                    sl = None
                else:
                    sl = dc % 4
                    dap = sDA[:, sl * T:(sl + 1) * T]
                    w = [dlt_toks[dc], t_a, G["da_free"][sl]]
                tok = p.op("scalar",
                           lambda e, d=dc, a=dap: e.activation(
                               a, sDLT[:, d * T:(d + 1) * T],
                               AF.Exp, scale=sA[:, d * 16 + 8: d * 16 + 9]),
                           waits=w, inc=S_ACT)
                pre_da[dc] = (tok, dap, sl)

            def emit_silus(j):
                for fb in range(8 * j, 8 * j + 8):
                    hid_toks[fb] = p.op("scalar",
                                        lambda e, f=fb: e.activation(
                                            sHID[:, f * T:(f + 1) * T],
                                            sHID[:, f * T:(f + 1) * T], AF.Silu),
                                        waits=[hid_toks[fb]], inc=S_ACT)

            G["p12_act"] = [ev_par[0], ev_par[1]]

            f2mm_rot = [None] * 4
            f2mm = {0: {}, 1: {}}
            f2_last_mm = None
            t3a_early = {}

            def emit_fc2_group(tcp, fb):
                nonlocal f2_last_mm
                rot = fb % 4
                if tcp == 0 and ti == 0:
                    wsrc = sWO
                    wbase = (fb % 4) * 1024
                    wgate = [deq["stt_fb"][(1, fb)]]
                else:
                    if f2mm_rot[rot] is not None:
                        gate = f2mm_rot[rot]
                    elif rot < 2:
                        gate = fc1mm_par[rot]
                    else:
                        gate = G["w_free"][rot]
                    wsrc = sW
                    wbase = rot * 1024
                    wgate = [p.op("sync",
                                  lambda e, f=fb, pr=rot: e.dma_start(
                                      out=sW[:, pr * 1024:(pr + 1) * 1024],
                                      in_=w2bf[f, :, :]),
                                  waits=[gate, wb2_done[fb]], inc=S_W2[rot],
                                  amt=16, dma=True)]
                for i2 in range(2):
                    tc = tcp * 2 + i2
                    for db in range(2):
                        pi = i2 * 2 + db
                        if fb == 0:
                            pfg = G["pf_free"][pi] if tcp == 0 else [t3a_early[pi]]
                        else:
                            pfg = []
                        tok = p.op("tensor",
                                   lambda e, f=fb, t_=tc, d2=db, p_=pi, W=wsrc,
                                   wb=wbase: e.matmul(
                                       PF[p_][:, 0:512],
                                       sHID[:, f * T + t_ * 128: f * T + (t_ + 1) * 128],
                                       W[:, wb + d2 * 512: wb + (d2 + 1) * 512],
                                       start=(f == 0), stop=(f == NFB - 1)),
                                   waits=wgate + [hid_toks[fb]] + pfg, inc=S_PE)
                        f2mm[tcp][pi] = tok
                        f2_last_mm = tok
                if tcp == 0 and ti == 0:
                    G["wo_rd"][fb % 4] = tok
                else:
                    f2mm_rot[rot] = tok

            for j in range(4):
                emit_silus(j)
            for n in range(8, 12):
                scan_group(n)
                for fb in range(8 * (n - 8), 8 * (n - 8) + 8):
                    if ti == 0:
                        emit_deq2(fb, G["p12_act"] if fb < 2 else [])
                    emit_fc2_group(0, fb)
                    if ti == 0 and fb + 3 < NFB:
                        emit_deq2_dma(fb + 3)
                if PRN is not None:
                    j = n - 8
                    if j == 0:
                        emit_prolog_piece(PRN, ti + 1, 0, tok_xn_next, [0, 1])
                    elif j == 1:
                        emit_prolog_piece(PRN, ti + 1, 0, tok_xn_next, [2, 3])
                        emit_prolog_piece(PRN, ti + 1, 1, tok_xn_next)
                    elif j == 2:
                        emit_prolog_piece(PRN, ti + 1, 2, tok_xn_next, [0, 1])
                    else:
                        emit_prolog_piece(PRN, ti + 1, 2, tok_xn_next, [2, 3])
                        emit_prolog_piece(PRN, ti + 1, 3, tok_xn_next)
            if ti == 0:
                late_free = [deq["stt_h"][0], deq["stt_h"][1],
                             wb2_done[NFB - 4], wb2_done[NFB - 3],
                             wb2_done[NFB - 2], wb2_done[NFB - 1]] + \
                    list(G["wo_rd"])
                G["t3_free"] = [list(late_free) for _ in range(4)]
                G["gh_free"] = [list(late_free), list(late_free)]
                G["cw_free"] = list(late_free)
                G["tail_gate"] = list(late_free)
                G["p12_mm_gate"] = list(late_free)
                G["p12_act"] = G["p12_act"] + [deq["stt_h"][0], deq["stt_h"][1]]

            for pi in range(4):
                db = pi % 2
                t3a_early[pi] = p.op("vector",
                                     lambda e, p_=pi, d2=db: e.scalar_tensor_tensor(
                                         sT3[:, p_ * T:(p_ + 1) * T], PF[p_][:, 0:512], 0.5,
                                         sB2B[:, d2 * 512:(d2 + 1) * 512],
                                         ALU.mult, ALU.add),
                                     waits=[f2mm[0][pi], G["t3_free"][pi], tk_b2],
                                     inc=S_DVE)

            for n in range(12, NST):
                scan_group(n)
                for fb in range(8 * (n - 12), 8 * (n - 12) + 8):
                    emit_fc2_group(1, fb)
                if PRN is not None and n - 8 < 6:
                    emit_prolog_piece(PRN, ti + 1, n - 8, tok_xn_next)
            if PRN is not None:
                emit_prolog_piece(PRN, ti + 1, 6, tok_xn_next)
                emit_prolog_piece(PRN, ti + 1, 7, tok_xn_next)

            comb_prev = None
            trt_last = None
            out_prev = G["out_gate"]
            syt_free = G["syt_last"]
            yt_free = [None, None]
            t3slot_g = [None] * 4
            if PRN is not None:
                p12_gate = [PRN.dlt[NDC - 2], PRN.dlt[NDC - 1]]
                out_wr_gate = [PRN.dlt[NDC - 2], PRN.dlt[NDC - 1]]
            else:
                p12_gate = [ev_par[0], ev_par[1]]
                out_wr_gate = [None, None]

            def emit_xc(tc, gate):
                return p.op("sync",
                            lambda e, t_=tc, r=r0: e.dma_start(
                                out=sXC[:, (t_ % 2) * D:(t_ % 2 + 1) * D],
                                in_=dram["xT"][r + t_ * 128: r + (t_ + 1) * 128, :]),
                            waits=[gate, G["tail_gate"]], inc=S_XC[tc % 2],
                            amt=16, dma=True)

            xc_toks = {0: emit_xc(0, G["xc_free"][0]),
                       1: emit_xc(1, G["xc_free"][1])}
            for tcp in range(2):
                if tcp == 1:
                    for pi in range(4):
                        db = pi % 2
                        t3a_early[pi] = p.op("vector",
                                             lambda e, p_=pi, d2=db: e.scalar_tensor_tensor(
                                                 sT3[:, p_ * T:(p_ + 1) * T],
                                                 PF[p_][:, 0:512], 0.5,
                                                 sB2B[:, d2 * 512:(d2 + 1) * 512],
                                                 ALU.mult, ALU.add),
                                             waits=[f2mm[1][pi], t3slot_g[pi], tk_b2],
                                             inc=S_DVE)
                        G["pf_free"][pi] = [t3a_early[pi]]
                for i2 in range(2):
                    tc = tcp * 2 + i2
                    xb = (tc % 2) * D
                    tok_xc = xc_toks[tc]
                    this_out_gate = out_prev
                    halves = []
                    for db in range(2):
                        pi = i2 * 2 + db
                        PY = P1 if db == 0 else P2
                        t_xn2 = p.op("vector",
                                     lambda e, t_=tc, d2=db, x0=xb, pp=ti % 2: e.tensor_scalar(
                                         sCW[:], sXC[:, x0 + d2 * 512:x0 + (d2 + 1) * 512],
                                         sSTAT[:, 64 + 8 * pp + 2 * t_: 65 + 8 * pp + 2 * t_],
                                         sSTAT[:, 65 + 8 * pp + 2 * t_: 66 + 8 * pp + 2 * t_],
                                         ALU.subtract, ALU.mult),
                                     waits=[tok_xc, PRO.musave[tc], G["cw_free"]], inc=S_DVE)
                        t_t3 = p.op("vector",
                                    lambda e, p_=pi: e.tensor_tensor(
                                        sT3[:, p_ * T:(p_ + 1) * T],
                                        sT3[:, p_ * T:(p_ + 1) * T], sCW[:], ALU.add),
                                    waits=[t3a_early[pi], t_xn2], inc=S_DVE)
                        t_g = p.op("scalar",
                                   lambda e, pr=db, p_=pi: e.activation(
                                       sGH[:, pr * T:(pr + 1) * T],
                                       sT3[:, p_ * T:(p_ + 1) * T], AF.Silu),
                                   waits=[t_t3, G["gh_free"][db]], inc=S_ACT)
                        t3slot_g[pi] = t_g
                        G["t3_free"][pi] = t_g
                        trt = []
                        for j in range(4):
                            dc = db * 4 + j
                            tok = p.op("tensor",
                                       lambda e, d=dc, t_=tc, jj=j, PYl=PY: e.transpose(
                                           PYl[:, jj * 128:(jj + 1) * 128],
                                           sY[:, d * T + t_ * 128: d * T + (t_ + 1) * 128],
                                           sIDF[:]),
                                       waits=[y_last[dc], yt_free[db],
                                              p12_gate[db]], inc=S_PE)
                            trt.append(tok)
                        trt_last = trt[-1]
                        t_yt = p.op("vector",
                                    lambda e, PYl=PY: e.tensor_copy(sYT[:], PYl[:, 0:512]),
                                    waits=trt + [G["tail_gate"], syt_free],
                                    inc=S_DVE)
                        yt_free[db] = t_yt
                        t_cw = p.op("vector",
                                    lambda e, d2=db: e.tensor_tensor(
                                        sCW[:], sCW[:],
                                        sDB[:, d2 * 512:(d2 + 1) * 512], ALU.mult),
                                    waits=[t_t3, t_yt, tk_db], inc=S_DVE)
                        t_s = p.op("vector",
                                   lambda e: e.tensor_tensor(sYT[:], sYT[:], sCW[:], ALU.add),
                                   waits=[t_cw], inc=S_DVE)
                        G["cw_free"] = t_s
                        t_m = p.op("vector",
                                   lambda e, pr=db: e.tensor_tensor(
                                       sYT[:], sYT[:], sGH[:, pr * T:(pr + 1) * T], ALU.mult),
                                   waits=[t_s, t_g], inc=S_DVE)
                        G["gh_free"][db] = t_m
                        t_o = p.op("vector",
                                   lambda e, d2=db, x0=xb: e.tensor_tensor(
                                       sOUT[:, d2 * 512:(d2 + 1) * 512], sYT[:],
                                       sXC[:, x0 + d2 * 512:x0 + (d2 + 1) * 512], ALU.add),
                                   waits=[t_m, tok_xc, this_out_gate,
                                          out_wr_gate[db]], inc=S_DVE)
                        comb_prev = t_o
                        syt_free = t_o
                        halves.append(t_o)
                    if tc + 2 < NTC:
                        xc_toks[tc + 2] = emit_xc(tc + 2, halves[-1])
                    G["xc_free"][tc % 2] = halves[-1]
                    t_store = p.op("sync",
                                   lambda e, t_=tc, r=r0: e.dma_start(
                                       out=outT[r + t_ * 128: r + (t_ + 1) * 128, :],
                                       in_=sOUT[:]),
                                   waits=halves, inc=S_OUT, amt=16, dma=True)
                    out_prev = t_store
            G["out_gate"] = out_prev
            G["syt_last"] = syt_free
            G["musave_rd"][ti % 2] = comb_prev
            G["ylast_pe"] = trt_last
            G["y_final"] = dict(y_last)
            G["hid_pe"] = f2_last_mm
            G["h2l_pe"] = fc1mm_par[1]
            G["w_free"] = list(f2mm_rot)
            if PRN is not None:
                adopt(PRN)
                PRO = PRN

        p.op("sync", lambda e: e.nop(), waits=[(S_OUT, S_OUT.n)])

        with nc.Block() as block:
            p.emit(block)

    return nc


def _host_stage(L0_attn_x, L0_attn_A_log, L0_attn_D, L0_attn_x_proj_w,
                L0_attn_dt_proj_w, L0_attn_dt_proj_b, L0_ffn_ln_gamma,
                L0_ffn_ln_beta, L0_ffn_fc1_q, L0_ffn_fc1_scales, L0_ffn_fc1_bias,
                L0_ffn_fc2_q, L0_ffn_fc2_scales, L0_ffn_fc2_bias):
    x = np.ascontiguousarray(np.asarray(L0_attn_x, dtype=np.float32))
    assert x.shape == (16, SEQ, D)

    A_c = np.ascontiguousarray(
        np.asarray(L0_attn_A_log, np.float32).reshape(NDC, 128, NST)
        .transpose(1, 0, 2).reshape(128, 128))
    xpwT = np.asarray(L0_attn_x_proj_w, np.float32).T
    xpwT_c = np.ascontiguousarray(
        xpwT.reshape(NDC, 128, 96).transpose(1, 0, 2).reshape(128, NDC * 96))
    dtwT = np.ascontiguousarray(np.asarray(L0_attn_dt_proj_w, np.float32).T)
    dtb_c = np.ascontiguousarray(np.asarray(L0_attn_dt_proj_b, np.float32).reshape(NDC, 128).T)
    fc1b_c = np.ascontiguousarray(np.asarray(L0_ffn_fc1_bias, np.float32).reshape(NFB, 128).T)
    beta_c = np.ascontiguousarray(np.asarray(L0_ffn_ln_beta, np.float32).reshape(NDC, 128).T)
    rows4 = np.ascontiguousarray(np.stack([
        np.asarray(L0_ffn_ln_gamma, np.float32),
        np.asarray(L0_ffn_ln_beta, np.float32),
        np.asarray(L0_attn_D, np.float32),
        np.asarray(L0_ffn_fc2_bias, np.float32)], axis=0))
    q1blk = np.ascontiguousarray(
        np.asarray(L0_ffn_fc1_q, np.int32).reshape(NDC, 128, NFB, 128).transpose(2, 0, 1, 3))
    sc1r = np.ascontiguousarray(
        np.asarray(L0_ffn_fc1_scales, np.float32).reshape(NDC, NFB, 128)
        .transpose(1, 0, 2).reshape(NFB, NDC * 128))
    q2blk = np.ascontiguousarray(np.asarray(L0_ffn_fc2_q, np.int32).reshape(NFB, 128, D))
    sc2r = np.ascontiguousarray(np.asarray(L0_ffn_fc2_scales, np.float32))
    idf = np.eye(128, dtype=np.float32)
    id2b = np.eye(128).astype(BF)
    onesf = np.ones((1, 128), dtype=np.float32)
    cvec = np.ascontiguousarray(np.stack([np.full(128, EPS, np.float32), np.ones(128, np.float32)], axis=1))
    glane = np.ascontiguousarray(
        np.asarray(L0_ffn_ln_gamma, np.float32).reshape(NDC, 128).T)
    selb = np.zeros((32, 32 * 128), dtype=ml_dtypes.float8_e4m3)
    for k in range(32):
        selb[k, k * 128:(k + 1) * 128] = ml_dtypes.float8_e4m3(1.0)

    shared = dict(A_c=A_c, xpwT_c=xpwT_c, dtwT=dtwT, dtb_c=dtb_c, fc1b_c=fc1b_c,
                  beta_c=beta_c,
                  rows4=rows4, q1blk=q1blk, sc1r=sc1r, q2blk=q2blk, sc2r=sc2r,
                  idf=idf, id2b=id2b, onesf=onesf, selb=selb, cvec=cvec,
                  glane=glane)
    return shared, x


def kernel(**inputs):
    shared, x = _host_stage(**inputs)
    nc = build_nc(NT)
    in_maps = []
    for c in range(NCORE):
        m = dict(shared)
        m["xT"] = np.ascontiguousarray(x[2 * c:2 * c + 2].reshape(TPC, D))
        in_maps.append(m)

    res = run_bass_kernel_spmd(nc, in_maps, core_ids=list(range(NCORE)))
    out = np.empty((16, SEQ, D), dtype=np.float32)
    for c in range(NCORE):
        out[2 * c:2 * c + 2] = np.asarray(res.results[c]["outT"]).reshape(2, SEQ, D)
    return out


# version 40
# speedup vs baseline: 1.2122x; 1.0125x over previous
import numpy as np
import ml_dtypes
import contextlib
import concourse.bass as bass
import concourse.mybir as mybir
from concourse.bass import AP
from concourse.bass_utils import run_bass_kernel_spmd

F32 = mybir.dt.float32
FP8 = mybir.dt.float8e4
BF16 = mybir.dt.bfloat16
I32 = mybir.dt.int32
ALU = mybir.AluOpType
AF = mybir.ActivationFunctionType
BF = ml_dtypes.bfloat16

D = 1024
NST = 16
RK = 64
FF = 4096
NCORE = 8
SEQ = 2048
TPC = 2 * SEQ
T = 512
NT = TPC // T
NTC = T // 128
NDC = D // 128
NFB = FF // 128
EPS = 1e-5

ENGS = ("sync", "tensor", "scalar", "vector", "gpsimd")


class Sem:
    __slots__ = ("h", "n", "name")

    def __init__(self, h, name):
        self.h = h
        self.n = 0
        self.name = name


class Planner:

    def __init__(self):
        self.q = {e: [] for e in ENGS}

    def op(self, eng, fn, waits=(), inc=None, amt=1, dma=False):
        ws = []
        for w in waits:
            if w is None:
                continue
            if isinstance(w, list):
                ws.extend(t for t in w if t is not None)
            else:
                ws.append(w)
        tgt = None
        if inc is not None:
            inc.n += amt
            tgt = inc.n
        self.q[eng].append((fn, tuple(ws), inc, tgt, amt, dma))
        return (inc, tgt) if inc is not None else None

    SELF_SER = False

    def emit(self, block):
        for ename in ENGS:
            oplist = self.q[ename]
            deco = getattr(block, ename)

            def section(eng, _ops=oplist):
                marks = {}
                prev_tok = None
                for fn, waits, inc, tgt, amt, dma in _ops:
                    allw = list(waits)
                    if Planner.SELF_SER and not dma and prev_tok is not None:
                        allw.append(prev_tok)
                    for (sem, val) in allw:
                        if marks.get(id(sem), -1) >= val:
                            continue
                        eng.wait_ge(sem.h, val)
                        marks[id(sem)] = val
                    instr = fn(eng)
                    if inc is not None:
                        instr.then_inc(inc.h, amt)
                        if not dma:
                            prev_tok = (inc, tgt)

            deco(section)


def _stat_save_src(sSTAT, c0):
    ap = sSTAT[:]
    return AP(ap.tensor, ap.offset + c0 + 2, [[ap.ap[0][0], 128], [4, 2]])


def _stat_save_dst(sSTAT, tc, par):
    ap = sSTAT[:]
    return AP(ap.tensor, ap.offset + 64 + 8 * par + 2 * tc, [[ap.ap[0][0], 128], [1, 2]])


def _state_pair_src(sH, parH):
    ap = sH[:]
    return AP(ap.tensor, ap.offset + (parH - 1) * T + (T - 1),
              [[ap.ap[0][0], 128], [T, 2]])


def _state_pair_dst(sSTATE, pair):
    ap = sSTATE[:]
    return AP(ap.tensor, ap.offset + (pair - NST), [[ap.ap[0][0], 128], [NST, 2]])


def lane_ap(tile, half, tc):
    ap = tile[:]
    base = (half * 4) * T + tc * 128
    return AP(ap.tensor, ap.offset + base, [[ap.ap[0][0], 128], [T, 4], [1, 128]])


def build_nc(nt):
    nc = bass.Bass("TRN2", target_bir_lowering=False)
    dram = {}

    def din(name, shape, dt):
        dram[name] = nc.dram_tensor(name, shape, dt, kind="ExternalInput").ap()

    din("xT", [TPC, D], F32)
    din("A_c", [128, 128], F32)
    din("xpwT_c", [128, NDC * 96], F32)
    din("dtwT", [RK, D], F32)
    din("dtb_c", [128, NDC], F32)
    din("fc1b_c", [128, NFB], F32)
    din("beta_c", [128, NDC], F32)
    din("rows4", [4, D], F32)
    din("q1blk", [NFB, NDC, 128, 128], I32)
    din("sc1r", [NFB, NDC * 128], F32)
    din("q2blk", [NFB, 128, D], I32)
    din("sc2r", [NFB, D], F32)
    din("idf", [128, 128], F32)
    din("id2b", [128, 128], BF16)
    din("glane", [128, NDC], F32)
    din("onesf", [1, 128], F32)
    din("cvec", [128, 2], F32)
    din("selb", [32, 32 * 128], FP8)

    outT = nc.dram_tensor("outT", [TPC, D], F32, kind="ExternalOutput").ap()
    w1bf = nc.dram_tensor("w1bf", [NFB, 128, NDC * 128], BF16, kind="Internal").ap()
    w2bf = nc.dram_tensor("w2bf", [NFB, 128, D], BF16, kind="Internal").ap()

    es = contextlib.ExitStack()
    with es:
        def sbuf(name, shape, dt):
            return es.enter_context(nc.sbuf_tensor(name, shape, dt))

        def psum(name, shape, dt):
            return es.enter_context(nc.psum_tensor(name, shape, dt))

        sA = sbuf("sA", [128, 128], F32)
        sXPW = sbuf("sXPW", [128, NDC * 96], BF16)
        sDTW = sbuf("sDTW", [RK, D], BF16)
        sDTB = sbuf("sDTB", [128, NDC], F32)
        sFC1B = sbuf("sFC1B", [128, NFB], F32)
        sBETA = sbuf("sBETA", [128, NDC], BF16)
        sDB = sbuf("sDB", [128, D], F32)
        sB2B = sbuf("sB2B", [128, D], F32)
        sIDF = sbuf("sIDF", [128, 128], F32)
        sID2 = sbuf("sID2", [128, 128], BF16)
        sONESF = sbuf("sONESF", [1, 128], F32)
        sCV = sbuf("sCV", [128, 2], F32)
        sGL = sbuf("sGL", [128, NDC], F32)
        sSEL = sbuf("sSEL", [32, 32 * 128], FP8)

        pro = contextlib.ExitStack()
        sCLD = pro.enter_context(nc.sbuf_tensor("sCLD", [128, 2048], F32))
        sROWP = pro.enter_context(nc.sbuf_tensor("sROWP", [1, 1024], F32))
        sSCF = pro.enter_context(nc.sbuf_tensor("sSCF", [32, 1024], F32))
        sWO = pro.enter_context(nc.sbuf_tensor("sWO", [128, 4 * 1024], BF16))
        sSC1 = pro.enter_context(nc.sbuf_tensor("sSC1", [NFB, NDC * 128], BF16))
        sSC2 = pro.enter_context(nc.sbuf_tensor("sSC2", [NFB, D], BF16))
        sQ = pro.enter_context(nc.sbuf_tensor("sQ", [128, 3 * 1024], I32))
        pro.close()
        sY = sbuf("sY", [128, NDC * T], F32)
        sT3 = sbuf("sT3", [128, 4 * T], F32)
        sGH = sbuf("sGH", [128, 2 * T], F32)
        sYT = sbuf("sYT", [128, T], F32)
        sCW = sbuf("sCW", [128, T], F32)
        sXC = sbuf("sXC", [128, 2 * D], F32)

        sXT = sbuf("sXT", [128, NTC * D], F32)
        sXNT = sbuf("sXNT", [128, NTC * D], F32)
        sSTAT = sbuf("sSTAT", [128, 80], F32)
        sXNBL = sbuf("sXNBL", [128, NDC * T], BF16)
        sH2T = sbuf("sH2T", [128, 2 * D], BF16)
        sH2L = sbuf("sH2L", [128, NDC * T], BF16)
        sDRK = sbuf("sDRK", [RK, T], BF16)
        sBCr = sbuf("sBCr", [32, T], BF16)
        sDLT = sbuf("sDLT", [128, NDC * T], F32)
        sUP = sbuf("sUP", [128, NDC * T], BF16)
        sDA = sbuf("sDA", [128, 4 * T], F32)
        sDBU = sbuf("sDBU", [128, 4 * T], BF16)
        sH = sbuf("sH", [128, 2 * T], BF16)
        sTMP = sbuf("sTMP", [128, 4 * T], BF16)
        sBCB = sbuf("sBCB", [128, 2 * T], BF16)
        sBCC = sbuf("sBCC", [128, 2 * T], BF16)
        sSTATE = sbuf("sSTATE", [128, 128], F32)
        sW = sbuf("sW", [128, 4 * 1024], BF16)
        sHID = sbuf("sHID", [128, NFB * T], BF16)
        sOUT = sbuf("sOUT", [128, D], F32)

        P1 = psum("P1", [128, 512], F32)
        P2 = psum("P2", [128, 512], F32)
        PB = psum("PB", [128, 512], F32)
        PC = psum("PC", [128, 512], F32)
        PF = [psum(f"PF{i}", [128, 512], F32) for i in range(4)]

        names = ["C", "SC", "ROW", "Qa", "Qb", "Qc", "WBa", "WBb", "WBc", "WBd",
                 "W1a", "W1b",
                 "W2a", "W2b", "W2c", "W2d", "X", "XCa", "XCb", "OUT", "PE",
                 "ACT", "DVE", "PL"] + [f"YA{d}" for d in range(8)]
        sems = {k: Sem(es.enter_context(nc.semaphore(k)), k) for k in names}
        S_C, S_SC, S_ROW, S_X, S_OUT, S_PE, S_ACT, S_DVE, S_PL = (
            sems[k] for k in ["C", "SC", "ROW", "X", "OUT", "PE", "ACT", "DVE", "PL"])
        S_Q = [sems["Qa"], sems["Qb"], sems["Qc"]]
        S_XC = [sems["XCa"], sems["XCb"]]
        S_YA = [sems[f"YA{d}"] for d in range(8)]
        S_WB = [sems["WBa"], sems["WBb"], sems["WBc"], sems["WBd"]]
        S_W1 = [sems["W1a"], sems["W1b"]]
        S_W2 = [sems["W2a"], sems["W2b"], sems["W2c"], sems["W2d"]]

        p = Planner()

        for nm, dst in [("A_c", sCLD[:, 0:128]), ("xpwT_c", sCLD[:, 128:128 + NDC * 96]),
                        ("dtwT", sCLD[0:RK, 1024:2048]), ("dtb_c", sDTB[:]),
                        ("fc1b_c", sFC1B[:]), ("beta_c", sCLD[0:128, 896:904]),
                        ("idf", sIDF[:]), ("id2b", sID2[:]), ("onesf", sONESF[:]),
                        ("cvec", sCV[:]), ("glane", sGL[:]), ("selb", sSEL[:])]:
            p.op("sync", lambda e, d=dst, n=nm: e.dma_start(out=d, in_=dram[n][:]),
                 inc=S_C, amt=16, dma=True)
        tok_consts = (S_C, S_C.n)
        tk_A = tk_xpw = tk_dtw = tk_dtb = tk_f1b = tk_idf = tk_onef = tk_cv = tk_sel = tok_consts
        tk_s1 = p.op("sync", lambda e: e.dma_start(out=sSCF[:], in_=dram["sc1r"][:]), inc=S_SC, amt=16, dma=True)

        t_ae = p.op("scalar", lambda e: e.activation(sA[:], sCLD[:, 0:128], AF.Exp), waits=[tk_A], inc=S_ACT)
        t_a = p.op("vector", lambda e: e.tensor_scalar(sA[:], sA[:], -1.0, None, ALU.mult), waits=[t_ae], inc=S_DVE)
        t_xpwb = p.op("vector", lambda e: e.tensor_copy(sXPW[:], sCLD[:, 128:128 + NDC * 96]), waits=[tk_xpw], inc=S_DVE)
        t_beta = p.op("vector", lambda e: e.tensor_copy(sBETA[:], sCLD[0:128, 896:904]), waits=[tok_consts], inc=S_DVE)
        t_dtwb = p.op("vector", lambda e: e.tensor_copy(sDTW[:], sCLD[0:RK, 1024:2048]), waits=[tk_dtw], inc=S_DVE)
        t_sc1b = p.op("vector", lambda e: e.tensor_copy(sSC1[:], sSCF[:]), waits=[tk_s1], inc=S_DVE)
        tk_s2 = p.op("sync", lambda e: e.dma_start(out=sSCF[:], in_=dram["sc2r"][:]), waits=[t_sc1b], inc=S_SC, amt=16, dma=True)
        t_sc2b = p.op("vector", lambda e: e.tensor_copy(sSC2[:], sSCF[:]), waits=[tk_s2], inc=S_DVE)

        prev_evs = []
        bc_done_dst = {}
        for idx, (dst, scl) in enumerate([(sDB, 1.0), (sB2B, 0.5)]):
            rowmap = [2, 3]
            tk_row = p.op("sync",
                          lambda e, i=rowmap[idx]: e.dma_start(out=sROWP[:], in_=dram["rows4"][i:i + 1, :]),
                          waits=prev_evs, inc=S_ROW, amt=16, dma=True)
            evs = []
            for h in range(2):
                PP = P1 if h == 0 else P2
                tm = p.op("tensor",
                          lambda e, hh=h, PPl=PP: e.matmul(
                              PPl[:, 0:512], sONESF[0:1, :],
                              sROWP[0:1, hh * 512:(hh + 1) * 512], start=True, stop=True),
                          waits=[tk_row, tk_onef] + prev_evs, inc=S_PE)
                te = p.op("scalar",
                          lambda e, d=dst, hh=h, s=scl, PPl=PP: e.activation(
                              d[:, hh * 512:(hh + 1) * 512], PPl[:, 0:512], AF.Copy, scale=s),
                          waits=[tm], inc=S_ACT)
                evs.append(te)
            prev_evs = list(evs)
            bc_done_dst[idx] = list(evs)
        bcast_done = evs[-1]
        tk_db, tk_b2 = (bc_done_dst[0], bc_done_dst[1])

        deq = dict(q_rd=[None] * 3, wb_par=[None] * 4,
                   stt_h=[None, None], pf_rd=[None] * 3, bfold_prev=None,
                   q1tok={}, q2tok={}, stt_fb={},
                   wb2_par=[None] * 4)
        wb1_done = {}
        wb2_done = {}
        bfold_toks = []
        deq_pf_free = [[], [], [], []]

        def emit_deq1_dma(fb):
            par = fb % 3
            qtoks = []
            for d2 in range(NDC):
                tok = p.op("sync",
                           lambda e, f=fb, d=d2, pr=par: e.dma_start(
                               out=sQ[:, pr * 1024 + d * 128: pr * 1024 + (d + 1) * 128],
                               in_=dram["q1blk"][f, d, :, :]),
                           waits=[deq["q_rd"][par]], inc=S_Q[par], amt=16,
                           dma=True)
                qtoks.append(tok)
            deq["q1tok"][fb] = qtoks[-1]

        def emit_deq1(fb):
            par = fb % 3
            s4 = fb % 4
            sts = []
            for h in range(2):
                PP = PF[h]
                tm = p.op("tensor",
                          lambda e, f=fb, hh=h, PPl=PP: e.matmul(
                              PPl[:, 0:512], sSEL[:, f * 128:(f + 1) * 128],
                              sSC1[:, hh * 512:(hh + 1) * 512], start=True, stop=True),
                          waits=[t_sc1b, tk_sel, deq["stt_h"][h]], inc=S_PE)
                ts_ = p.op("vector",
                           lambda e, hh=h, pr=par, w4=s4, PPl=PP: e.scalar_tensor_tensor(
                               sWO[:, w4 * 1024 + hh * 512: w4 * 1024 + (hh + 1) * 512],
                               sQ[:, pr * 1024 + hh * 512: pr * 1024 + (hh + 1) * 512],
                               128.0, PPl[:, 0:512], ALU.subtract, ALU.mult),
                           waits=[deq["q1tok"][fb], tm, deq["wb_par"][s4],
                                  G["wo_rd"][s4]],
                           inc=S_DVE)
                sts.append(ts_)
                deq["stt_h"][h] = ts_
                deq_pf_free[h] = [ts_]
            deq["q_rd"][par] = sts[-1]
            bmms = []
            for d2 in range(NDC):
                tok = p.op("tensor",
                           lambda e, d=d2, w4=s4: e.matmul(
                               PF[2][0:128, 0:1],
                               sWO[:, w4 * 1024 + d * 128: w4 * 1024 + (d + 1) * 128],
                               sBETA[:, d:d + 1], start=(d == 0), stop=(d == NDC - 1)),
                           waits=sts + [t_beta] +
                                 ([deq["bfold_prev"]] if deq["bfold_prev"] else []),
                           inc=S_PE)
                bmms.append(tok)
            bf = p.op("vector",
                      lambda e, f=fb: e.scalar_tensor_tensor(
                          sFC1B[:, f:f + 1], PF[2][0:128, 0:1], 1.0,
                          sFC1B[:, f:f + 1], ALU.mult, ALU.add),
                      waits=[bmms[-1], tok_consts], inc=S_DVE)
            deq["bfold_prev"] = bf
            bfold_toks.append(bf)
            deq["stt_fb"][fb] = list(sts)
            deq_pf_free[2] = [bf]
            tok = p.op("sync",
                       lambda e, f=fb, w4=s4: e.dma_start(
                           out=w1bf[f, :, :], in_=sWO[:, w4 * 1024:(w4 + 1) * 1024]),
                       waits=sts, inc=S_WB[s4], amt=16, dma=True)
            deq["wb_par"][s4] = tok
            wb1_done[fb] = tok

        def emit_deq2_dma(fb):
            par = fb % 3
            deq["q2tok"][fb] = p.op("sync",
                                    lambda e, f=fb, pr=par: e.dma_start(
                                        out=sQ[:, pr * 1024:(pr + 1) * 1024],
                                        in_=dram["q2blk"][f, :, :]),
                                    waits=[deq["q_rd"][par]],
                                    inc=S_Q[par], amt=16, dma=True)

        def emit_deq2(fb, p12_gate_first):
            par = fb % 3
            s4 = fb % 4
            tq = deq["q2tok"][fb]
            sts = []
            for h in range(2):
                PP = P1 if h == 0 else P2
                tm = p.op("tensor",
                          lambda e, f=fb, hh=h, PPl=PP: e.matmul(
                              PPl[:, 0:512], sSEL[:, f * 128:(f + 1) * 128],
                              sSC2[:, hh * 512:(hh + 1) * 512], start=True, stop=True),
                          waits=[t_sc2b, deq["stt_h"][h]] + p12_gate_first,
                          inc=S_PE)
                ts_ = p.op("vector",
                           lambda e, hh=h, pr=par, w4=s4, PPl=PP: e.scalar_tensor_tensor(
                               sWO[:, w4 * 1024 + hh * 512: w4 * 1024 + (hh + 1) * 512],
                               sQ[:, pr * 1024 + hh * 512: pr * 1024 + (hh + 1) * 512],
                               128.0, PPl[:, 0:512], ALU.subtract, ALU.mult),
                           waits=[tq, tm, deq["wb2_par"][s4], G["wo_rd"][s4]],
                           inc=S_DVE)
                sts.append(ts_)
                deq["stt_h"][h] = ts_
            deq["q_rd"][par] = sts[-1]
            deq["stt_fb"][(1, fb)] = list(sts)
            tok = p.op("sync",
                       lambda e, f=fb, w4=s4: e.dma_start(
                           out=w2bf[f, :, :], in_=sWO[:, w4 * 1024:(w4 + 1) * 1024]),
                       waits=sts, inc=S_WB[s4], amt=16, dma=True)
            deq["wb2_par"][s4] = tok
            wb2_done[fb] = tok

        stage_early = [t_ae, t_xpwb, t_dtwb, t_beta, t_sc2b, bcast_done]

        ROT = 4
        G = {
            "w_free": [None] * 4,
            "pf_free": [[], [], [], []],
            "da_free": [None] * ROT, "dbu_free": [None, None],
            "h_free": [None, None], "tmp_free": [None] * ROT,
            "bcb_rd": [None, None], "bcc_free": [None, None],
            "pb_free": None, "pc_free": None,
            "t3_free": [None] * 4, "gh_free": [None, None],
            "out_gate": None, "xc_free": [None, None], "cw_free": None,
            "tail_gate": None, "p12_mm_gate": None, "wo_rd": [None] * 4,
            "ylast_pe": None,
            "y_final": {dc: list(stage_early) for dc in range(NDC)},
            "hid_pe": None, "h2l_pe": None, "xnbl_pe": None, "up_dve": None,
            "stat_dve": None, "h2t_pe": [None, None],
            "xt_act": None, "xt_dve": None, "xnt_act": None,
            "p12_act": list(bc_done_dst[1]), "p12_pe": None,
            "musave_rd": {0: None, 1: None},
            "st_tok": {}, "syt_last": None,
        }

        def emit_xload_tc(ti, tc, chain=None):
            return p.op("sync",
                        lambda e, t_=tc, r=ti * T: e.dma_start(
                            out=sXT[:, t_ * D:(t_ + 1) * D],
                            in_=dram["xT"][r + t_ * 128: r + (t_ + 1) * 128, :]),
                        waits=[G["xt_act"], G["xt_dve"], chain], inc=S_X,
                        amt=16, dma=True)

        def emit_xload(ti):
            toks = {}
            for tc in range(NTC):
                toks[tc] = emit_xload_tc(ti, tc, toks.get(tc - 1))
            return toks

        class Prolog:
            pass

        def emit_prolog_piece(PR, ti, k, tok_x, tcs=None):
            if tcs is None:
                tcs = range(NTC)
            if k == 0:
                for tc in tcs:
                    xs = sXT[:, tc * D:(tc + 1) * D]
                    c0 = tc * 8
                    t_sx = p.op("scalar",
                                lambda e, a=xs, c=c0: e.activation(
                                    sOUT[:, 0:D], a, AF.Identity,
                                    accum_out=sSTAT[:, c:c + 1]),
                                waits=[tok_x[tc], G["stat_dve"], G["out_gate"]],
                                inc=S_ACT)
                    t_sq = p.op("scalar",
                                lambda e, a=xs, c=c0: e.activation(
                                    sOUT[:, 0:D], a, AF.Square,
                                    accum_out=sSTAT[:, c + 1:c + 2]),
                                waits=[tok_x[tc]], inc=S_ACT)
                    PR.sums1[tc] = (t_sx, t_sq)
            elif k == 1:
                rss = []
                for tc in range(NTC):
                    c0 = tc * 8
                    t_sx, t_sq = PR.sums1[tc]
                    t_mu = p.op("vector",
                                lambda e, c=c0: e.tensor_scalar(
                                    sSTAT[:, c + 2:c + 3], sSTAT[:, c:c + 1], 1.0 / D,
                                    None, ALU.mult),
                                waits=[t_sx], inc=S_DVE)
                    t_m2 = p.op("vector",
                                lambda e, c=c0: e.tensor_tensor(
                                    sSTAT[:, c + 3:c + 4], sSTAT[:, c + 2:c + 3],
                                    sSTAT[:, c + 2:c + 3], ALU.mult),
                                waits=[t_mu], inc=S_DVE)
                    t_var = p.op("vector",
                                 lambda e, c=c0: e.scalar_tensor_tensor(
                                     sSTAT[:, c + 4:c + 5], sSTAT[:, c + 1:c + 2], 1.0 / D,
                                     sSTAT[:, c + 3:c + 4], ALU.mult, ALU.subtract),
                                 waits=[t_sq, t_m2], inc=S_DVE)
                    PR.var1[tc] = t_var
                for tc in range(NTC):
                    c0 = tc * 8
                    t_ln = p.op("scalar",
                                lambda e, c=c0: e.activation(
                                    sSTAT[:, c + 5:c + 6], sSTAT[:, c + 4:c + 5],
                                    AF.Ln, bias=sCV[:, 0:1]),
                                waits=[PR.var1[tc], tk_cv], inc=S_ACT)
                    t_rs = p.op("scalar",
                                lambda e, c=c0: e.activation(
                                    sSTAT[:, c + 6:c + 7], sSTAT[:, c + 5:c + 6],
                                    AF.Exp, scale=-0.5),
                                waits=[t_ln], inc=S_ACT)
                    PR.rs1[tc] = t_rs
            elif k == 2:
                for tc in tcs:
                    c0 = tc * 8
                    c1 = 32 + tc * 8
                    t_xn = p.op("vector",
                                lambda e, t_=tc, c=c0: e.tensor_scalar(
                                    sXNT[:, t_ * D:(t_ + 1) * D], sXT[:, t_ * D:(t_ + 1) * D],
                                    sSTAT[:, c + 2:c + 3], sSTAT[:, c + 6:c + 7],
                                    ALU.subtract, ALU.mult),
                                waits=[PR.rs1[tc], G["xnt_act"]], inc=S_DVE)
                    PR.xn[tc] = t_xn
                    PR.musave[tc] = p.op("vector",
                                         lambda e, t_=tc, c=c0, pp=ti % 2: e.tensor_copy(
                                             _stat_save_dst(sSTAT, t_, pp),
                                             _stat_save_src(sSTAT, c)),
                                         waits=[PR.rs1[tc], G["musave_rd"][ti % 2]],
                                         inc=S_DVE)
                for tc in tcs:
                    c1 = 32 + tc * 8
                    xns = sXNT[:, tc * D:(tc + 1) * D]
                    t_sx2 = p.op("scalar",
                                 lambda e, a=xns, c=c1: e.activation(
                                     sOUT[:, 0:D], a, AF.Identity,
                                     accum_out=sSTAT[:, c:c + 1]),
                                 waits=[PR.xn[tc]], inc=S_ACT)
                    t_sq2 = p.op("scalar",
                                 lambda e, a=xns, c=c1: e.activation(
                                     sOUT[:, 0:D], a, AF.Square,
                                     accum_out=sSTAT[:, c + 1:c + 2]),
                                 waits=[PR.xn[tc]], inc=S_ACT)
                    PR.sums2[tc] = (t_sx2, t_sq2)
                if NTC - 1 not in tcs:
                    return
                PR.xt_act = PR.sums2[NTC - 1][1]
                PR.xt_dve = PR.xn[NTC - 1]
                PR.xnt_act = PR.sums2[NTC - 1][1]
            elif k == 3:
                for tc in range(NTC):
                    c1 = 32 + tc * 8
                    t_sx2, t_sq2 = PR.sums2[tc]
                    t_mu2 = p.op("vector",
                                 lambda e, c=c1: e.tensor_scalar(
                                     sSTAT[:, c + 2:c + 3], sSTAT[:, c:c + 1], 1.0 / D,
                                     None, ALU.mult),
                                 waits=[t_sx2], inc=S_DVE)
                    t_m22 = p.op("vector",
                                 lambda e, c=c1: e.tensor_tensor(
                                     sSTAT[:, c + 3:c + 4], sSTAT[:, c + 2:c + 3],
                                     sSTAT[:, c + 2:c + 3], ALU.mult),
                                 waits=[t_mu2], inc=S_DVE)
                    t_var2 = p.op("vector",
                                  lambda e, c=c1: e.scalar_tensor_tensor(
                                      sSTAT[:, c + 4:c + 5], sSTAT[:, c + 1:c + 2], 1.0 / D,
                                      sSTAT[:, c + 3:c + 4], ALU.mult, ALU.subtract),
                                  waits=[t_sq2, t_m22], inc=S_DVE)
                    PR.var2[tc] = t_var2
                for tc in range(NTC):
                    c1 = 32 + tc * 8
                    t_ln2 = p.op("scalar",
                                 lambda e, c=c1: e.activation(
                                     sSTAT[:, c + 5:c + 6], sSTAT[:, c + 4:c + 5],
                                     AF.Ln, bias=sCV[:, 0:1]),
                                 waits=[PR.var2[tc]], inc=S_ACT)
                    t_rs2 = p.op("scalar",
                                 lambda e, c=c1: e.activation(
                                     sSTAT[:, c + 6:c + 7], sSTAT[:, c + 5:c + 6],
                                     AF.Exp, scale=-0.5),
                                 waits=[t_ln2], inc=S_ACT)
                    PR.rs2[tc] = t_rs2
            elif k == 4 or k == 5:
                for tc in [2 * (k - 4), 2 * (k - 4) + 1]:
                    c1 = 32 + tc * 8
                    h2s = (tc % 2) * D
                    xns = sXNT[:, tc * D:(tc + 1) * D]
                    t_t2 = p.op("vector",
                                lambda e, a=xns, c=c1, s=h2s: e.tensor_scalar(
                                    sH2T[:, s:s + D], a, sSTAT[:, c + 2:c + 3],
                                    sSTAT[:, c + 6:c + 7], ALU.subtract, ALU.mult),
                                waits=[PR.rs2[tc], G["h2t_pe"][tc % 2],
                                       PR.h2l_ev[-1] if PR.h2l_ev else None], inc=S_DVE)
                    t_h2 = t_t2
                    if tc == NTC - 1:
                        PR.stat_dve = t_t2
                    for half in range(2):
                        trx, trh = [], []
                        for j in range(4):
                            dc = half * 4 + j
                            tok = p.op("tensor",
                                       lambda e, d=dc, t_=tc, jj=j: e.transpose(
                                           P1[:, jj * 128:(jj + 1) * 128],
                                           sXNT[:, t_ * D + d * 128: t_ * D + (d + 1) * 128],
                                           sIDF[:]),
                                       waits=[PR.xn[tc],
                                              PR.xnl_ev[-1] if PR.xnl_ev else None] +
                                             (G["p12_act"] if not PR.xnl_ev else []),
                                       inc=S_PE)
                            trx.append(tok)
                        te = p.op("scalar",
                                  lambda e, hh=half, t_=tc: e.activation(
                                      lane_ap(sXNBL, hh, t_), P1[:, 0:512], AF.Identity),
                                  waits=trx + [G["xnbl_pe"], G["up_dve"]], inc=S_ACT)
                        PR.xnl_ev.append(te)
                        for j in range(4):
                            dc = half * 4 + j
                            tok = p.op("tensor",
                                       lambda e, d=dc, jj=j, s=h2s: e.matmul(
                                           P2[:, jj * 128:(jj + 1) * 128],
                                           sH2T[:, s + d * 128: s + (d + 1) * 128],
                                           sID2[:], start=True, stop=True),
                                       waits=[t_h2,
                                              PR.h2l_ev[-1] if PR.h2l_ev else None] +
                                             (G["p12_act"] if not PR.h2l_ev else []),
                                       inc=S_PE)
                            trh.append(tok)
                        for j in range(4):
                            dc = half * 4 + j
                            te2 = p.op("scalar",
                                       lambda e, d=dc, jj=j, t_=tc: e.activation(
                                           sH2L[:, d * T + t_ * 128: d * T + (t_ + 1) * 128],
                                           P2[:, jj * 128:(jj + 1) * 128],
                                           AF.Identity, scale=sGL[:, d:d + 1]),
                                       waits=[trh[j], tok_consts, G["h2l_pe"]],
                                       inc=S_ACT)
                        PR.h2l_ev.append(te2)
                    G["h2t_pe"][tc % 2] = trh[-1]
            elif k == 6:
                drk_mm, bcr_mm = [], []
                for dc in range(NDC):
                    tok = p.op("tensor",
                               lambda e, d=dc: e.matmul(
                                   P1[0:64, 0:512], sXPW[:, d * 96: d * 96 + 64],
                                   sXNBL[:, d * T:(d + 1) * T],
                                   start=(d == 0), stop=(d == NDC - 1)),
                               waits=[PR.xnl_ev[-1], t_xpwb], inc=S_PE)
                    drk_mm.append(tok)
                for dc in range(NDC):
                    tok = p.op("tensor",
                               lambda e, d=dc: e.matmul(
                                   P2[0:32, 0:512], sXPW[:, d * 96 + 64: d * 96 + 96],
                                   sXNBL[:, d * T:(d + 1) * T],
                                   start=(d == 0), stop=(d == NDC - 1)),
                               waits=[PR.xnl_ev[-1], PR.h2l_ev[-1], t_xpwb], inc=S_PE)
                    bcr_mm.append(tok)
                PR.xnbl_pe = bcr_mm[-1]
                PR.t_drk = p.op("vector", lambda e: e.tensor_copy(sDRK[:], P1[0:64, 0:512]),
                                waits=[drk_mm[-1]], inc=S_DVE)
                PR.t_bcr = p.op("vector", lambda e: e.tensor_copy(sBCr[:], P2[0:32, 0:512]),
                                waits=[bcr_mm[-1]], inc=S_DVE)
            elif k == 7:
                sp_prev = [None, None]
                for dc in range(NDC):
                    par = dc % 2
                    PP = P1 if par == 0 else P2
                    t_mm = p.op("tensor",
                                lambda e, d=dc, PPl=PP: e.matmul(
                                    PPl[:, 0:512], sDTW[:, d * 128:(d + 1) * 128], sDRK[:],
                                    start=True, stop=True),
                                waits=[PR.t_drk, PR.t_bcr, t_dtwb, sp_prev[par],
                                       G["p12_mm_gate"]],
                                inc=S_PE)
                    t_e = p.op("scalar",
                               lambda e, d=dc, PPl=PP, pr=par: e.activation(
                                   sOUT[:, pr * 512:(pr + 1) * 512], PPl[:, 0:512],
                                   AF.Exp, bias=sDTB[:, d:d + 1]),
                               waits=[t_mm] + ([G["out_gate"]] if dc < 2 else []),
                               inc=S_ACT)
                    t_d = p.op("scalar",
                               lambda e, d=dc, pr=par: e.activation(
                                   sDLT[:, d * T:(d + 1) * T],
                                   sOUT[:, pr * 512:(pr + 1) * 512],
                                   AF.Ln, bias=sCV[:, 1:2]),
                               waits=[t_e, tk_cv, G["up_dve"]], inc=S_ACT)
                    sp_prev[par] = t_e
                    PR.dlt[dc] = t_d
                    PR.up[dc] = p.op("vector",
                                     lambda e, d=dc: e.tensor_tensor(
                                         sUP[:, d * T:(d + 1) * T],
                                         sDLT[:, d * T:(d + 1) * T],
                                         sXNBL[:, d * T:(d + 1) * T], ALU.mult),
                                     waits=[t_d, PR.xnl_ev[-1]], inc=S_DVE)
                PR.up_dve = PR.up[NDC - 1]
                PR.delta_mm_last = t_mm

        def new_prolog():
            PR = Prolog()
            PR.xn = {}
            PR.musave = {}
            PR.sums1 = {}
            PR.var1 = {}
            PR.rs1 = {}
            PR.sums2 = {}
            PR.var2 = {}
            PR.rs2 = {}
            PR.xnl_ev = []
            PR.h2l_ev = []
            PR.dlt = {}
            PR.up = {}
            return PR

        def adopt(PR):
            for k in ["xt_act", "xt_dve", "xnt_act", "stat_dve",
                      "xnbl_pe", "up_dve"]:
                G[k] = getattr(PR, k)

        tok_x0 = emit_xload(0)
        PRO = new_prolog()
        for k in range(8):
            emit_prolog_piece(PRO, 0, k, tok_x0)
        adopt(PRO)
        emit_deq1_dma(0)
        emit_deq1_dma(1)
        emit_deq1_dma(2)

        for ti in range(nt):
            r0 = ti * T
            init_zero = (r0 % SEQ) == 0
            dlt_toks = PRO.dlt
            up_toks = PRO.up
            h2l_last = PRO.h2l_ev[-1]
            t_bcr = PRO.t_bcr

            tok_xn_next = {} if ti + 1 < nt else None

            y_ms = {}
            for dc in range(NDC):
                y_ms[dc] = p.op("gpsimd",
                                lambda e, d=dc: e.memset(sY[:, d * T:(d + 1) * T], 0.0),
                                waits=[G["ylast_pe"], G["y_final"][dc]], inc=S_PL)

            y_last = dict(y_ms)
            scan_idx = 0
            hid_toks = {}
            ev_par = [None, None]
            fc1mm_par = [None, None]
            state_rd = None
            PRN = new_prolog() if ti + 1 < nt else None

            def emit_fc1(fb):
                par = fb % 2
                PP = P1 if par == 0 else P2
                if ti == 0:
                    wsrc = sWO
                    wbase = (fb % 4) * 1024
                    wgate = [deq["stt_fb"][fb]]
                else:
                    wsrc = sW
                    wbase = par * 1024
                    wgate = [p.op("sync",
                                  lambda e, f=fb, pr=par: e.dma_start(
                                      out=sW[:, pr * 1024:(pr + 1) * 1024],
                                      in_=w1bf[f, :, :]),
                                  waits=[fc1mm_par[par] if fc1mm_par[par]
                                         else G["w_free"][par], wb1_done[fb]],
                                  inc=S_W1[par], amt=16, dma=True)]
                mms = []
                for dc in range(NDC):
                    tok = p.op("tensor",
                               lambda e, d=dc, wb=wbase, W=wsrc, PPl=PP: e.matmul(
                                   PPl[:, 0:512],
                                   W[:, wb + d * 128: wb + (d + 1) * 128],
                                   sH2L[:, d * T:(d + 1) * T],
                                   start=(d == 0), stop=(d == NDC - 1)),
                               waits=wgate + [h2l_last, ev_par[par],
                                              dlt_toks[NDC - 1]], inc=S_PE)
                    mms.append(tok)
                fc1mm_par[par] = mms[-1]
                if ti == 0:
                    G["wo_rd"][fb % 4] = mms[-1]
                t_si = p.op("scalar",
                            lambda e, f=fb, PPl=PP: e.activation(
                                sHID[:, f * T:(f + 1) * T], PPl[:, 0:512],
                                AF.Identity, bias=sFC1B[:, f:f + 1]),
                            waits=[mms[-1], tk_f1b, G["hid_pe"], bfold_toks[fb]],
                            inc=S_ACT)
                ev_par[par] = t_si
                hid_toks[fb] = t_si

            bc_ev = {}
            pre_da = {}

            def emit_bc_pe(n):
                t_bcB = p.op("tensor",
                             lambda e, nn=n: e.matmul(
                                 PB[:, 0:512], sSEL[:, nn * 128:(nn + 1) * 128],
                                 sBCr[:], start=True, stop=True),
                             waits=[t_bcr, G["pb_free"]], inc=S_PE)
                t_bcC = p.op("tensor",
                             lambda e, nn=n: e.matmul(
                                 PC[:, 0:512], sSEL[:, (nn + 16) * 128:(nn + 17) * 128],
                                 sBCr[:], start=True, stop=True),
                             waits=[t_bcr, G["pc_free"]], inc=S_PE)
                return (t_bcB, t_bcC)

            def emit_bc_ev(n, pe_toks):
                par = n % 2
                t_evB = p.op("scalar",
                             lambda e, pr=par: e.activation(
                                 sBCB[:, pr * T:(pr + 1) * T], PB[:, 0:512], AF.Identity),
                             waits=[pe_toks[0], G["bcb_rd"][par]], inc=S_ACT)
                G["pb_free"] = t_evB
                t_evC = p.op("scalar",
                             lambda e, pr=par: e.activation(
                                 sBCC[:, pr * T:(pr + 1) * T], PC[:, 0:512], AF.Identity),
                             waits=[pe_toks[1], G["bcc_free"][par]], inc=S_ACT)
                G["pc_free"] = t_evC
                bc_ev[n] = (t_evB, t_evC)

            def scan_group(n):
                nonlocal scan_idx, state_rd
                parB = n % 2
                parC = n % 2
                t_evB, t_evC = bc_ev[n]
                pe_next = emit_bc_pe(n + 1) if n + 1 < NST else None
                last_db = None

                def rep2(tile, par):
                    ap = tile[:]
                    return AP(ap.tensor, ap.offset + par * T,
                              [[ap.ap[0][0], 128], [0, 2], [1, T]])

                for j in range(NDC // 2):
                    dc0, dc1 = 2 * j, 2 * j + 1
                    pair0 = dc0 * NST + n
                    pair1 = dc1 * NST + n
                    parA0 = scan_idx % ROT
                    parA1 = (scan_idx + 1) % ROT
                    pj = scan_idx // 2
                    parU = pj % 2
                    parM = pj % 2
                    if n == 8 and dc0 in pre_da:
                        t_da0, ap0, slot0 = pre_da[dc0]
                    else:
                        ap0 = sDA[:, parA0 * T:(parA0 + 1) * T]
                        slot0 = parA0
                        t_da0 = p.op("scalar",
                                     lambda e, d=dc0, nn=n, a=ap0: e.activation(
                                         a, sDLT[:, d * T:(d + 1) * T],
                                         AF.Exp, scale=sA[:, d * 16 + nn: d * 16 + nn + 1]),
                                     waits=[dlt_toks[dc0], t_a, G["da_free"][parA0]],
                                     inc=S_ACT)
                    if n == 8 and dc1 in pre_da:
                        t_da1, ap1, slot1 = pre_da[dc1]
                    else:
                        ap1 = sDA[:, parA1 * T:(parA1 + 1) * T]
                        slot1 = parA1
                        t_da1 = p.op("scalar",
                                     lambda e, d=dc1, nn=n, a=ap1: e.activation(
                                         a, sDLT[:, d * T:(d + 1) * T],
                                         AF.Exp, scale=sA[:, d * 16 + nn: d * 16 + nn + 1]),
                                     waits=[dlt_toks[dc1], t_a, G["da_free"][parA1]],
                                     inc=S_ACT)
                    t_db = p.op("vector",
                                lambda e, d=dc0, pu=parU, prB=parB: e.tensor_tensor(
                                    sDBU[:, pu * 2 * T:(pu + 1) * 2 * T],
                                    sUP[:, d * T:(d + 2) * T],
                                    rep2(sBCB, prB), ALU.mult),
                                waits=[up_toks[dc0], up_toks[dc1], t_evB,
                                       G["dbu_free"][parU]], inc=S_DVE)
                    last_db = t_db
                    init0 = 0.0 if init_zero else sSTATE[:, pair0:pair0 + 1]
                    init1 = 0.0 if init_zero else sSTATE[:, pair1:pair1 + 1]
                    t_sc0 = p.op("vector",
                                 lambda e, a=ap0, pu=parU, ii=init0: e.tensor_tensor_scan(
                                     sH[:, 0:T], a,
                                     sDBU[:, pu * 2 * T:pu * 2 * T + T],
                                     ii, ALU.mult, ALU.add),
                                 waits=[t_da0, t_db, G["h_free"][0]] +
                                       ([] if init_zero else [G["st_tok"].get(pair0)]),
                                 inc=S_DVE)
                    t_sc1 = p.op("vector",
                                 lambda e, a=ap1, pu=parU, ii=init1: e.tensor_tensor_scan(
                                     sH[:, T:2 * T], a,
                                     sDBU[:, pu * 2 * T + T:(pu + 1) * 2 * T],
                                     ii, ALU.mult, ALU.add),
                                 waits=[t_da1, G["h_free"][1]] +
                                       ([] if init_zero else [G["st_tok"].get(pair1)]),
                                 inc=S_DVE)
                    if slot0 is not None:
                        G["da_free"][slot0] = t_sc0
                    else:
                        G["p12_act"] = G["p12_act"] + [t_sc0]
                    if slot1 is not None:
                        G["da_free"][slot1] = t_sc1
                    else:
                        G["p12_act"] = G["p12_act"] + [t_sc1]
                    G["dbu_free"][parU] = t_sc1
                    t_st = p.op("vector",
                                lambda e, pp=pair1: e.tensor_copy(
                                    _state_pair_dst(sSTATE, pp),
                                    _state_pair_src(sH, 1)),
                                waits=[t_sc1], inc=S_DVE)
                    state_rd = t_st
                    G["st_tok"][pair0] = t_st
                    G["st_tok"][pair1] = t_st
                    t_tm = p.op("vector",
                                lambda e, pm=parM, prC=parC: e.tensor_tensor(
                                    sTMP[:, pm * 2 * T:(pm + 1) * 2 * T],
                                    sH[:, 0:2 * T],
                                    rep2(sBCC, prC), ALU.mult),
                                waits=[t_sc1, t_evC, G["tmp_free"][parM]],
                                inc=S_DVE)
                    G["h_free"][0] = t_tm
                    G["h_free"][1] = t_tm
                    t_y = p.op("gpsimd",
                               lambda e, d=dc0, pm=parM: e.dma_start(
                                   out=sY[:, d * T:(d + 2) * T],
                                   in_=sTMP[:, pm * 2 * T:(pm + 1) * 2 * T],
                                   accum_op=ALU.add),
                               waits=[t_tm, y_last[dc0], y_last[dc1]],
                               inc=S_YA[dc0], amt=16, dma=True)
                    G["tmp_free"][parM] = t_y
                    y_last[dc0] = t_y
                    y_last[dc1] = t_y
                    scan_idx += 2
                G["bcb_rd"][parB] = last_db
                G["bcc_free"][parC] = y_last[NDC - 1]
                if pe_next is not None:
                    emit_bc_ev(n + 1, pe_next)

            emit_bc_ev(0, emit_bc_pe(0))
            for n in range(8):
                if tok_xn_next is not None and n % 2 == 0:
                    tok_xn_next[n // 2] = emit_xload_tc(
                        ti + 1, n // 2, tok_xn_next.get(n // 2 - 1))
                scan_group(n)
                for f in range(4 * n, 4 * n + 4):
                    if ti == 0:
                        emit_deq1(f)
                    emit_fc1(f)
                    if ti == 0 and f + 3 < NFB:
                        emit_deq1_dma(f + 3)
            if ti == 0:
                G["pf_free"] = [list(t) for t in deq_pf_free]
                deq["wb2_par"] = list(deq["wb_par"])
                emit_deq2_dma(0)
                emit_deq2_dma(1)
                emit_deq2_dma(2)

            for dc in range(6 if ti != 0 else 4):
                if ti != 0 and dc < 2:
                    dap = (P1 if dc == 0 else P2)[:, 0:512]
                    w = [dlt_toks[dc], t_a, ev_par[dc]]
                    sl = None
                else:
                    sl = dc % 4
                    dap = sDA[:, sl * T:(sl + 1) * T]
                    w = [dlt_toks[dc], t_a, G["da_free"][sl]]
                tok = p.op("scalar",
                           lambda e, d=dc, a=dap: e.activation(
                               a, sDLT[:, d * T:(d + 1) * T],
                               AF.Exp, scale=sA[:, d * 16 + 8: d * 16 + 9]),
                           waits=w, inc=S_ACT)
                pre_da[dc] = (tok, dap, sl)

            def emit_silus(j):
                for fb in range(8 * j, 8 * j + 8):
                    hid_toks[fb] = p.op("scalar",
                                        lambda e, f=fb: e.activation(
                                            sHID[:, f * T:(f + 1) * T],
                                            sHID[:, f * T:(f + 1) * T], AF.Silu),
                                        waits=[hid_toks[fb]], inc=S_ACT)

            G["p12_act"] = [ev_par[0], ev_par[1]]

            f2mm_rot = [None] * 4
            f2mm = {0: {}, 1: {}}
            f2_last_mm = None
            t3a_early = {}

            def emit_fc2_group(tcp, fb):
                nonlocal f2_last_mm
                rot = fb % 4
                if tcp == 0 and ti == 0:
                    wsrc = sWO
                    wbase = (fb % 4) * 1024
                    wgate = [deq["stt_fb"][(1, fb)]]
                else:
                    if f2mm_rot[rot] is not None:
                        gate = f2mm_rot[rot]
                    elif rot < 2:
                        gate = fc1mm_par[rot]
                    else:
                        gate = G["w_free"][rot]
                    wsrc = sW
                    wbase = rot * 1024
                    wgate = [p.op("sync",
                                  lambda e, f=fb, pr=rot: e.dma_start(
                                      out=sW[:, pr * 1024:(pr + 1) * 1024],
                                      in_=w2bf[f, :, :]),
                                  waits=[gate, wb2_done[fb]], inc=S_W2[rot],
                                  amt=16, dma=True)]
                for i2 in range(2):
                    tc = tcp * 2 + i2
                    for db in range(2):
                        pi = i2 * 2 + db
                        if fb == 0:
                            pfg = G["pf_free"][pi] if tcp == 0 else [t3a_early[pi]]
                        else:
                            pfg = []
                        tok = p.op("tensor",
                                   lambda e, f=fb, t_=tc, d2=db, p_=pi, W=wsrc,
                                   wb=wbase: e.matmul(
                                       PF[p_][:, 0:512],
                                       sHID[:, f * T + t_ * 128: f * T + (t_ + 1) * 128],
                                       W[:, wb + d2 * 512: wb + (d2 + 1) * 512],
                                       start=(f == 0), stop=(f == NFB - 1)),
                                   waits=wgate + [hid_toks[fb]] + pfg, inc=S_PE)
                        f2mm[tcp][pi] = tok
                        f2_last_mm = tok
                if tcp == 0 and ti == 0:
                    G["wo_rd"][fb % 4] = tok
                else:
                    f2mm_rot[rot] = tok

            for j in range(4):
                emit_silus(j)
            for n in range(8, 12):
                scan_group(n)
                for fb in range(8 * (n - 8), 8 * (n - 8) + 8):
                    if ti == 0:
                        emit_deq2(fb, G["p12_act"] if fb < 2 else [])
                    emit_fc2_group(0, fb)
                    if ti == 0 and fb + 3 < NFB:
                        emit_deq2_dma(fb + 3)
                if PRN is not None:
                    j = n - 8
                    if j == 0:
                        emit_prolog_piece(PRN, ti + 1, 0, tok_xn_next, [0, 1])
                    elif j == 1:
                        emit_prolog_piece(PRN, ti + 1, 0, tok_xn_next, [2, 3])
                        emit_prolog_piece(PRN, ti + 1, 1, tok_xn_next)
                    elif j == 2:
                        emit_prolog_piece(PRN, ti + 1, 2, tok_xn_next, [0, 1])
                    else:
                        emit_prolog_piece(PRN, ti + 1, 2, tok_xn_next, [2, 3])
                        emit_prolog_piece(PRN, ti + 1, 3, tok_xn_next)
            if ti == 0:
                late_free = [deq["stt_h"][0], deq["stt_h"][1],
                             wb2_done[NFB - 4], wb2_done[NFB - 3],
                             wb2_done[NFB - 2], wb2_done[NFB - 1]] + \
                    list(G["wo_rd"])
                G["t3_free"] = [list(late_free) for _ in range(4)]
                G["gh_free"] = [list(late_free), list(late_free)]
                G["cw_free"] = list(late_free)
                G["tail_gate"] = list(late_free)
                G["p12_mm_gate"] = list(late_free)
                G["p12_act"] = G["p12_act"] + [deq["stt_h"][0], deq["stt_h"][1]]

            for pi in range(4):
                db = pi % 2
                t3a_early[pi] = p.op("vector",
                                     lambda e, p_=pi, d2=db: e.scalar_tensor_tensor(
                                         sT3[:, p_ * T:(p_ + 1) * T], PF[p_][:, 0:512], 0.5,
                                         sB2B[:, d2 * 512:(d2 + 1) * 512],
                                         ALU.mult, ALU.add),
                                     waits=[f2mm[0][pi], G["t3_free"][pi], tk_b2],
                                     inc=S_DVE)

            for n in range(12, NST):
                scan_group(n)
                for fb in range(8 * (n - 12), 8 * (n - 12) + 8):
                    emit_fc2_group(1, fb)
                if PRN is not None and n - 8 < 6:
                    emit_prolog_piece(PRN, ti + 1, n - 8, tok_xn_next)
            if PRN is not None:
                emit_prolog_piece(PRN, ti + 1, 6, tok_xn_next)
                emit_prolog_piece(PRN, ti + 1, 7, tok_xn_next)

            comb_prev = None
            trt_last = None
            out_prev = G["out_gate"]
            syt_free = G["syt_last"]
            yt_free = [None, None]
            t3slot_g = [None] * 4
            if PRN is not None:
                p12_gate = [PRN.dlt[NDC - 2], PRN.dlt[NDC - 1]]
                out_wr_gate = [PRN.dlt[NDC - 2], PRN.dlt[NDC - 1]]
            else:
                p12_gate = [ev_par[0], ev_par[1]]
                out_wr_gate = [None, None]

            def emit_xc(tc, gate):
                return p.op("sync",
                            lambda e, t_=tc, r=r0: e.dma_start(
                                out=sXC[:, (t_ % 2) * D:(t_ % 2 + 1) * D],
                                in_=dram["xT"][r + t_ * 128: r + (t_ + 1) * 128, :]),
                            waits=[gate, G["tail_gate"]], inc=S_XC[tc % 2],
                            amt=16, dma=True)

            xc_toks = {0: emit_xc(0, G["xc_free"][0]),
                       1: emit_xc(1, G["xc_free"][1])}
            for tcp in range(2):
                if tcp == 1:
                    for pi in range(4):
                        db = pi % 2
                        t3a_early[pi] = p.op("vector",
                                             lambda e, p_=pi, d2=db: e.scalar_tensor_tensor(
                                                 sT3[:, p_ * T:(p_ + 1) * T],
                                                 PF[p_][:, 0:512], 0.5,
                                                 sB2B[:, d2 * 512:(d2 + 1) * 512],
                                                 ALU.mult, ALU.add),
                                             waits=[f2mm[1][pi], t3slot_g[pi], tk_b2],
                                             inc=S_DVE)
                        G["pf_free"][pi] = [t3a_early[pi]]
                for i2 in range(2):
                    tc = tcp * 2 + i2
                    xb = (tc % 2) * D
                    tok_xc = xc_toks[tc]
                    this_out_gate = out_prev
                    halves = []
                    for db in range(2):
                        pi = i2 * 2 + db
                        PY = P1 if db == 0 else P2
                        t_xn2 = p.op("vector",
                                     lambda e, t_=tc, d2=db, x0=xb, pp=ti % 2: e.tensor_scalar(
                                         sCW[:], sXC[:, x0 + d2 * 512:x0 + (d2 + 1) * 512],
                                         sSTAT[:, 64 + 8 * pp + 2 * t_: 65 + 8 * pp + 2 * t_],
                                         sSTAT[:, 65 + 8 * pp + 2 * t_: 66 + 8 * pp + 2 * t_],
                                         ALU.subtract, ALU.mult),
                                     waits=[tok_xc, PRO.musave[tc], G["cw_free"]], inc=S_DVE)
                        t_t3 = p.op("vector",
                                    lambda e, p_=pi: e.tensor_tensor(
                                        sT3[:, p_ * T:(p_ + 1) * T],
                                        sT3[:, p_ * T:(p_ + 1) * T], sCW[:], ALU.add),
                                    waits=[t3a_early[pi], t_xn2], inc=S_DVE)
                        t_g = p.op("scalar",
                                   lambda e, pr=db, p_=pi: e.activation(
                                       sGH[:, pr * T:(pr + 1) * T],
                                       sT3[:, p_ * T:(p_ + 1) * T], AF.Silu),
                                   waits=[t_t3, G["gh_free"][db]], inc=S_ACT)
                        t3slot_g[pi] = t_g
                        G["t3_free"][pi] = t_g
                        trt = []
                        for j in range(4):
                            dc = db * 4 + j
                            tok = p.op("tensor",
                                       lambda e, d=dc, t_=tc, jj=j, PYl=PY: e.transpose(
                                           PYl[:, jj * 128:(jj + 1) * 128],
                                           sY[:, d * T + t_ * 128: d * T + (t_ + 1) * 128],
                                           sIDF[:]),
                                       waits=[y_last[dc], yt_free[db],
                                              p12_gate[db]], inc=S_PE)
                            trt.append(tok)
                        trt_last = trt[-1]
                        t_yt = p.op("vector",
                                    lambda e, PYl=PY: e.tensor_copy(sYT[:], PYl[:, 0:512]),
                                    waits=trt + [G["tail_gate"], syt_free],
                                    inc=S_DVE)
                        yt_free[db] = t_yt
                        t_cw = p.op("vector",
                                    lambda e, d2=db: e.tensor_tensor(
                                        sCW[:], sCW[:],
                                        sDB[:, d2 * 512:(d2 + 1) * 512], ALU.mult),
                                    waits=[t_t3, t_yt, tk_db], inc=S_DVE)
                        t_s = p.op("vector",
                                   lambda e: e.tensor_tensor(sYT[:], sYT[:], sCW[:], ALU.add),
                                   waits=[t_cw], inc=S_DVE)
                        G["cw_free"] = t_s
                        t_m = p.op("vector",
                                   lambda e, pr=db: e.tensor_tensor(
                                       sYT[:], sYT[:], sGH[:, pr * T:(pr + 1) * T], ALU.mult),
                                   waits=[t_s, t_g], inc=S_DVE)
                        G["gh_free"][db] = t_m
                        t_o = p.op("vector",
                                   lambda e, d2=db, x0=xb: e.tensor_tensor(
                                       sOUT[:, d2 * 512:(d2 + 1) * 512], sYT[:],
                                       sXC[:, x0 + d2 * 512:x0 + (d2 + 1) * 512], ALU.add),
                                   waits=[t_m, tok_xc, this_out_gate,
                                          out_wr_gate[db]], inc=S_DVE)
                        comb_prev = t_o
                        syt_free = t_o
                        halves.append(t_o)
                    if tc + 2 < NTC:
                        xc_toks[tc + 2] = emit_xc(tc + 2, halves[-1])
                    G["xc_free"][tc % 2] = halves[-1]
                    t_store = p.op("sync",
                                   lambda e, t_=tc, r=r0: e.dma_start(
                                       out=outT[r + t_ * 128: r + (t_ + 1) * 128, :],
                                       in_=sOUT[:]),
                                   waits=halves, inc=S_OUT, amt=16, dma=True)
                    out_prev = t_store
            G["out_gate"] = out_prev
            G["syt_last"] = syt_free
            G["musave_rd"][ti % 2] = comb_prev
            G["ylast_pe"] = trt_last
            G["y_final"] = dict(y_last)
            G["hid_pe"] = f2_last_mm
            G["h2l_pe"] = fc1mm_par[1]
            G["w_free"] = list(f2mm_rot)
            if PRN is not None:
                adopt(PRN)
                PRO = PRN

        p.op("sync", lambda e: e.nop(), waits=[(S_OUT, S_OUT.n)])

        with nc.Block() as block:
            p.emit(block)

    return nc


def _host_stage(L0_attn_x, L0_attn_A_log, L0_attn_D, L0_attn_x_proj_w,
                L0_attn_dt_proj_w, L0_attn_dt_proj_b, L0_ffn_ln_gamma,
                L0_ffn_ln_beta, L0_ffn_fc1_q, L0_ffn_fc1_scales, L0_ffn_fc1_bias,
                L0_ffn_fc2_q, L0_ffn_fc2_scales, L0_ffn_fc2_bias):
    x = np.ascontiguousarray(np.asarray(L0_attn_x, dtype=np.float32))
    assert x.shape == (16, SEQ, D)

    A_c = np.ascontiguousarray(
        np.asarray(L0_attn_A_log, np.float32).reshape(NDC, 128, NST)
        .transpose(1, 0, 2).reshape(128, 128))
    xpwT = np.asarray(L0_attn_x_proj_w, np.float32).T
    xpwT_c = np.ascontiguousarray(
        xpwT.reshape(NDC, 128, 96).transpose(1, 0, 2).reshape(128, NDC * 96))
    dtwT = np.ascontiguousarray(np.asarray(L0_attn_dt_proj_w, np.float32).T)
    dtb_c = np.ascontiguousarray(np.asarray(L0_attn_dt_proj_b, np.float32).reshape(NDC, 128).T)
    fc1b_c = np.ascontiguousarray(np.asarray(L0_ffn_fc1_bias, np.float32).reshape(NFB, 128).T)
    beta_c = np.ascontiguousarray(np.asarray(L0_ffn_ln_beta, np.float32).reshape(NDC, 128).T)
    rows4 = np.ascontiguousarray(np.stack([
        np.asarray(L0_ffn_ln_gamma, np.float32),
        np.asarray(L0_ffn_ln_beta, np.float32),
        np.asarray(L0_attn_D, np.float32),
        np.asarray(L0_ffn_fc2_bias, np.float32)], axis=0))
    q1blk = np.ascontiguousarray(
        np.asarray(L0_ffn_fc1_q, np.int32).reshape(NDC, 128, NFB, 128).transpose(2, 0, 1, 3))
    sc1r = np.ascontiguousarray(
        np.asarray(L0_ffn_fc1_scales, np.float32).reshape(NDC, NFB, 128)
        .transpose(1, 0, 2).reshape(NFB, NDC * 128))
    q2blk = np.ascontiguousarray(np.asarray(L0_ffn_fc2_q, np.int32).reshape(NFB, 128, D))
    sc2r = np.ascontiguousarray(np.asarray(L0_ffn_fc2_scales, np.float32))
    idf = np.eye(128, dtype=np.float32)
    id2b = np.eye(128).astype(BF)
    onesf = np.ones((1, 128), dtype=np.float32)
    cvec = np.ascontiguousarray(np.stack([np.full(128, EPS, np.float32), np.ones(128, np.float32)], axis=1))
    glane = np.ascontiguousarray(
        np.asarray(L0_ffn_ln_gamma, np.float32).reshape(NDC, 128).T)
    selb = np.zeros((32, 32 * 128), dtype=ml_dtypes.float8_e4m3)
    for k in range(32):
        selb[k, k * 128:(k + 1) * 128] = ml_dtypes.float8_e4m3(1.0)

    shared = dict(A_c=A_c, xpwT_c=xpwT_c, dtwT=dtwT, dtb_c=dtb_c, fc1b_c=fc1b_c,
                  beta_c=beta_c,
                  rows4=rows4, q1blk=q1blk, sc1r=sc1r, q2blk=q2blk, sc2r=sc2r,
                  idf=idf, id2b=id2b, onesf=onesf, selb=selb, cvec=cvec,
                  glane=glane)
    return shared, x


def kernel(**inputs):
    shared, x = _host_stage(**inputs)
    nc = build_nc(NT)
    in_maps = []
    for c in range(NCORE):
        m = dict(shared)
        m["xT"] = np.ascontiguousarray(x[2 * c:2 * c + 2].reshape(TPC, D))
        in_maps.append(m)

    res = run_bass_kernel_spmd(nc, in_maps, core_ids=list(range(NCORE)))
    out = np.empty((16, SEQ, D), dtype=np.float32)
    for c in range(NCORE):
        out[2 * c:2 * c + 2] = np.asarray(res.results[c]["outT"]).reshape(2, SEQ, D)
    return out


# version 41
# speedup vs baseline: 1.2177x; 1.0045x over previous
import numpy as np
import ml_dtypes
import contextlib
import concourse.bass as bass
import concourse.mybir as mybir
from concourse.bass import AP
from concourse.bass_utils import run_bass_kernel_spmd

F32 = mybir.dt.float32
FP8 = mybir.dt.float8e4
BF16 = mybir.dt.bfloat16
I32 = mybir.dt.int32
ALU = mybir.AluOpType
AF = mybir.ActivationFunctionType
BF = ml_dtypes.bfloat16

D = 1024
NST = 16
RK = 64
FF = 4096
NCORE = 8
SEQ = 2048
TPC = 2 * SEQ
T = 512
NT = TPC // T
NTC = T // 128
NDC = D // 128
NFB = FF // 128
EPS = 1e-5

ENGS = ("sync", "tensor", "scalar", "vector", "gpsimd")


class Sem:
    __slots__ = ("h", "n", "name")

    def __init__(self, h, name):
        self.h = h
        self.n = 0
        self.name = name


class Planner:

    def __init__(self):
        self.q = {e: [] for e in ENGS}

    def op(self, eng, fn, waits=(), inc=None, amt=1, dma=False):
        ws = []
        for w in waits:
            if w is None:
                continue
            if isinstance(w, list):
                ws.extend(t for t in w if t is not None)
            else:
                ws.append(w)
        tgt = None
        if inc is not None:
            inc.n += amt
            tgt = inc.n
        self.q[eng].append((fn, tuple(ws), inc, tgt, amt, dma))
        return (inc, tgt) if inc is not None else None

    SELF_SER = False

    def emit(self, block):
        for ename in ENGS:
            oplist = self.q[ename]
            deco = getattr(block, ename)

            def section(eng, _ops=oplist):
                marks = {}
                prev_tok = None
                for fn, waits, inc, tgt, amt, dma in _ops:
                    allw = list(waits)
                    if Planner.SELF_SER and not dma and prev_tok is not None:
                        allw.append(prev_tok)
                    for (sem, val) in allw:
                        if marks.get(id(sem), -1) >= val:
                            continue
                        eng.wait_ge(sem.h, val)
                        marks[id(sem)] = val
                    instr = fn(eng)
                    if inc is not None:
                        instr.then_inc(inc.h, amt)
                        if not dma:
                            prev_tok = (inc, tgt)

            deco(section)


def _stat_save_src(sSTAT, c0):
    ap = sSTAT[:]
    return AP(ap.tensor, ap.offset + c0 + 2, [[ap.ap[0][0], 128], [4, 2]])


def _stat_save_dst(sSTAT, tc, par):
    ap = sSTAT[:]
    return AP(ap.tensor, ap.offset + 64 + 8 * par + 2 * tc, [[ap.ap[0][0], 128], [1, 2]])


def _state_pair_src(sH, parH):
    ap = sH[:]
    return AP(ap.tensor, ap.offset + (parH - 1) * T + (T - 1),
              [[ap.ap[0][0], 128], [T, 2]])


def _state_pair_dst(sSTATE, pair):
    ap = sSTATE[:]
    return AP(ap.tensor, ap.offset + (pair - NST), [[ap.ap[0][0], 128], [NST, 2]])


def lane_ap(tile, half, tc):
    ap = tile[:]
    base = (half * 4) * T + tc * 128
    return AP(ap.tensor, ap.offset + base, [[ap.ap[0][0], 128], [T, 4], [1, 128]])


def build_nc(nt):
    nc = bass.Bass("TRN2", target_bir_lowering=False)
    dram = {}

    def din(name, shape, dt):
        dram[name] = nc.dram_tensor(name, shape, dt, kind="ExternalInput").ap()

    din("xT", [TPC, D], F32)
    din("A_c", [128, 128], F32)
    din("xpwT_c", [128, NDC * 96], F32)
    din("dtwT", [RK, D], F32)
    din("dtb_c", [128, NDC], F32)
    din("fc1b_c", [128, NFB], F32)
    din("beta_c", [128, NDC], F32)
    din("rows4", [4, D], F32)
    din("q1blk", [NFB, NDC, 128, 128], I32)
    din("sc1r", [NFB, NDC * 128], F32)
    din("q2blk", [NFB, 128, D], I32)
    din("sc2r", [NFB, D], F32)
    din("idf", [128, 128], F32)
    din("id2b", [128, 128], BF16)
    din("glane", [128, NDC], F32)
    din("onesf", [1, 128], F32)
    din("cvec", [128, 2], F32)
    din("selb", [32, 32 * 128], FP8)

    outT = nc.dram_tensor("outT", [TPC, D], F32, kind="ExternalOutput").ap()
    w1bf = nc.dram_tensor("w1bf", [NFB, 128, NDC * 128], BF16, kind="Internal").ap()
    w2bf = nc.dram_tensor("w2bf", [NFB, 128, D], BF16, kind="Internal").ap()

    es = contextlib.ExitStack()
    with es:
        def sbuf(name, shape, dt):
            return es.enter_context(nc.sbuf_tensor(name, shape, dt))

        def psum(name, shape, dt):
            return es.enter_context(nc.psum_tensor(name, shape, dt))

        sA = sbuf("sA", [128, 128], F32)
        sXPW = sbuf("sXPW", [128, NDC * 96], BF16)
        sDTW = sbuf("sDTW", [RK, D], BF16)
        sDTB = sbuf("sDTB", [128, NDC], F32)
        sFC1B = sbuf("sFC1B", [128, NFB], F32)
        sBETA = sbuf("sBETA", [128, NDC], BF16)
        sDB = sbuf("sDB", [128, D], F32)
        sB2B = sbuf("sB2B", [128, D], F32)
        sIDF = sbuf("sIDF", [128, 128], F32)
        sID2 = sbuf("sID2", [128, 128], BF16)
        sONESF = sbuf("sONESF", [1, 128], F32)
        sCV = sbuf("sCV", [128, 2], F32)
        sGL = sbuf("sGL", [128, NDC], F32)
        sSEL = sbuf("sSEL", [32, 32 * 128], FP8)

        pro = contextlib.ExitStack()
        sCLD = pro.enter_context(nc.sbuf_tensor("sCLD", [128, 2048], F32))
        sROWP = pro.enter_context(nc.sbuf_tensor("sROWP", [1, 1024], F32))
        sSCF = pro.enter_context(nc.sbuf_tensor("sSCF", [32, 1024], F32))
        sWO = pro.enter_context(nc.sbuf_tensor("sWO", [128, 4 * 1024], BF16))
        sSC1 = pro.enter_context(nc.sbuf_tensor("sSC1", [NFB, NDC * 128], BF16))
        sSC2 = pro.enter_context(nc.sbuf_tensor("sSC2", [NFB, D], BF16))
        sQ = pro.enter_context(nc.sbuf_tensor("sQ", [128, 3 * 1024], I32))
        pro.close()
        sY = sbuf("sY", [128, NDC * T], F32)
        sT3 = sbuf("sT3", [128, 4 * T], F32)
        sGH = sbuf("sGH", [128, 2 * T], F32)
        sYT = sbuf("sYT", [128, T], F32)
        sCW = sbuf("sCW", [128, T], F32)
        sXC = sbuf("sXC", [128, 2 * D], F32)

        sXT = sbuf("sXT", [128, NTC * D], F32)
        sXNT = sbuf("sXNT", [128, NTC * D], F32)
        sSTAT = sbuf("sSTAT", [128, 80], F32)
        sXNBL = sbuf("sXNBL", [128, NDC * T], BF16)
        sH2T = sbuf("sH2T", [128, 2 * D], BF16)
        sH2L = sbuf("sH2L", [128, NDC * T], BF16)
        sDRK = sbuf("sDRK", [RK, T], BF16)
        sBCr = sbuf("sBCr", [32, T], BF16)
        sDLT = sbuf("sDLT", [128, NDC * T], F32)
        sUP = sbuf("sUP", [128, NDC * T], BF16)
        sDA = sbuf("sDA", [128, 5 * T], F32)
        sDBU = sbuf("sDBU", [128, 4 * T], BF16)
        sH = sbuf("sH", [128, 2 * T], BF16)
        sTMP = sbuf("sTMP", [128, 4 * T], BF16)
        sBCB = sbuf("sBCB", [128, 2 * T], BF16)
        sBCC = sbuf("sBCC", [128, 2 * T], BF16)
        sSTATE = sbuf("sSTATE", [128, 128], F32)
        sW = sbuf("sW", [128, 4 * 1024], BF16)
        sHID = sbuf("sHID", [128, NFB * T], BF16)
        sOUT = sbuf("sOUT", [128, D], F32)

        P1 = psum("P1", [128, 512], F32)
        P2 = psum("P2", [128, 512], F32)
        PB = psum("PB", [128, 512], F32)
        PC = psum("PC", [128, 512], F32)
        PF = [psum(f"PF{i}", [128, 512], F32) for i in range(4)]

        names = ["C", "SC", "ROW", "Qa", "Qb", "Qc", "WBa", "WBb", "WBc", "WBd",
                 "W1a", "W1b",
                 "W2a", "W2b", "W2c", "W2d", "X", "XCa", "XCb", "OUT", "PE",
                 "ACT", "DVE", "PL"] + [f"YA{d}" for d in range(8)]
        sems = {k: Sem(es.enter_context(nc.semaphore(k)), k) for k in names}
        S_C, S_SC, S_ROW, S_X, S_OUT, S_PE, S_ACT, S_DVE, S_PL = (
            sems[k] for k in ["C", "SC", "ROW", "X", "OUT", "PE", "ACT", "DVE", "PL"])
        S_Q = [sems["Qa"], sems["Qb"], sems["Qc"]]
        S_XC = [sems["XCa"], sems["XCb"]]
        S_YA = [sems[f"YA{d}"] for d in range(8)]
        S_WB = [sems["WBa"], sems["WBb"], sems["WBc"], sems["WBd"]]
        S_W1 = [sems["W1a"], sems["W1b"]]
        S_W2 = [sems["W2a"], sems["W2b"], sems["W2c"], sems["W2d"]]

        p = Planner()

        for nm, dst in [("A_c", sCLD[:, 0:128]), ("xpwT_c", sCLD[:, 128:128 + NDC * 96]),
                        ("dtwT", sCLD[0:RK, 1024:2048]), ("dtb_c", sDTB[:]),
                        ("fc1b_c", sFC1B[:]), ("beta_c", sCLD[0:128, 896:904]),
                        ("idf", sIDF[:]), ("id2b", sID2[:]), ("onesf", sONESF[:]),
                        ("cvec", sCV[:]), ("glane", sGL[:]), ("selb", sSEL[:])]:
            p.op("sync", lambda e, d=dst, n=nm: e.dma_start(out=d, in_=dram[n][:]),
                 inc=S_C, amt=16, dma=True)
        tok_consts = (S_C, S_C.n)
        tk_A = tk_xpw = tk_dtw = tk_dtb = tk_f1b = tk_idf = tk_onef = tk_cv = tk_sel = tok_consts
        tk_s1 = p.op("sync", lambda e: e.dma_start(out=sSCF[:], in_=dram["sc1r"][:]), inc=S_SC, amt=16, dma=True)

        t_ae = p.op("scalar", lambda e: e.activation(sA[:], sCLD[:, 0:128], AF.Exp), waits=[tk_A], inc=S_ACT)
        t_a = p.op("vector", lambda e: e.tensor_scalar(sA[:], sA[:], -1.0, None, ALU.mult), waits=[t_ae], inc=S_DVE)
        t_xpwb = p.op("vector", lambda e: e.tensor_copy(sXPW[:], sCLD[:, 128:128 + NDC * 96]), waits=[tk_xpw], inc=S_DVE)
        t_beta = p.op("vector", lambda e: e.tensor_copy(sBETA[:], sCLD[0:128, 896:904]), waits=[tok_consts], inc=S_DVE)
        t_dtwb = p.op("vector", lambda e: e.tensor_copy(sDTW[:], sCLD[0:RK, 1024:2048]), waits=[tk_dtw], inc=S_DVE)
        t_sc1b = p.op("vector", lambda e: e.tensor_copy(sSC1[:], sSCF[:]), waits=[tk_s1], inc=S_DVE)
        tk_s2 = p.op("sync", lambda e: e.dma_start(out=sSCF[:], in_=dram["sc2r"][:]), waits=[t_sc1b], inc=S_SC, amt=16, dma=True)
        t_sc2b = p.op("vector", lambda e: e.tensor_copy(sSC2[:], sSCF[:]), waits=[tk_s2], inc=S_DVE)

        prev_evs = []
        bc_done_dst = {}
        for idx, (dst, scl) in enumerate([(sDB, 1.0), (sB2B, 0.5)]):
            rowmap = [2, 3]
            tk_row = p.op("sync",
                          lambda e, i=rowmap[idx]: e.dma_start(out=sROWP[:], in_=dram["rows4"][i:i + 1, :]),
                          waits=prev_evs, inc=S_ROW, amt=16, dma=True)
            evs = []
            for h in range(2):
                PP = P1 if h == 0 else P2
                tm = p.op("tensor",
                          lambda e, hh=h, PPl=PP: e.matmul(
                              PPl[:, 0:512], sONESF[0:1, :],
                              sROWP[0:1, hh * 512:(hh + 1) * 512], start=True, stop=True),
                          waits=[tk_row, tk_onef] + prev_evs, inc=S_PE)
                te = p.op("scalar",
                          lambda e, d=dst, hh=h, s=scl, PPl=PP: e.activation(
                              d[:, hh * 512:(hh + 1) * 512], PPl[:, 0:512], AF.Copy, scale=s),
                          waits=[tm], inc=S_ACT)
                evs.append(te)
            prev_evs = list(evs)
            bc_done_dst[idx] = list(evs)
        bcast_done = evs[-1]
        tk_db, tk_b2 = (bc_done_dst[0], bc_done_dst[1])

        deq = dict(q_rd=[None] * 3, wb_par=[None] * 4,
                   stt_h=[None, None], pf_rd=[None] * 3, bfold_prev=None,
                   q1tok={}, q2tok={}, stt_fb={},
                   wb2_par=[None] * 4)
        wb1_done = {}
        wb2_done = {}
        bfold_toks = []
        deq_pf_free = [[], [], [], []]

        def emit_deq1_dma(fb):
            par = fb % 3
            qtoks = []
            for d2 in range(NDC):
                tok = p.op("sync",
                           lambda e, f=fb, d=d2, pr=par: e.dma_start(
                               out=sQ[:, pr * 1024 + d * 128: pr * 1024 + (d + 1) * 128],
                               in_=dram["q1blk"][f, d, :, :]),
                           waits=[deq["q_rd"][par]], inc=S_Q[par], amt=16,
                           dma=True)
                qtoks.append(tok)
            deq["q1tok"][fb] = qtoks[-1]

        def emit_deq1(fb):
            par = fb % 3
            s4 = fb % 4
            sts = []
            for h in range(2):
                PP = PF[h]
                tm = p.op("tensor",
                          lambda e, f=fb, hh=h, PPl=PP: e.matmul(
                              PPl[:, 0:512], sSEL[:, f * 128:(f + 1) * 128],
                              sSC1[:, hh * 512:(hh + 1) * 512], start=True, stop=True),
                          waits=[t_sc1b, tk_sel, deq["stt_h"][h]], inc=S_PE)
                ts_ = p.op("vector",
                           lambda e, hh=h, pr=par, w4=s4, PPl=PP: e.scalar_tensor_tensor(
                               sWO[:, w4 * 1024 + hh * 512: w4 * 1024 + (hh + 1) * 512],
                               sQ[:, pr * 1024 + hh * 512: pr * 1024 + (hh + 1) * 512],
                               128.0, PPl[:, 0:512], ALU.subtract, ALU.mult),
                           waits=[deq["q1tok"][fb], tm, deq["wb_par"][s4],
                                  G["wo_rd"][s4]],
                           inc=S_DVE)
                sts.append(ts_)
                deq["stt_h"][h] = ts_
                deq_pf_free[h] = [ts_]
            deq["q_rd"][par] = sts[-1]
            bmms = []
            for d2 in range(NDC):
                tok = p.op("tensor",
                           lambda e, d=d2, w4=s4: e.matmul(
                               PF[2][0:128, 0:1],
                               sWO[:, w4 * 1024 + d * 128: w4 * 1024 + (d + 1) * 128],
                               sBETA[:, d:d + 1], start=(d == 0), stop=(d == NDC - 1)),
                           waits=sts + [t_beta] +
                                 ([deq["bfold_prev"]] if deq["bfold_prev"] else []),
                           inc=S_PE)
                bmms.append(tok)
            bf = p.op("vector",
                      lambda e, f=fb: e.scalar_tensor_tensor(
                          sFC1B[:, f:f + 1], PF[2][0:128, 0:1], 1.0,
                          sFC1B[:, f:f + 1], ALU.mult, ALU.add),
                      waits=[bmms[-1], tok_consts], inc=S_DVE)
            deq["bfold_prev"] = bf
            bfold_toks.append(bf)
            deq["stt_fb"][fb] = list(sts)
            deq_pf_free[2] = [bf]
            tok = p.op("sync",
                       lambda e, f=fb, w4=s4: e.dma_start(
                           out=w1bf[f, :, :], in_=sWO[:, w4 * 1024:(w4 + 1) * 1024]),
                       waits=sts, inc=S_WB[s4], amt=16, dma=True)
            deq["wb_par"][s4] = tok
            wb1_done[fb] = tok

        def emit_deq2_dma(fb):
            par = fb % 3
            deq["q2tok"][fb] = p.op("sync",
                                    lambda e, f=fb, pr=par: e.dma_start(
                                        out=sQ[:, pr * 1024:(pr + 1) * 1024],
                                        in_=dram["q2blk"][f, :, :]),
                                    waits=[deq["q_rd"][par]],
                                    inc=S_Q[par], amt=16, dma=True)

        def emit_deq2(fb, p12_gate_first):
            par = fb % 3
            s4 = fb % 4
            tq = deq["q2tok"][fb]
            sts = []
            for h in range(2):
                PP = P1 if h == 0 else P2
                tm = p.op("tensor",
                          lambda e, f=fb, hh=h, PPl=PP: e.matmul(
                              PPl[:, 0:512], sSEL[:, f * 128:(f + 1) * 128],
                              sSC2[:, hh * 512:(hh + 1) * 512], start=True, stop=True),
                          waits=[t_sc2b, deq["stt_h"][h]] + p12_gate_first,
                          inc=S_PE)
                ts_ = p.op("vector",
                           lambda e, hh=h, pr=par, w4=s4, PPl=PP: e.scalar_tensor_tensor(
                               sWO[:, w4 * 1024 + hh * 512: w4 * 1024 + (hh + 1) * 512],
                               sQ[:, pr * 1024 + hh * 512: pr * 1024 + (hh + 1) * 512],
                               128.0, PPl[:, 0:512], ALU.subtract, ALU.mult),
                           waits=[tq, tm, deq["wb2_par"][s4], G["wo_rd"][s4]],
                           inc=S_DVE)
                sts.append(ts_)
                deq["stt_h"][h] = ts_
            deq["q_rd"][par] = sts[-1]
            deq["stt_fb"][(1, fb)] = list(sts)
            tok = p.op("sync",
                       lambda e, f=fb, w4=s4: e.dma_start(
                           out=w2bf[f, :, :], in_=sWO[:, w4 * 1024:(w4 + 1) * 1024]),
                       waits=sts, inc=S_WB[s4], amt=16, dma=True)
            deq["wb2_par"][s4] = tok
            wb2_done[fb] = tok

        stage_early = [t_ae, t_xpwb, t_dtwb, t_beta, t_sc2b, bcast_done]

        ROT = 5
        G = {
            "w_free": [None] * 4,
            "pf_free": [[], [], [], []],
            "da_free": [None] * ROT, "dbu_free": [None, None],
            "h_free": [None, None], "tmp_free": [None] * ROT,
            "bcb_rd": [None, None], "bcc_free": [None, None],
            "pb_free": None, "pc_free": None,
            "t3_free": [None] * 4, "gh_free": [None, None],
            "out_gate": None, "xc_free": [None, None], "cw_free": None,
            "tail_gate": None, "p12_mm_gate": None, "wo_rd": [None] * 4,
            "ylast_pe": None,
            "y_final": {dc: list(stage_early) for dc in range(NDC)},
            "hid_pe": None, "h2l_pe": None, "xnbl_pe": None, "up_dve": None,
            "stat_dve": None, "h2t_pe": [None, None],
            "xt_act": None, "xt_dve": None, "xnt_act": None,
            "p12_act": list(bc_done_dst[1]), "p12_pe": None,
            "musave_rd": {0: None, 1: None},
            "st_tok": {}, "syt_last": None,
        }

        def emit_xload_tc(ti, tc, chain=None):
            return p.op("sync",
                        lambda e, t_=tc, r=ti * T: e.dma_start(
                            out=sXT[:, t_ * D:(t_ + 1) * D],
                            in_=dram["xT"][r + t_ * 128: r + (t_ + 1) * 128, :]),
                        waits=[G["xt_act"], G["xt_dve"], chain], inc=S_X,
                        amt=16, dma=True)

        def emit_xload(ti):
            toks = {}
            for tc in range(NTC):
                toks[tc] = emit_xload_tc(ti, tc, toks.get(tc - 1))
            return toks

        class Prolog:
            pass

        def emit_prolog_piece(PR, ti, k, tok_x, tcs=None):
            if tcs is None:
                tcs = range(NTC)
            if k == 0:
                for tc in tcs:
                    xs = sXT[:, tc * D:(tc + 1) * D]
                    c0 = tc * 8
                    t_sx = p.op("scalar",
                                lambda e, a=xs, c=c0: e.activation(
                                    sOUT[:, 0:D], a, AF.Identity,
                                    accum_out=sSTAT[:, c:c + 1]),
                                waits=[tok_x[tc], G["stat_dve"], G["out_gate"]],
                                inc=S_ACT)
                    t_sq = p.op("scalar",
                                lambda e, a=xs, c=c0: e.activation(
                                    sOUT[:, 0:D], a, AF.Square,
                                    accum_out=sSTAT[:, c + 1:c + 2]),
                                waits=[tok_x[tc]], inc=S_ACT)
                    PR.sums1[tc] = (t_sx, t_sq)
            elif k == 1:
                rss = []
                for tc in range(NTC):
                    c0 = tc * 8
                    t_sx, t_sq = PR.sums1[tc]
                    t_mu = p.op("vector",
                                lambda e, c=c0: e.tensor_scalar(
                                    sSTAT[:, c + 2:c + 3], sSTAT[:, c:c + 1], 1.0 / D,
                                    None, ALU.mult),
                                waits=[t_sx], inc=S_DVE)
                    t_m2 = p.op("vector",
                                lambda e, c=c0: e.tensor_tensor(
                                    sSTAT[:, c + 3:c + 4], sSTAT[:, c + 2:c + 3],
                                    sSTAT[:, c + 2:c + 3], ALU.mult),
                                waits=[t_mu], inc=S_DVE)
                    t_var = p.op("vector",
                                 lambda e, c=c0: e.scalar_tensor_tensor(
                                     sSTAT[:, c + 4:c + 5], sSTAT[:, c + 1:c + 2], 1.0 / D,
                                     sSTAT[:, c + 3:c + 4], ALU.mult, ALU.subtract),
                                 waits=[t_sq, t_m2], inc=S_DVE)
                    PR.var1[tc] = t_var
                for tc in range(NTC):
                    c0 = tc * 8
                    t_ln = p.op("scalar",
                                lambda e, c=c0: e.activation(
                                    sSTAT[:, c + 5:c + 6], sSTAT[:, c + 4:c + 5],
                                    AF.Ln, bias=sCV[:, 0:1]),
                                waits=[PR.var1[tc], tk_cv], inc=S_ACT)
                    t_rs = p.op("scalar",
                                lambda e, c=c0: e.activation(
                                    sSTAT[:, c + 6:c + 7], sSTAT[:, c + 5:c + 6],
                                    AF.Exp, scale=-0.5),
                                waits=[t_ln], inc=S_ACT)
                    PR.rs1[tc] = t_rs
            elif k == 2:
                for tc in tcs:
                    c0 = tc * 8
                    c1 = 32 + tc * 8
                    t_xn = p.op("vector",
                                lambda e, t_=tc, c=c0: e.tensor_scalar(
                                    sXNT[:, t_ * D:(t_ + 1) * D], sXT[:, t_ * D:(t_ + 1) * D],
                                    sSTAT[:, c + 2:c + 3], sSTAT[:, c + 6:c + 7],
                                    ALU.subtract, ALU.mult),
                                waits=[PR.rs1[tc], G["xnt_act"]], inc=S_DVE)
                    PR.xn[tc] = t_xn
                    PR.musave[tc] = p.op("vector",
                                         lambda e, t_=tc, c=c0, pp=ti % 2: e.tensor_copy(
                                             _stat_save_dst(sSTAT, t_, pp),
                                             _stat_save_src(sSTAT, c)),
                                         waits=[PR.rs1[tc], G["musave_rd"][ti % 2]],
                                         inc=S_DVE)
                for tc in tcs:
                    c1 = 32 + tc * 8
                    xns = sXNT[:, tc * D:(tc + 1) * D]
                    t_sx2 = p.op("scalar",
                                 lambda e, a=xns, c=c1: e.activation(
                                     sOUT[:, 0:D], a, AF.Identity,
                                     accum_out=sSTAT[:, c:c + 1]),
                                 waits=[PR.xn[tc]], inc=S_ACT)
                    t_sq2 = p.op("scalar",
                                 lambda e, a=xns, c=c1: e.activation(
                                     sOUT[:, 0:D], a, AF.Square,
                                     accum_out=sSTAT[:, c + 1:c + 2]),
                                 waits=[PR.xn[tc]], inc=S_ACT)
                    PR.sums2[tc] = (t_sx2, t_sq2)
                if NTC - 1 not in tcs:
                    return
                PR.xt_act = PR.sums2[NTC - 1][1]
                PR.xt_dve = PR.xn[NTC - 1]
                PR.xnt_act = PR.sums2[NTC - 1][1]
            elif k == 3:
                for tc in range(NTC):
                    c1 = 32 + tc * 8
                    t_sx2, t_sq2 = PR.sums2[tc]
                    t_mu2 = p.op("vector",
                                 lambda e, c=c1: e.tensor_scalar(
                                     sSTAT[:, c + 2:c + 3], sSTAT[:, c:c + 1], 1.0 / D,
                                     None, ALU.mult),
                                 waits=[t_sx2], inc=S_DVE)
                    t_m22 = p.op("vector",
                                 lambda e, c=c1: e.tensor_tensor(
                                     sSTAT[:, c + 3:c + 4], sSTAT[:, c + 2:c + 3],
                                     sSTAT[:, c + 2:c + 3], ALU.mult),
                                 waits=[t_mu2], inc=S_DVE)
                    t_var2 = p.op("vector",
                                  lambda e, c=c1: e.scalar_tensor_tensor(
                                      sSTAT[:, c + 4:c + 5], sSTAT[:, c + 1:c + 2], 1.0 / D,
                                      sSTAT[:, c + 3:c + 4], ALU.mult, ALU.subtract),
                                  waits=[t_sq2, t_m22], inc=S_DVE)
                    PR.var2[tc] = t_var2
                for tc in range(NTC):
                    c1 = 32 + tc * 8
                    t_ln2 = p.op("scalar",
                                 lambda e, c=c1: e.activation(
                                     sSTAT[:, c + 5:c + 6], sSTAT[:, c + 4:c + 5],
                                     AF.Ln, bias=sCV[:, 0:1]),
                                 waits=[PR.var2[tc]], inc=S_ACT)
                    t_rs2 = p.op("scalar",
                                 lambda e, c=c1: e.activation(
                                     sSTAT[:, c + 6:c + 7], sSTAT[:, c + 5:c + 6],
                                     AF.Exp, scale=-0.5),
                                 waits=[t_ln2], inc=S_ACT)
                    PR.rs2[tc] = t_rs2
            elif k == 4 or k == 5:
                for tc in [2 * (k - 4), 2 * (k - 4) + 1]:
                    c1 = 32 + tc * 8
                    h2s = (tc % 2) * D
                    xns = sXNT[:, tc * D:(tc + 1) * D]
                    t_t2 = p.op("vector",
                                lambda e, a=xns, c=c1, s=h2s: e.tensor_scalar(
                                    sH2T[:, s:s + D], a, sSTAT[:, c + 2:c + 3],
                                    sSTAT[:, c + 6:c + 7], ALU.subtract, ALU.mult),
                                waits=[PR.rs2[tc], G["h2t_pe"][tc % 2],
                                       PR.h2l_ev[-1] if PR.h2l_ev else None], inc=S_DVE)
                    t_h2 = t_t2
                    if tc == NTC - 1:
                        PR.stat_dve = t_t2
                    for half in range(2):
                        trx, trh = [], []
                        for j in range(4):
                            dc = half * 4 + j
                            tok = p.op("tensor",
                                       lambda e, d=dc, t_=tc, jj=j: e.transpose(
                                           P1[:, jj * 128:(jj + 1) * 128],
                                           sXNT[:, t_ * D + d * 128: t_ * D + (d + 1) * 128],
                                           sIDF[:]),
                                       waits=[PR.xn[tc],
                                              PR.xnl_ev[-1] if PR.xnl_ev else None] +
                                             (G["p12_act"] if not PR.xnl_ev else []),
                                       inc=S_PE)
                            trx.append(tok)
                        te = p.op("scalar",
                                  lambda e, hh=half, t_=tc: e.activation(
                                      lane_ap(sXNBL, hh, t_), P1[:, 0:512], AF.Identity),
                                  waits=trx + [G["xnbl_pe"], G["up_dve"]], inc=S_ACT)
                        PR.xnl_ev.append(te)
                        for j in range(4):
                            dc = half * 4 + j
                            tok = p.op("tensor",
                                       lambda e, d=dc, jj=j, s=h2s: e.matmul(
                                           P2[:, jj * 128:(jj + 1) * 128],
                                           sH2T[:, s + d * 128: s + (d + 1) * 128],
                                           sID2[:], start=True, stop=True),
                                       waits=[t_h2,
                                              PR.h2l_ev[-1] if PR.h2l_ev else None] +
                                             (G["p12_act"] if not PR.h2l_ev else []),
                                       inc=S_PE)
                            trh.append(tok)
                        for j in range(4):
                            dc = half * 4 + j
                            te2 = p.op("scalar",
                                       lambda e, d=dc, jj=j, t_=tc: e.activation(
                                           sH2L[:, d * T + t_ * 128: d * T + (t_ + 1) * 128],
                                           P2[:, jj * 128:(jj + 1) * 128],
                                           AF.Identity, scale=sGL[:, d:d + 1]),
                                       waits=[trh[j], tok_consts, G["h2l_pe"]],
                                       inc=S_ACT)
                        PR.h2l_ev.append(te2)
                    G["h2t_pe"][tc % 2] = trh[-1]
            elif k == 6:
                drk_mm, bcr_mm = [], []
                for dc in range(NDC):
                    tok = p.op("tensor",
                               lambda e, d=dc: e.matmul(
                                   P1[0:64, 0:512], sXPW[:, d * 96: d * 96 + 64],
                                   sXNBL[:, d * T:(d + 1) * T],
                                   start=(d == 0), stop=(d == NDC - 1)),
                               waits=[PR.xnl_ev[-1], t_xpwb], inc=S_PE)
                    drk_mm.append(tok)
                for dc in range(NDC):
                    tok = p.op("tensor",
                               lambda e, d=dc: e.matmul(
                                   P2[0:32, 0:512], sXPW[:, d * 96 + 64: d * 96 + 96],
                                   sXNBL[:, d * T:(d + 1) * T],
                                   start=(d == 0), stop=(d == NDC - 1)),
                               waits=[PR.xnl_ev[-1], PR.h2l_ev[-1], t_xpwb], inc=S_PE)
                    bcr_mm.append(tok)
                PR.xnbl_pe = bcr_mm[-1]
                PR.t_drk = p.op("vector", lambda e: e.tensor_copy(sDRK[:], P1[0:64, 0:512]),
                                waits=[drk_mm[-1]], inc=S_DVE)
                PR.t_bcr = p.op("vector", lambda e: e.tensor_copy(sBCr[:], P2[0:32, 0:512]),
                                waits=[bcr_mm[-1]], inc=S_DVE)
            elif k == 7:
                sp_prev = [None, None]
                for dc in range(NDC):
                    par = dc % 2
                    PP = P1 if par == 0 else P2
                    t_mm = p.op("tensor",
                                lambda e, d=dc, PPl=PP: e.matmul(
                                    PPl[:, 0:512], sDTW[:, d * 128:(d + 1) * 128], sDRK[:],
                                    start=True, stop=True),
                                waits=[PR.t_drk, PR.t_bcr, t_dtwb, sp_prev[par],
                                       G["p12_mm_gate"]],
                                inc=S_PE)
                    t_e = p.op("scalar",
                               lambda e, d=dc, PPl=PP, pr=par: e.activation(
                                   sOUT[:, pr * 512:(pr + 1) * 512], PPl[:, 0:512],
                                   AF.Exp, bias=sDTB[:, d:d + 1]),
                               waits=[t_mm] + ([G["out_gate"]] if dc < 2 else []),
                               inc=S_ACT)
                    t_d = p.op("scalar",
                               lambda e, d=dc, pr=par: e.activation(
                                   sDLT[:, d * T:(d + 1) * T],
                                   sOUT[:, pr * 512:(pr + 1) * 512],
                                   AF.Ln, bias=sCV[:, 1:2]),
                               waits=[t_e, tk_cv, G["up_dve"]], inc=S_ACT)
                    sp_prev[par] = t_e
                    PR.dlt[dc] = t_d
                    PR.up[dc] = p.op("vector",
                                     lambda e, d=dc: e.tensor_tensor(
                                         sUP[:, d * T:(d + 1) * T],
                                         sDLT[:, d * T:(d + 1) * T],
                                         sXNBL[:, d * T:(d + 1) * T], ALU.mult),
                                     waits=[t_d, PR.xnl_ev[-1]], inc=S_DVE)
                PR.up_dve = PR.up[NDC - 1]
                PR.delta_mm_last = t_mm

        def new_prolog():
            PR = Prolog()
            PR.xn = {}
            PR.musave = {}
            PR.sums1 = {}
            PR.var1 = {}
            PR.rs1 = {}
            PR.sums2 = {}
            PR.var2 = {}
            PR.rs2 = {}
            PR.xnl_ev = []
            PR.h2l_ev = []
            PR.dlt = {}
            PR.up = {}
            return PR

        def adopt(PR):
            for k in ["xt_act", "xt_dve", "xnt_act", "stat_dve",
                      "xnbl_pe", "up_dve"]:
                G[k] = getattr(PR, k)

        tok_x0 = emit_xload(0)
        PRO = new_prolog()
        for k in range(8):
            emit_prolog_piece(PRO, 0, k, tok_x0)
        adopt(PRO)
        emit_deq1_dma(0)
        emit_deq1_dma(1)
        emit_deq1_dma(2)

        for ti in range(nt):
            r0 = ti * T
            init_zero = (r0 % SEQ) == 0
            dlt_toks = PRO.dlt
            up_toks = PRO.up
            h2l_last = PRO.h2l_ev[-1]
            t_bcr = PRO.t_bcr

            tok_xn_next = {} if ti + 1 < nt else None

            y_ms = {}
            for dc in range(NDC):
                y_ms[dc] = p.op("gpsimd",
                                lambda e, d=dc: e.memset(sY[:, d * T:(d + 1) * T], 0.0),
                                waits=[G["ylast_pe"], G["y_final"][dc]], inc=S_PL)

            y_last = dict(y_ms)
            scan_idx = 0
            hid_toks = {}
            ev_par = [None, None]
            fc1mm_par = [None, None]
            state_rd = None
            PRN = new_prolog() if ti + 1 < nt else None

            def emit_fc1(fb):
                par = fb % 2
                PP = P1 if par == 0 else P2
                if ti == 0:
                    wsrc = sWO
                    wbase = (fb % 4) * 1024
                    wgate = [deq["stt_fb"][fb]]
                else:
                    wsrc = sW
                    wbase = par * 1024
                    wgate = [p.op("sync",
                                  lambda e, f=fb, pr=par: e.dma_start(
                                      out=sW[:, pr * 1024:(pr + 1) * 1024],
                                      in_=w1bf[f, :, :]),
                                  waits=[fc1mm_par[par] if fc1mm_par[par]
                                         else G["w_free"][par], wb1_done[fb]],
                                  inc=S_W1[par], amt=16, dma=True)]
                mms = []
                for dc in range(NDC):
                    tok = p.op("tensor",
                               lambda e, d=dc, wb=wbase, W=wsrc, PPl=PP: e.matmul(
                                   PPl[:, 0:512],
                                   W[:, wb + d * 128: wb + (d + 1) * 128],
                                   sH2L[:, d * T:(d + 1) * T],
                                   start=(d == 0), stop=(d == NDC - 1)),
                               waits=wgate + [h2l_last, ev_par[par],
                                              dlt_toks[NDC - 1]], inc=S_PE)
                    mms.append(tok)
                fc1mm_par[par] = mms[-1]
                if ti == 0:
                    G["wo_rd"][fb % 4] = mms[-1]
                t_si = p.op("scalar",
                            lambda e, f=fb, PPl=PP: e.activation(
                                sHID[:, f * T:(f + 1) * T], PPl[:, 0:512],
                                AF.Identity, bias=sFC1B[:, f:f + 1]),
                            waits=[mms[-1], tk_f1b, G["hid_pe"], bfold_toks[fb]],
                            inc=S_ACT)
                ev_par[par] = t_si
                hid_toks[fb] = t_si

            bc_ev = {}
            pre_da = {}

            def emit_bc_pe(n):
                t_bcB = p.op("tensor",
                             lambda e, nn=n: e.matmul(
                                 PB[:, 0:512], sSEL[:, nn * 128:(nn + 1) * 128],
                                 sBCr[:], start=True, stop=True),
                             waits=[t_bcr, G["pb_free"]], inc=S_PE)
                t_bcC = p.op("tensor",
                             lambda e, nn=n: e.matmul(
                                 PC[:, 0:512], sSEL[:, (nn + 16) * 128:(nn + 17) * 128],
                                 sBCr[:], start=True, stop=True),
                             waits=[t_bcr, G["pc_free"]], inc=S_PE)
                return (t_bcB, t_bcC)

            def emit_bc_ev(n, pe_toks):
                par = n % 2
                t_evB = p.op("scalar",
                             lambda e, pr=par: e.activation(
                                 sBCB[:, pr * T:(pr + 1) * T], PB[:, 0:512], AF.Identity),
                             waits=[pe_toks[0], G["bcb_rd"][par]], inc=S_ACT)
                G["pb_free"] = t_evB
                t_evC = p.op("scalar",
                             lambda e, pr=par: e.activation(
                                 sBCC[:, pr * T:(pr + 1) * T], PC[:, 0:512], AF.Identity),
                             waits=[pe_toks[1], G["bcc_free"][par]], inc=S_ACT)
                G["pc_free"] = t_evC
                bc_ev[n] = (t_evB, t_evC)

            def scan_group(n):
                nonlocal scan_idx, state_rd
                parB = n % 2
                parC = n % 2
                t_evB, t_evC = bc_ev[n]
                pe_next = emit_bc_pe(n + 1) if n + 1 < NST else None
                last_db = None

                def rep2(tile, par):
                    ap = tile[:]
                    return AP(ap.tensor, ap.offset + par * T,
                              [[ap.ap[0][0], 128], [0, 2], [1, T]])

                for j in range(NDC // 2):
                    dc0, dc1 = 2 * j, 2 * j + 1
                    pair0 = dc0 * NST + n
                    pair1 = dc1 * NST + n
                    parA0 = scan_idx % ROT
                    parA1 = (scan_idx + 1) % ROT
                    pj = scan_idx // 2
                    parU = pj % 2
                    parM = pj % 2
                    if n == 8 and dc0 in pre_da:
                        t_da0, ap0, slot0 = pre_da[dc0]
                    else:
                        ap0 = sDA[:, parA0 * T:(parA0 + 1) * T]
                        slot0 = parA0
                        t_da0 = p.op("scalar",
                                     lambda e, d=dc0, nn=n, a=ap0: e.activation(
                                         a, sDLT[:, d * T:(d + 1) * T],
                                         AF.Exp, scale=sA[:, d * 16 + nn: d * 16 + nn + 1]),
                                     waits=[dlt_toks[dc0], t_a, G["da_free"][parA0]],
                                     inc=S_ACT)
                    if n == 8 and dc1 in pre_da:
                        t_da1, ap1, slot1 = pre_da[dc1]
                    else:
                        ap1 = sDA[:, parA1 * T:(parA1 + 1) * T]
                        slot1 = parA1
                        t_da1 = p.op("scalar",
                                     lambda e, d=dc1, nn=n, a=ap1: e.activation(
                                         a, sDLT[:, d * T:(d + 1) * T],
                                         AF.Exp, scale=sA[:, d * 16 + nn: d * 16 + nn + 1]),
                                     waits=[dlt_toks[dc1], t_a, G["da_free"][parA1]],
                                     inc=S_ACT)
                    t_db = p.op("vector",
                                lambda e, d=dc0, pu=parU, prB=parB: e.tensor_tensor(
                                    sDBU[:, pu * 2 * T:(pu + 1) * 2 * T],
                                    sUP[:, d * T:(d + 2) * T],
                                    rep2(sBCB, prB), ALU.mult),
                                waits=[up_toks[dc0], up_toks[dc1], t_evB,
                                       G["dbu_free"][parU]], inc=S_DVE)
                    last_db = t_db
                    init0 = 0.0 if init_zero else sSTATE[:, pair0:pair0 + 1]
                    init1 = 0.0 if init_zero else sSTATE[:, pair1:pair1 + 1]
                    t_sc0 = p.op("vector",
                                 lambda e, a=ap0, pu=parU, ii=init0: e.tensor_tensor_scan(
                                     sH[:, 0:T], a,
                                     sDBU[:, pu * 2 * T:pu * 2 * T + T],
                                     ii, ALU.mult, ALU.add),
                                 waits=[t_da0, t_db, G["h_free"][0]] +
                                       ([] if init_zero else [G["st_tok"].get(pair0)]),
                                 inc=S_DVE)
                    t_sc1 = p.op("vector",
                                 lambda e, a=ap1, pu=parU, ii=init1: e.tensor_tensor_scan(
                                     sH[:, T:2 * T], a,
                                     sDBU[:, pu * 2 * T + T:(pu + 1) * 2 * T],
                                     ii, ALU.mult, ALU.add),
                                 waits=[t_da1, G["h_free"][1]] +
                                       ([] if init_zero else [G["st_tok"].get(pair1)]),
                                 inc=S_DVE)
                    if slot0 is not None:
                        G["da_free"][slot0] = t_sc0
                    else:
                        G["p12_act"] = G["p12_act"] + [t_sc0]
                    if slot1 is not None:
                        G["da_free"][slot1] = t_sc1
                    else:
                        G["p12_act"] = G["p12_act"] + [t_sc1]
                    G["dbu_free"][parU] = t_sc1
                    t_st = p.op("vector",
                                lambda e, pp=pair1: e.tensor_copy(
                                    _state_pair_dst(sSTATE, pp),
                                    _state_pair_src(sH, 1)),
                                waits=[t_sc1], inc=S_DVE)
                    state_rd = t_st
                    G["st_tok"][pair0] = t_st
                    G["st_tok"][pair1] = t_st
                    t_tm = p.op("vector",
                                lambda e, pm=parM, prC=parC: e.tensor_tensor(
                                    sTMP[:, pm * 2 * T:(pm + 1) * 2 * T],
                                    sH[:, 0:2 * T],
                                    rep2(sBCC, prC), ALU.mult),
                                waits=[t_sc1, t_evC, G["tmp_free"][parM]],
                                inc=S_DVE)
                    G["h_free"][0] = t_tm
                    G["h_free"][1] = t_tm
                    t_y = p.op("gpsimd",
                               lambda e, d=dc0, pm=parM: e.dma_start(
                                   out=sY[:, d * T:(d + 2) * T],
                                   in_=sTMP[:, pm * 2 * T:(pm + 1) * 2 * T],
                                   accum_op=ALU.add),
                               waits=[t_tm, y_last[dc0], y_last[dc1]],
                               inc=S_YA[dc0], amt=16, dma=True)
                    G["tmp_free"][parM] = t_y
                    y_last[dc0] = t_y
                    y_last[dc1] = t_y
                    scan_idx += 2
                G["bcb_rd"][parB] = last_db
                G["bcc_free"][parC] = y_last[NDC - 1]
                if pe_next is not None:
                    emit_bc_ev(n + 1, pe_next)

            emit_bc_ev(0, emit_bc_pe(0))
            for n in range(8):
                if tok_xn_next is not None and n % 2 == 0:
                    tok_xn_next[n // 2] = emit_xload_tc(
                        ti + 1, n // 2, tok_xn_next.get(n // 2 - 1))
                scan_group(n)
                for f in range(4 * n, 4 * n + 4):
                    if ti == 0:
                        emit_deq1(f)
                    emit_fc1(f)
                    if ti == 0 and f + 3 < NFB:
                        emit_deq1_dma(f + 3)
            if ti == 0:
                G["pf_free"] = [list(t) for t in deq_pf_free]
                deq["wb2_par"] = list(deq["wb_par"])
                emit_deq2_dma(0)
                emit_deq2_dma(1)
                emit_deq2_dma(2)

            for dc in range(6 if ti != 0 else 4):
                if ti != 0 and dc < 2:
                    dap = (P1 if dc == 0 else P2)[:, 0:512]
                    w = [dlt_toks[dc], t_a, ev_par[dc]]
                    sl = None
                else:
                    sl = (64 + dc) % ROT
                    dap = sDA[:, sl * T:(sl + 1) * T]
                    w = [dlt_toks[dc], t_a, G["da_free"][sl]]
                tok = p.op("scalar",
                           lambda e, d=dc, a=dap: e.activation(
                               a, sDLT[:, d * T:(d + 1) * T],
                               AF.Exp, scale=sA[:, d * 16 + 8: d * 16 + 9]),
                           waits=w, inc=S_ACT)
                pre_da[dc] = (tok, dap, sl)

            def emit_silus(j):
                for fb in range(8 * j, 8 * j + 8):
                    hid_toks[fb] = p.op("scalar",
                                        lambda e, f=fb: e.activation(
                                            sHID[:, f * T:(f + 1) * T],
                                            sHID[:, f * T:(f + 1) * T], AF.Silu),
                                        waits=[hid_toks[fb]], inc=S_ACT)

            G["p12_act"] = [ev_par[0], ev_par[1]]

            f2mm_rot = [None] * 4
            f2mm = {0: {}, 1: {}}
            f2_last_mm = None
            t3a_early = {}

            def emit_fc2_group(tcp, fb):
                nonlocal f2_last_mm
                rot = fb % 4
                if tcp == 0 and ti == 0:
                    wsrc = sWO
                    wbase = (fb % 4) * 1024
                    wgate = [deq["stt_fb"][(1, fb)]]
                else:
                    if f2mm_rot[rot] is not None:
                        gate = f2mm_rot[rot]
                    elif rot < 2:
                        gate = fc1mm_par[rot]
                    else:
                        gate = G["w_free"][rot]
                    wsrc = sW
                    wbase = rot * 1024
                    wgate = [p.op("sync",
                                  lambda e, f=fb, pr=rot: e.dma_start(
                                      out=sW[:, pr * 1024:(pr + 1) * 1024],
                                      in_=w2bf[f, :, :]),
                                  waits=[gate, wb2_done[fb]], inc=S_W2[rot],
                                  amt=16, dma=True)]
                for i2 in range(2):
                    tc = tcp * 2 + i2
                    for db in range(2):
                        pi = i2 * 2 + db
                        if fb == 0:
                            pfg = G["pf_free"][pi] if tcp == 0 else [t3a_early[pi]]
                        else:
                            pfg = []
                        tok = p.op("tensor",
                                   lambda e, f=fb, t_=tc, d2=db, p_=pi, W=wsrc,
                                   wb=wbase: e.matmul(
                                       PF[p_][:, 0:512],
                                       sHID[:, f * T + t_ * 128: f * T + (t_ + 1) * 128],
                                       W[:, wb + d2 * 512: wb + (d2 + 1) * 512],
                                       start=(f == 0), stop=(f == NFB - 1)),
                                   waits=wgate + [hid_toks[fb]] + pfg, inc=S_PE)
                        f2mm[tcp][pi] = tok
                        f2_last_mm = tok
                if tcp == 0 and ti == 0:
                    G["wo_rd"][fb % 4] = tok
                else:
                    f2mm_rot[rot] = tok

            for j in range(4):
                emit_silus(j)
            for n in range(8, 12):
                scan_group(n)
                for fb in range(8 * (n - 8), 8 * (n - 8) + 8):
                    if ti == 0:
                        emit_deq2(fb, G["p12_act"] if fb < 2 else [])
                    emit_fc2_group(0, fb)
                    if ti == 0 and fb + 3 < NFB:
                        emit_deq2_dma(fb + 3)
                if PRN is not None:
                    j = n - 8
                    if j == 0:
                        emit_prolog_piece(PRN, ti + 1, 0, tok_xn_next, [0, 1])
                    elif j == 1:
                        emit_prolog_piece(PRN, ti + 1, 0, tok_xn_next, [2, 3])
                        emit_prolog_piece(PRN, ti + 1, 1, tok_xn_next)
                    elif j == 2:
                        emit_prolog_piece(PRN, ti + 1, 2, tok_xn_next, [0, 1])
                    else:
                        emit_prolog_piece(PRN, ti + 1, 2, tok_xn_next, [2, 3])
                        emit_prolog_piece(PRN, ti + 1, 3, tok_xn_next)
            if ti == 0:
                late_free = [deq["stt_h"][0], deq["stt_h"][1],
                             wb2_done[NFB - 4], wb2_done[NFB - 3],
                             wb2_done[NFB - 2], wb2_done[NFB - 1]] + \
                    list(G["wo_rd"])
                G["t3_free"] = [list(late_free) for _ in range(4)]
                G["gh_free"] = [list(late_free), list(late_free)]
                G["cw_free"] = list(late_free)
                G["tail_gate"] = list(late_free)
                G["p12_mm_gate"] = list(late_free)
                G["p12_act"] = G["p12_act"] + [deq["stt_h"][0], deq["stt_h"][1]]

            for pi in range(4):
                db = pi % 2
                t3a_early[pi] = p.op("vector",
                                     lambda e, p_=pi, d2=db: e.scalar_tensor_tensor(
                                         sT3[:, p_ * T:(p_ + 1) * T], PF[p_][:, 0:512], 0.5,
                                         sB2B[:, d2 * 512:(d2 + 1) * 512],
                                         ALU.mult, ALU.add),
                                     waits=[f2mm[0][pi], G["t3_free"][pi], tk_b2],
                                     inc=S_DVE)

            for n in range(12, NST):
                scan_group(n)
                for fb in range(8 * (n - 12), 8 * (n - 12) + 8):
                    emit_fc2_group(1, fb)
                if PRN is not None and n - 8 < 6:
                    emit_prolog_piece(PRN, ti + 1, n - 8, tok_xn_next)
            if PRN is not None:
                emit_prolog_piece(PRN, ti + 1, 6, tok_xn_next)
                emit_prolog_piece(PRN, ti + 1, 7, tok_xn_next)

            comb_prev = None
            trt_last = None
            out_prev = G["out_gate"]
            syt_free = G["syt_last"]
            yt_free = [None, None]
            t3slot_g = [None] * 4
            if PRN is not None:
                p12_gate = [PRN.dlt[NDC - 2], PRN.dlt[NDC - 1]]
                out_wr_gate = [PRN.dlt[NDC - 2], PRN.dlt[NDC - 1]]
            else:
                p12_gate = [ev_par[0], ev_par[1]]
                out_wr_gate = [None, None]

            def emit_xc(tc, gate):
                return p.op("sync",
                            lambda e, t_=tc, r=r0: e.dma_start(
                                out=sXC[:, (t_ % 2) * D:(t_ % 2 + 1) * D],
                                in_=dram["xT"][r + t_ * 128: r + (t_ + 1) * 128, :]),
                            waits=[gate, G["tail_gate"]], inc=S_XC[tc % 2],
                            amt=16, dma=True)

            xc_toks = {0: emit_xc(0, G["xc_free"][0]),
                       1: emit_xc(1, G["xc_free"][1])}
            for tcp in range(2):
                if tcp == 1:
                    for pi in range(4):
                        db = pi % 2
                        t3a_early[pi] = p.op("vector",
                                             lambda e, p_=pi, d2=db: e.scalar_tensor_tensor(
                                                 sT3[:, p_ * T:(p_ + 1) * T],
                                                 PF[p_][:, 0:512], 0.5,
                                                 sB2B[:, d2 * 512:(d2 + 1) * 512],
                                                 ALU.mult, ALU.add),
                                             waits=[f2mm[1][pi], t3slot_g[pi], tk_b2],
                                             inc=S_DVE)
                        G["pf_free"][pi] = [t3a_early[pi]]
                for i2 in range(2):
                    tc = tcp * 2 + i2
                    xb = (tc % 2) * D
                    tok_xc = xc_toks[tc]
                    this_out_gate = out_prev
                    halves = []
                    for db in range(2):
                        pi = i2 * 2 + db
                        PY = P1 if db == 0 else P2
                        t_xn2 = p.op("vector",
                                     lambda e, t_=tc, d2=db, x0=xb, pp=ti % 2: e.tensor_scalar(
                                         sCW[:], sXC[:, x0 + d2 * 512:x0 + (d2 + 1) * 512],
                                         sSTAT[:, 64 + 8 * pp + 2 * t_: 65 + 8 * pp + 2 * t_],
                                         sSTAT[:, 65 + 8 * pp + 2 * t_: 66 + 8 * pp + 2 * t_],
                                         ALU.subtract, ALU.mult),
                                     waits=[tok_xc, PRO.musave[tc], G["cw_free"]], inc=S_DVE)
                        t_t3 = p.op("vector",
                                    lambda e, p_=pi: e.tensor_tensor(
                                        sT3[:, p_ * T:(p_ + 1) * T],
                                        sT3[:, p_ * T:(p_ + 1) * T], sCW[:], ALU.add),
                                    waits=[t3a_early[pi], t_xn2], inc=S_DVE)
                        t_g = p.op("scalar",
                                   lambda e, pr=db, p_=pi: e.activation(
                                       sGH[:, pr * T:(pr + 1) * T],
                                       sT3[:, p_ * T:(p_ + 1) * T], AF.Silu),
                                   waits=[t_t3, G["gh_free"][db]], inc=S_ACT)
                        t3slot_g[pi] = t_g
                        G["t3_free"][pi] = t_g
                        trt = []
                        for j in range(4):
                            dc = db * 4 + j
                            tok = p.op("tensor",
                                       lambda e, d=dc, t_=tc, jj=j, PYl=PY: e.transpose(
                                           PYl[:, jj * 128:(jj + 1) * 128],
                                           sY[:, d * T + t_ * 128: d * T + (t_ + 1) * 128],
                                           sIDF[:]),
                                       waits=[y_last[dc], yt_free[db],
                                              p12_gate[db]], inc=S_PE)
                            trt.append(tok)
                        trt_last = trt[-1]
                        t_yt = p.op("vector",
                                    lambda e, PYl=PY: e.tensor_copy(sYT[:], PYl[:, 0:512]),
                                    waits=trt + [G["tail_gate"], syt_free],
                                    inc=S_DVE)
                        yt_free[db] = t_yt
                        t_cw = p.op("vector",
                                    lambda e, d2=db: e.tensor_tensor(
                                        sCW[:], sCW[:],
                                        sDB[:, d2 * 512:(d2 + 1) * 512], ALU.mult),
                                    waits=[t_t3, t_yt, tk_db], inc=S_DVE)
                        t_s = p.op("vector",
                                   lambda e: e.tensor_tensor(sYT[:], sYT[:], sCW[:], ALU.add),
                                   waits=[t_cw], inc=S_DVE)
                        G["cw_free"] = t_s
                        t_m = p.op("vector",
                                   lambda e, pr=db: e.tensor_tensor(
                                       sYT[:], sYT[:], sGH[:, pr * T:(pr + 1) * T], ALU.mult),
                                   waits=[t_s, t_g], inc=S_DVE)
                        G["gh_free"][db] = t_m
                        t_o = p.op("vector",
                                   lambda e, d2=db, x0=xb: e.tensor_tensor(
                                       sOUT[:, d2 * 512:(d2 + 1) * 512], sYT[:],
                                       sXC[:, x0 + d2 * 512:x0 + (d2 + 1) * 512], ALU.add),
                                   waits=[t_m, tok_xc, this_out_gate,
                                          out_wr_gate[db]], inc=S_DVE)
                        comb_prev = t_o
                        syt_free = t_o
                        halves.append(t_o)
                    if tc + 2 < NTC:
                        xc_toks[tc + 2] = emit_xc(tc + 2, halves[-1])
                    G["xc_free"][tc % 2] = halves[-1]
                    t_store = p.op("sync",
                                   lambda e, t_=tc, r=r0: e.dma_start(
                                       out=outT[r + t_ * 128: r + (t_ + 1) * 128, :],
                                       in_=sOUT[:]),
                                   waits=halves, inc=S_OUT, amt=16, dma=True)
                    out_prev = t_store
            G["out_gate"] = out_prev
            G["syt_last"] = syt_free
            G["musave_rd"][ti % 2] = comb_prev
            G["ylast_pe"] = trt_last
            G["y_final"] = dict(y_last)
            G["hid_pe"] = f2_last_mm
            G["h2l_pe"] = fc1mm_par[1]
            G["w_free"] = list(f2mm_rot)
            if PRN is not None:
                adopt(PRN)
                PRO = PRN

        p.op("sync", lambda e: e.nop(), waits=[(S_OUT, S_OUT.n)])

        with nc.Block() as block:
            p.emit(block)

    return nc


def _host_stage(L0_attn_x, L0_attn_A_log, L0_attn_D, L0_attn_x_proj_w,
                L0_attn_dt_proj_w, L0_attn_dt_proj_b, L0_ffn_ln_gamma,
                L0_ffn_ln_beta, L0_ffn_fc1_q, L0_ffn_fc1_scales, L0_ffn_fc1_bias,
                L0_ffn_fc2_q, L0_ffn_fc2_scales, L0_ffn_fc2_bias):
    x = np.ascontiguousarray(np.asarray(L0_attn_x, dtype=np.float32))
    assert x.shape == (16, SEQ, D)

    A_c = np.ascontiguousarray(
        np.asarray(L0_attn_A_log, np.float32).reshape(NDC, 128, NST)
        .transpose(1, 0, 2).reshape(128, 128))
    xpwT = np.asarray(L0_attn_x_proj_w, np.float32).T
    xpwT_c = np.ascontiguousarray(
        xpwT.reshape(NDC, 128, 96).transpose(1, 0, 2).reshape(128, NDC * 96))
    dtwT = np.ascontiguousarray(np.asarray(L0_attn_dt_proj_w, np.float32).T)
    dtb_c = np.ascontiguousarray(np.asarray(L0_attn_dt_proj_b, np.float32).reshape(NDC, 128).T)
    fc1b_c = np.ascontiguousarray(np.asarray(L0_ffn_fc1_bias, np.float32).reshape(NFB, 128).T)
    beta_c = np.ascontiguousarray(np.asarray(L0_ffn_ln_beta, np.float32).reshape(NDC, 128).T)
    rows4 = np.ascontiguousarray(np.stack([
        np.asarray(L0_ffn_ln_gamma, np.float32),
        np.asarray(L0_ffn_ln_beta, np.float32),
        np.asarray(L0_attn_D, np.float32),
        np.asarray(L0_ffn_fc2_bias, np.float32)], axis=0))
    q1blk = np.ascontiguousarray(
        np.asarray(L0_ffn_fc1_q, np.int32).reshape(NDC, 128, NFB, 128).transpose(2, 0, 1, 3))
    sc1r = np.ascontiguousarray(
        np.asarray(L0_ffn_fc1_scales, np.float32).reshape(NDC, NFB, 128)
        .transpose(1, 0, 2).reshape(NFB, NDC * 128))
    q2blk = np.ascontiguousarray(np.asarray(L0_ffn_fc2_q, np.int32).reshape(NFB, 128, D))
    sc2r = np.ascontiguousarray(np.asarray(L0_ffn_fc2_scales, np.float32))
    idf = np.eye(128, dtype=np.float32)
    id2b = np.eye(128).astype(BF)
    onesf = np.ones((1, 128), dtype=np.float32)
    cvec = np.ascontiguousarray(np.stack([np.full(128, EPS, np.float32), np.ones(128, np.float32)], axis=1))
    glane = np.ascontiguousarray(
        np.asarray(L0_ffn_ln_gamma, np.float32).reshape(NDC, 128).T)
    selb = np.zeros((32, 32 * 128), dtype=ml_dtypes.float8_e4m3)
    for k in range(32):
        selb[k, k * 128:(k + 1) * 128] = ml_dtypes.float8_e4m3(1.0)

    shared = dict(A_c=A_c, xpwT_c=xpwT_c, dtwT=dtwT, dtb_c=dtb_c, fc1b_c=fc1b_c,
                  beta_c=beta_c,
                  rows4=rows4, q1blk=q1blk, sc1r=sc1r, q2blk=q2blk, sc2r=sc2r,
                  idf=idf, id2b=id2b, onesf=onesf, selb=selb, cvec=cvec,
                  glane=glane)
    return shared, x


def kernel(**inputs):
    shared, x = _host_stage(**inputs)
    nc = build_nc(NT)
    in_maps = []
    for c in range(NCORE):
        m = dict(shared)
        m["xT"] = np.ascontiguousarray(x[2 * c:2 * c + 2].reshape(TPC, D))
        in_maps.append(m)

    res = run_bass_kernel_spmd(nc, in_maps, core_ids=list(range(NCORE)))
    out = np.empty((16, SEQ, D), dtype=np.float32)
    for c in range(NCORE):
        out[2 * c:2 * c + 2] = np.asarray(res.results[c]["outT"]).reshape(2, SEQ, D)
    return out
